# version 06
# baseline (speedup 1.0000x reference)
import contextlib
import numpy as np
import ml_dtypes
import concourse.bass as bass
import concourse.mybir as mybir
from concourse.bass_utils import run_bass_kernel_spmd
from concourse.bass_core_rust import IndirectOffsetOnAxis

F32 = mybir.dt.float32
BF16 = mybir.dt.bfloat16
I32 = mybir.dt.int32
AF = mybir.ActivationFunctionType
ALU = mybir.AluOpType
AX = mybir.AxisListType
BF = ml_dtypes.bfloat16

NCORES = 8
D = 1024
T_FULL = 16384
NT = T_FULL // NCORES
NH, FEAT, HD = 16, 16, 64
CH = 128
NCH = NT // CH
PRE = T_FULL - NT
PT = 512
NPT = PRE // PT
DE = 160
EV = 65
E_EXP = 32
I_FFN = 2048
CAP = 128
TRASH = E_EXP * CAP
NSLOT = E_EXP * CAP + 16
LN_EPS = 1e-5
ATT_EPS = 1e-12

DEBUG_OUTS = []
PHASE_LIMIT = 99


def _tri_consts():
    pairs = [(i, j) for i in range(FEAT) for j in range(i, FEAT)]
    R1 = np.zeros((DE, FEAT + 1), np.float32)
    R2k = np.zeros((DE, FEAT + 1), np.float32)
    R2q = np.zeros((DE, FEAT + 1), np.float32)
    for f, (i, j) in enumerate(pairs):
        R1[f, i] = 1.0
        R2k[f, j] = 1.0
        R2q[f, j] = (1.0 / 32.0) if i == j else (1.0 / 16.0)
    for i in range(FEAT):
        R1[136 + i, i] = 1.0
        R2k[136 + i, FEAT] = 1.0
        R2q[136 + i, FEAT] = 0.25
    R1[152, FEAT] = 1.0
    R2k[152, FEAT] = 1.0
    R2q[152, FEAT] = 1.0
    return R1, R2k, R2q


def _consts():
    c = {}
    R1, R2k, R2q = _tri_consts()
    c["R1KT"] = np.ascontiguousarray(R1.T).astype(BF)
    c["R2KT"] = np.ascontiguousarray(R2k.T).astype(BF)
    c["R12KT"] = np.ascontiguousarray(np.concatenate([R1.T, R2k.T], axis=1)).astype(BF)
    c["R1QD1"] = np.ascontiguousarray(R1[0:128].T).astype(BF)
    c["R2QD1"] = np.ascontiguousarray(R2q[0:128].T).astype(BF)
    c["R1QD2"] = np.ascontiguousarray(R1[128:160].T).astype(BF)
    c["R2QD2"] = np.ascontiguousarray(R2q[128:160].T).astype(BF)
    c["IDENTF"] = np.eye(128, dtype=np.float32)
    c["IDENTB"] = np.eye(128, dtype=BF)
    mh = 0.5 * np.triu(np.ones((CH, CH), np.float32))
    c["MASKH"] = np.tile(mh, (1, NH)).astype(BF)
    c["UEXCL"] = np.triu(np.ones((CH, CH), np.float32), 1).astype(BF)
    c["EIDX"] = np.tile(np.arange(E_EXP, dtype=np.float32)[None, :] * CAP,
                        (128, 1)).astype(np.float32)
    es = np.zeros((FEAT, NH * EV), np.float32)
    for h in range(NH):
        es[h, h * EV:(h + 1) * EV] = 1.0
    c["ESEL"] = es.astype(BF)
    c["ONESB"] = np.ones((128, 256), BF)
    c["ONECB"] = np.ones((128, 1), BF)
    c["ONECF"] = np.ones((128, 1), np.float32)
    c["ONESF"] = np.ones((1, 128), np.float32)
    return c


CONSTS = _consts()


class S:
    def __init__(self, nc, stack):
        self.h = {}
        self.cnt = {}
        for name in ("pe", "dve", "act", "gp", "ld", "gld", "st", "sct", "wt0", "wt1"):
            self.h[name] = stack.enter_context(nc.semaphore(name))
            self.cnt[name] = 0

    def inc(self, name, inst, n=1):
        inst.then_inc(self.h[name], n)
        self.cnt[name] += n
        return (name, self.cnt[name])

    DMA_SEMS = ("ld", "gld", "st", "sct", "wt0", "wt1")

    def wait(self, eng, *toks):
        seen = {}
        for t in toks:
            if t is None:
                continue
            name, v = t
            if name in self.DMA_SEMS:
                v = self.cnt[name]
            if v > 0:
                seen[name] = max(seen.get(name, 0), v)
        for name, v in seen.items():
            eng.wait_ge(self.h[name], v)


class _EarlyFinishDone(Exception):
    pass


def build(debug_outs=()):
    try:
        return _build(debug_outs)
    except _EarlyFinishDone as ex:
        return ex.args[0]


def _build(debug_outs=()):
    nc = bass.Bass("TRN2", target_bir_lowering=False)
    stack = contextlib.ExitStack()

    def dram_in(name, shape, dtype=F32):
        return nc.dram_tensor(name, list(shape), dtype, kind="ExternalInput").ap()

    ins = {}
    ins["XS_T"] = dram_in("XS_T", (D, NT))
    ins["XPRE_T"] = dram_in("XPRE_T", (D, PRE))
    ins["PMT"] = dram_in("PMT", (128, PRE // CH))
    for w, shp in (("WqT", (D, 256)), ("WkT", (D, 256)), ("WvT", (D, D)),
                   ("WoT", (D, D)), ("GATE", (D, E_EXP)),
                   ("SGW", (D, I_FFN)), ("SUW", (D, I_FFN)), ("SDW", (I_FFN, D)),
                   ("RGW", (E_EXP, D, I_FFN)), ("RUW", (E_EXP, D, I_FFN)),
                   ("RDW", (E_EXP, I_FFN, D))):
        ins[w] = dram_in(w, shp)
    for k, v in CONSTS.items():
        ins[k] = dram_in(k, v.shape, BF16 if v.dtype == BF else F32)

    OUT = nc.dram_tensor("OUT", [D, NT], F32, kind="ExternalOutput").ap()
    dbg = {}

    def dbg_out(name, shape, dtype=F32):
        if name in debug_outs:
            dbg[name] = nc.dram_tensor("dbg_" + name, list(shape), dtype,
                                       kind="ExternalOutput").ap()
        return dbg.get(name)

    KBOWN = nc.dram_tensor("KBOWN", [2, FEAT + 1, NH, NT], BF16, kind="Internal").ap()
    KBPRE = nc.dram_tensor("KBPRE", [2, FEAT + 1, NH, PT], BF16, kind="Internal").ap()
    ROWB = nc.dram_tensor("ROWB", [NT], F32, kind="Internal").ap()
    XSG = nc.dram_tensor("XSG", [NSLOT, D], BF16, kind="Internal").ap()
    RTD = nc.dram_tensor("RTD", [NSLOT, D], BF16, kind="Internal").ap()

    pe, dve, act, gp, sq = nc.tensor, nc.vector, nc.scalar, nc.gpsimd, nc.sync
    sm = S(nc, stack)

    def bar():
        return (("pe", sm.cnt["pe"]), ("dve", sm.cnt["dve"]),
                ("act", sm.cnt["act"]), ("gp", sm.cnt["gp"]), ("st", 1),
                ("ld", 1), ("gld", 1))

    def phase_gate(n):
        if PHASE_LIMIT > n:
            return
        for name in ("pe", "dve", "act", "gp", "ld", "gld", "st", "sct", "wt0", "wt1"):
            if sm.cnt[name] > 0:
                sq.wait_ge(sm.h[name], sm.cnt[name])
        t_fin = None
        for k in range(8):
            t_fin = sm.inc("st", sq.dma_start(
                out=OUT[128 * k:128 * (k + 1), :], in_=r[k][:]), 16)
        sm.wait(sq, t_fin)
        stack.close()
        raise _EarlyFinishDone((nc, dbg))
    frees = {}
    _seq = [0]

    def _uname(name):
        _seq[0] += 1
        return f"{name}_{_seq[0]}"

    def sb(name, shape, dtype):
        assert name not in frees, name
        cm = nc.sbuf_tensor(_uname(name), list(shape), dtype)
        t = cm.__enter__()
        frees[name] = cm
        return t

    def psum(name, shape, dtype=F32):
        assert name not in frees, name
        cm = nc.psum_tensor(_uname(name), list(shape), dtype)
        t = cm.__enter__()
        frees[name] = cm
        return t

    def free(*names):
        order = list(frees.keys())
        for n in sorted(names, key=lambda n: -order.index(n)):
            frees.pop(n).__exit__(None, None, None)

    r = [sb(f"r{k}", (128, NT), F32) for k in range(8)]
    cst = {}
    for k, v in CONSTS.items():
        cst[k] = sb("c_" + k, v.shape, BF16 if v.dtype == BF else F32)
    gate = sb("gate", (128, 8 * E_EXP), F32)
    pmt = sb("pmt", (128, PRE // CH), F32)

    t_r_in = []
    for k in range(8):
        t_r_in.append(sm.inc("ld", sq.dma_start(
            out=r[k][:], in_=ins["XS_T"][128 * k:128 * (k + 1), :]), 16))
    for k in CONSTS:
        sm.inc("ld", sq.dma_start(out=cst[k][:], in_=ins[k]), 16)
    sm.inc("ld", sq.dma_start(out=pmt[:], in_=ins["PMT"]), 16)
    t_gate = sm.inc("ld", sq.dma_start(
        out=gate[:].rearrange("p (k c) -> p k c", k=8),
        in_=ins["GATE"].rearrange("(k p) c -> p k c", p=128)), 16)
    t_consts = ("ld", sm.cnt["ld"])

    def wload(dst, src, ktiles=8):
        return sm.inc("gld", gp.dma_start(
            out=dst[:].rearrange("p (k c) -> p k c", k=ktiles),
            in_=src.rearrange("(k p) c -> p k c", p=128)), 16)


    v_sb = sb("v_sb", (128, NCH * NH * EV), BF16)
    S1 = sb("S1", (128, NH * EV), F32)
    S2 = sb("S2", (128, 4 * EV), F32)
    S1b = sb("S1b", (128, NH * EV), BF16)
    S2b = sb("S2b", (128, 4 * EV), BF16)
    wk = sb("wk", (128, 8 * 256), BF16)
    wv = sb("wv", (128, 8 * 1024), BF16)
    wq = sb("wq", (128, 8 * 256), BF16)
    t_wq = wload(wq, ins["WqT"])
    t_wk = wload(wk, ins["WkT"])
    t_wv = wload(wv, ins["WvT"])

    ln_bf = [sb(f"ln{k}", (128, NT), BF16) for k in range(8)]

    def ln_exact(tag, t_src, rowM, rowR):
        sqs = sb("sqs" + tag, (128, 512), F32)
        rowS = sb("rowS" + tag, (1, NT), F32)
        rowMb = sb("rowMb" + tag, (1, NT), BF16)
        rowRb = sb("rowRb" + tag, (1, NT), BF16)
        last = None
        t_sqs = None
        for ct in range(NT // 512):
            cs = slice(512 * ct, 512 * (ct + 1))
            pA = psum(f"{tag}A", (1, 512))
            pB = psum(f"{tag}B", (1, 512))
            for k in range(8):
                sm.wait(pe, t_src[k], t_consts, last)
                i1 = pe.matmul(pA[0:1, :], cst["ONECF"][:], r[k][:, cs],
                               start=(k == 0), stop=(k == 7))
                sm.wait(act, t_src[k], t_sqs)
                t2 = sm.inc("act", act.activation(sqs[:], r[k][:, cs], AF.Square))
                sm.wait(pe, t2)
                i3 = pe.matmul(pB[0:1, :], cst["ONECF"][:], sqs[:],
                               start=(k == 0), stop=(k == 7))
                t_sqs = sm.inc("pe", i3)
            t_mm = sm.inc("pe", i1)
            sm.wait(dve, t_mm)
            t1 = sm.inc("dve", dve.tensor_scalar_mul(rowM[0:1, cs], pA[0:1, :], 1.0 / D))
            sm.wait(dve, t_sqs, t1)
            t2 = sm.inc("dve", dve.tensor_scalar_mul(rowS[0:1, cs], pB[0:1, :], 1.0 / D))
            sm.wait(dve, t1)
            t3 = sm.inc("dve", dve.tensor_tensor(rowR[0:1, cs], rowM[0:1, cs],
                                                 rowM[0:1, cs], ALU.mult))
            sm.wait(dve, t2, t3)
            t4 = sm.inc("dve", dve.tensor_tensor(rowS[0:1, cs], rowS[0:1, cs],
                                                 rowR[0:1, cs], ALU.subtract))
            sm.wait(dve, t4)
            t5 = sm.inc("dve", dve.tensor_scalar_add(rowS[0:1, cs], rowS[0:1, cs], LN_EPS))
            sm.wait(dve, t5)
            t6 = sm.inc("dve", dve.reciprocal(rowS[0:1, cs], rowS[0:1, cs]))
            sm.wait(act, t6)
            t7 = sm.inc("act", act.activation(rowR[0:1, cs], rowS[0:1, cs], AF.Sqrt))
            sm.wait(dve, t1, t7)
            t8 = sm.inc("dve", dve.tensor_copy(rowMb[0:1, cs], rowM[0:1, cs]))
            sm.wait(dve, t8)
            t9 = sm.inc("dve", dve.tensor_copy(rowRb[0:1, cs], rowR[0:1, cs]))
            free(f"{tag}A", f"{tag}B")
            pM = psum(f"{tag}M", (128, 512))
            pR = psum(f"{tag}R", (128, 512))
            sm.wait(pe, t8)
            t_mb = sm.inc("pe", pe.matmul(pM[:], cst["ONESB"][0:1, 0:128], rowMb[0:1, cs],
                                          start=True, stop=True))
            sm.wait(pe, t9)
            t_rb = sm.inc("pe", pe.matmul(pR[:], cst["ONESB"][0:1, 0:128], rowRb[0:1, cs],
                                          start=True, stop=True))
            for k in range(8):
                sm.wait(dve, t_mb, t_rb, t_sqs)
                ta = sm.inc("dve", dve.tensor_tensor(sqs[:], r[k][:, cs], pM[:],
                                                     ALU.subtract))
                t_sqs = ta
                sm.wait(dve, ta)
                last = sm.inc("dve", dve.tensor_tensor(ln_bf[k][:, cs], sqs[:], pR[:],
                                                       ALU.mult))
                t_sqs = last
            free(f"{tag}M", f"{tag}R")
        free("rowRb" + tag, "rowMb" + tag, "rowS" + tag, "sqs" + tag)
        return last

    rowM1 = sb("rowM1", (1, NT), F32)
    rowR1 = sb("rowR1", (1, NT), F32)
    t_ln1 = ln_exact("l1", t_r_in, rowM1, rowR1)
    o = dbg_out("ln1", (128, 8 * NT), BF16)
    if o is not None:
        sm.wait(sq, t_ln1)
        for k in range(8):
            sm.inc("st", sq.dma_start(out=o[:, NT * k:NT * (k + 1)], in_=ln_bf[k][:]), 16)
    phase_gate(1)

    qknat = sb("qknat", (128, NT), BF16)

    t_kb = []
    tg = None
    for iqk, wmat, t_w in ((0, wq, t_wq), (1, wk, t_wk)):
        for grp in range(2):
            for ct in range(NT // 512):
                cs = slice(512 * ct, 512 * (ct + 1))
                pg = psum("pj", (128, 512))
                for k in range(8):
                    sm.wait(pe, t_w, t_ln1, tg)
                    ii = pe.matmul(pg[:], wmat[:, 256 * k + 128 * grp:256 * k + 128 * (grp + 1)],
                                   ln_bf[k][:, cs], start=(k == 0), stop=(k == 7))
                t_mm = sm.inc("pe", ii)
                sm.wait(act, t_mm, tg, t_kb[-1] if t_kb else None)
                tg = sm.inc("act", act.copy(qknat[:, cs], pg[:]))
                free("pj")
            sm.wait(sq, tg)
            t_kb.append(sm.inc("st", sq.dma_start(
                out=KBOWN[iqk, 0:FEAT, 8 * grp:8 * (grp + 1), :].rearrange("f h t -> h f t"),
                in_=qknat[:]), 16))
    for iqk in range(2):
        sm.wait(sq, t_consts)
        t_kb.append(sm.inc("st", sq.dma_start(
            out=KBOWN[iqk, FEAT, :, :].rearrange("h t -> (h t)").rearrange("(a b) -> a b", a=128),
            in_=cst["ONESB"][:, 0:256]), 16))
    t_kb_all = t_kb[-1]
    free("qknat")

    t_vown = tg
    for tt in range(NCH):
        ts_ = slice(128 * tt, 128 * (tt + 1))
        for half in range(2):
            pv = psum("pv", (128, 512))
            for k in range(8):
                sm.wait(pe, t_wv, t_ln1, t_vown)
                ii = pe.matmul(pv[:], ln_bf[k][:, ts_],
                               wv[:, 1024 * k + 512 * half:1024 * k + 512 * (half + 1)],
                               start=(k == 0), stop=(k == 7))
            t_mm = sm.inc("pe", ii)
            sm.wait(act, t_mm, t_vown)
            t_vown = sm.inc("act", act.copy(
                v_sb[:].rearrange("p (t h e) -> p t h e", t=NCH, h=NH)[:, tt, 8 * half:8 * (half + 1), 0:64],
                pv[:].rearrange("p (h e) -> p h e", h=8)))
            free("pv")
    sm.wait(dve, t_vown)
    t_vones = sm.inc("dve", dve.memset(
        v_sb[:].rearrange("p (t h e) -> p t h e", t=NCH, h=NH)[:, :, :, 64:65], 1.0))
    o = dbg_out("vsb", (128, NCH * NH * EV), BF16)
    if o is not None:
        sm.wait(sq, t_vones)
        sm.inc("st", sq.dma_start(out=o, in_=v_sb[:]), 16)
    phase_gate(2)
    free(*[f"ln{k}" for k in range(8)] + ["rowR1", "rowM1", "wq"])

    t_sz1 = sm.inc("dve", dve.memset(S1[:], 0.0))
    t_sz2 = sm.inc("dve", dve.memset(S2[:], 0.0))

    xbf = [[sb(f"xbf{p}_{k}", (128, PT), BF16) for k in range(8)] for p in range(2)]
    xh = [[sb(f"xh{p}_{k}", (128, PT), BF16) for k in range(8)] for p in range(2)]
    sqb = sb("sqb", (128, PT), BF16)
    prowM = sb("prowM", (1, PT), F32)
    prowR = sb("prowR", (1, PT), F32)
    prowS = sb("prowS", (1, PT), F32)
    prowMb = sb("prowMb", (1, PT), BF16)
    prowRb = sb("prowRb", (1, PT), BF16)
    knat = sb("knat", (128, 2 * PT), BF16)
    ktp = sb("ktp", (FEAT + 1, NH * PT), BF16)
    kf = sb("kf", (128, 1024), BF16)
    rep_sb = sb("rep_sb", (128, 2 * DE), BF16)
    vpre = sb("vpre", (128, 4 * NH * EV), BF16)
    pmb = sb("pmb", (128, 4), F32)

    t_pm = None
    t_pm_rd = None
    t_row_rd = None
    t_rep_act = None
    t_xbf = [None] * 8
    t_xh = [None] * 8
    t_xh_war = [[None] * 8, [None] * 8]
    t_knat = None
    t_ktp_w = [None, None]
    t_ktp_r = [None, None]
    t_kf = None
    t_vpre = None
    t_state = None
    t_u1p = t_u2p = None

    t_pre_ones = None
    for ib in range(2):
        for hhalf in range(2):
            sm.wait(sq, t_consts)
            t_pre_ones = sm.inc("st", sq.dma_start(
                out=KBPRE[ib, FEAT, 8 * hhalf:8 * (hhalf + 1), :]
                    .rearrange("h t -> (h t)").rearrange("(a b) -> a b", a=16),
                in_=cst["ONESB"][0:16, 0:256]), 16)

    t_bar3 = bar()
    t_rep_rd = None
    for j in range(NPT):
        js = slice(PT * j, PT * (j + 1))
        buf = j % 2
        xb, xhh = xbf[buf], xh[buf]
        t_xb_prev = t_xbf[buf * 0:8] if False else None
        for k in range(8):
            sm.wait(gp, t_xh_war[buf][k], *(t_bar3 if j == 0 else ()))
            t_xbf[k] = sm.inc("wt%d" % buf, gp.dma_start(
                out=xb[k][:], in_=ins["XPRE_T"][128 * k:128 * (k + 1), js]), 16)
        sm.wait(dve, t_consts, t_pm_rd, *(t_bar3 if j == 0 else ()))
        t_pm = sm.inc("dve", dve.tensor_copy(pmb[:], pmt[:, 4 * j:4 * (j + 1)]))
        pA = psum("ppA", (1, PT))
        pB = psum("ppB", (1, PT))
        t_sqb = None
        for k in range(8):
            sm.wait(pe, t_xbf[k], t_consts, t_pm, t_vown, t_row_rd)
            i1 = pe.matmul(pA[0:1, :], cst["ONECB"][:], xb[k][:],
                           start=(k == 0), stop=(k == 7))
            sm.wait(act, t_xbf[k], t_sqb)
            t2 = sm.inc("act", act.activation(sqb[:], xb[k][:], AF.Square))
            sm.wait(pe, t2)
            i3 = pe.matmul(pB[0:1, :], cst["ONECB"][:], sqb[:],
                           start=(k == 0), stop=(k == 7))
            t_sqb = sm.inc("pe", i3)
        t_mm = sm.inc("pe", i1)
        sm.wait(dve, t_mm, t_pm)
        t1 = sm.inc("dve", dve.tensor_scalar_mul(prowM[0:1, :], pA[0:1, :], 1.0 / D))
        sm.wait(dve, t_sqb, t1)
        t2 = sm.inc("dve", dve.tensor_scalar_mul(prowS[0:1, :], pB[0:1, :], 1.0 / D))
        sm.wait(dve, t1)
        t3 = sm.inc("dve", dve.tensor_tensor(prowR[0:1, :], prowM[0:1, :],
                                             prowM[0:1, :], ALU.mult))
        sm.wait(dve, t2, t3)
        t4 = sm.inc("dve", dve.tensor_tensor(prowS[0:1, :], prowS[0:1, :],
                                             prowR[0:1, :], ALU.subtract))
        sm.wait(dve, t4)
        t5 = sm.inc("dve", dve.tensor_scalar_add(prowS[0:1, :], prowS[0:1, :], LN_EPS))
        sm.wait(dve, t5)
        t6 = sm.inc("dve", dve.reciprocal(prowS[0:1, :], prowS[0:1, :]))
        sm.wait(act, t6)
        t7 = sm.inc("act", act.activation(prowR[0:1, :], prowS[0:1, :], AF.Sqrt))
        sm.wait(dve, t1, t7)
        t8 = sm.inc("dve", dve.tensor_copy(prowMb[0:1, :], prowM[0:1, :]))
        sm.wait(dve, t8)
        t9 = sm.inc("dve", dve.tensor_copy(prowRb[0:1, :], prowR[0:1, :]))
        free("ppA", "ppB")
        pM = psum("ppM", (128, PT))
        pR = psum("ppR", (128, PT))
        sm.wait(pe, t8)
        t_mb = sm.inc("pe", pe.matmul(pM[:], cst["ONESB"][0:1, 0:128], prowMb[0:1, :],
                                      start=True, stop=True))
        sm.wait(pe, t9)
        t_rb = sm.inc("pe", pe.matmul(pR[:], cst["ONESB"][0:1, 0:128], prowRb[0:1, :],
                                      start=True, stop=True))
        t_row_rd = t_rb
        t_xh_new = [None] * 8
        for k in range(8):
            sm.wait(dve, t_mb, t_rb, t_xbf[k], t_knat, t_xh_war[buf][k])
            ta = sm.inc("dve", dve.tensor_tensor(xb[k][:], xb[k][:], pM[:],
                                                 ALU.subtract))
            sm.wait(dve, ta)
            t_xh_new[k] = sm.inc("dve", dve.tensor_tensor(xhh[k][:], xb[k][:], pR[:],
                                                          ALU.mult))
        t_xh = t_xh_new
        free("ppM", "ppR")
        pK = psum("ppK", (128, PT))
        tk_ev = t_knat
        for grp in range(2):
            for k in range(8):
                sm.wait(pe, t_wk, t_xh[k], t_xh[7], tk_ev)
                ii = pe.matmul(pK[:], wk[:, 256 * k + 128 * grp:256 * k + 128 * (grp + 1)],
                               xhh[k][:], start=(k == 0), stop=(k == 7))
            t_mm = sm.inc("pe", ii)
            sm.wait(act, t_mm, tk_ev)
            tk_ev = sm.inc("act", act.copy(knat[:, PT * grp:PT * (grp + 1)], pK[:]))
        t_knat = tk_ev
        free("ppK")
        sm.wait(sq, t_knat, t_ktp_r[buf], t_pre_ones)
        tw = sm.inc("st", sq.dma_start(
            out=KBPRE[buf, 0:FEAT, :, :].rearrange("f (g h) t -> h f g t", g=2),
            in_=knat[:].rearrange("p (g t) -> p g t", g=2)), 16)
        t_ktp_w[buf] = tw
        sm.wait(sq, tw)
        tr = sm.inc("ld", sq.dma_start(
            out=ktp[:].rearrange("f (h t) -> f h t", h=NH),
            in_=KBPRE[buf, :, :, :]), 16)
        t_ktp_r[buf] = tr
        t_vev = t_vown
        for cc in range(4):
            tsl = slice(128 * cc, 128 * (cc + 1))
            pV = psum("ppV", (128, 512))
            for half in range(2):
                for k in range(8):
                    sm.wait(pe, t_wv, t_xh[k], t_vev, t_knat, t_state)
                    ii = pe.matmul(pV[:], xhh[k][:, tsl],
                                   wv[:, 1024 * k + 512 * half:1024 * k + 512 * (half + 1)],
                                   start=(k == 0), stop=(k == 7))
                t_mm = sm.inc("pe", ii)
                sm.wait(act, t_mm, t_vev)
                t_vev = sm.inc("act", act.copy(
                    vpre[:].rearrange("p (c h e) -> p c h e", c=4, h=NH)
                        [:, cc, 8 * half:8 * (half + 1), 0:64],
                    pV[:].rearrange("p (h e) -> p h e", h=8)))
            free("ppV")
            sm.wait(dve, t_vev, t_pm)
            t_vm = sm.inc("dve", dve.tensor_scalar_mul(
                vpre[:].rearrange("p (c h e) -> p c h e", c=4, h=NH)[:, cc, :, 64:65]
                    .rearrange("p h o -> p (h o)"),
                cst["ONESB"][:, 0:16], pmb[:, cc:cc + 1]))
        t_vpre = t_vm
        t_pm_rd = t_vm
        PSS1p = psum("PSS1p", (128, 2048))
        PSS2p = psum("PSS2p", (128, 512))
        pR1 = psum("ppR1", (128, 1024))
        t_kf = None
        for h in range(NH):
            for sub in range(2):
                for tt2 in range(2):
                    cc = 2 * sub + tt2
                    lhs = ktp[:, PT * h + 128 * cc:PT * h + 128 * (cc + 1)]
                    sm.wait(pe, tr, t_consts, t_kf, t_rep_rd)
                    i1 = pe.matmul(pR1[:, 512 * tt2:512 * tt2 + 2 * DE], lhs,
                                   cst["R12KT"][:], start=True, stop=True)
                t_rep = sm.inc("pe", i1)
                sm.wait(act, t_rep, t_rep_act)
                te = sm.inc("act", act.copy(
                    rep_sb[:].rearrange("p (a f) -> p a f", a=2),
                    pR1[:].rearrange("p (a f) -> p a f", a=2)[:, :, 0:DE]))
                t_rep_act = te
                sm.wait(dve, te, t_kf)
                t_kf = sm.inc("dve", dve.tensor_tensor(
                    kf[:].rearrange("p (c f) -> p c f", c=4)[:, 2 * sub:2 * sub + 2, 0:DE],
                    pR1[:].rearrange("p (a f) -> p a f", a=2)[:, :, DE:2 * DE],
                    rep_sb[:].rearrange("p (a f) -> p a f", a=2), ALU.mult))
            t_rep_rd = t_kf
            for cc in range(4):
                sm.wait(pe, t_kf, t_vpre, t_u2p)
                i1 = pe.matmul(PSS1p[:, 128 * h:128 * h + EV],
                               kf[:, 256 * cc:256 * cc + 128],
                               vpre[:, 1040 * cc + EV * h:1040 * cc + EV * (h + 1)],
                               start=(cc == 0), stop=(cc == 3),
                               skip_group_check=True)
                i2 = pe.matmul(PSS2p[32 * (h % 4):32 * (h % 4) + 32,
                                     EV * (h // 4):EV * (h // 4) + EV],
                               kf[:, 256 * cc + 128:256 * cc + DE],
                               vpre[:, 1040 * cc + EV * h:1040 * cc + EV * (h + 1)],
                               start=(cc == 0), stop=(cc == 3),
                               tile_position=(0, 32 * (h % 4)),
                               skip_group_check=True)
            t_state = sm.inc("pe", i2)
        sm.wait(dve, t_state, t_sz1, t_sz2, t_u1p, t_u2p)
        t_u1p = sm.inc("dve", dve.tensor_tensor(
            S1[:], S1[:],
            PSS1p[:].rearrange("p (h e) -> p h e", h=NH)[:, :, 0:EV], ALU.add))
        sm.wait(dve, t_u1p)
        t_u2p = sm.inc("dve", dve.tensor_tensor(S2[:], S2[:],
                                                PSS2p[:, 0:4 * EV], ALU.add))
        free("ppR1", "PSS1p", "PSS2p")
        for k in range(8):
            t_xh_war[buf][k] = t_state

    free(*([f"xbf{p}_{k}" for p in range(2) for k in range(8)] +
           [f"xh{p}_{k}" for p in range(2) for k in range(8)] +
           ["sqb", "prowM", "prowR", "prowS", "prowMb", "prowRb", "knat", "ktp",
            "kf", "rep_sb", "vpre", "pmb", "wv", "wk"]))
    phase_gate(3)

    t_s1 = t_u1p
    t_s2 = t_u2p
    sm.wait(act, t_u1p, t_u2p, t_sz1, t_sz2)
    t_s1b = sm.inc("act", act.copy(S1b[:], S1[:]))
    t_s2b = sm.inc("act", act.copy(S2b[:], S2[:]))
    o = dbg_out("s1", (128, NH * EV), F32)
    if o is not None:
        sm.wait(sq, t_s1)
        sm.inc("st", sq.dma_start(out=o, in_=S1[:]), 16)

    y_sb = [sb(f"y{k}", (128, NT), BF16) for k in range(8)]
    qt = sb("qt", (FEAT + 1, NH * CH), BF16)
    kt = sb("kt", (FEAT + 1, NH * CH), BF16)
    qf1 = sb("qf1", (128, NH * CH), BF16)
    qf2 = sb("qf2", (128, 4 * CH), BF16)
    kfo = sb("kfo", (128, NH * DE), BF16)
    repo = sb("repo", (128, 4 * DE), BF16)
    qrep = sb("qrep", (128, 512), BF16)
    sqf = sb("sqf", (128, NH * CH), BF16)
    at = sb("at", (128, NH * CH), BF16)
    nums = sb("nums", (EV, NH * CH), BF16)
    dens = sb("dens", (128, NH), F32)
    recs = sb("recs", (128, NH), F32)
    recT = sb("recT", (16, CH), BF16)

    t_qt = t_kt = None
    t_kfo = t_qf = t_at = t_num_ev = t_y = t_recT = None
    t_u1 = t_u2 = None
    t_pss = t_dmm = t_bc = t_tr = None
    t_bar4 = bar()

    for c in range(NCH):
        tsl = slice(CH * c, CH * (c + 1))
        sm.wait(sq, t_kb_all, t_qt, t_dmm, *(t_bar4 if c == 0 else ()))
        t_qt = sm.inc("ld", sq.dma_start(
            out=qt[:].rearrange("f (h t) -> f h t", h=NH),
            in_=KBOWN[0, :, :, tsl]), 16)
        sm.wait(sq, t_kt, t_dmm)
        t_kt = sm.inc("ld", sq.dma_start(
            out=kt[:].rearrange("f (h t) -> f h t", h=NH),
            in_=KBOWN[1, :, :, tsl]), 16)

        pRc = psum("scRc", (128, 2048))
        t_kfo_new = t_kfo
        for rnd in range(4):
            for hh in range(4):
                h = 4 * rnd + hh
                lhs = kt[:, CH * h:CH * (h + 1)]
                sm.wait(pe, t_kt, t_consts, t_kfo_new, t_num_ev, t_y)
                i1 = pe.matmul(pRc[:, 512 * hh:512 * hh + 2 * DE], lhs,
                               cst["R12KT"][:], start=True, stop=True)
            t_rep = sm.inc("pe", i1)
            sm.wait(act, t_rep)
            te = sm.inc("act", act.copy(
                repo[:].rearrange("p (a f) -> p a f", a=4),
                pRc[:].rearrange("p (a f) -> p a f", a=4)[:, :, 0:DE]))
            sm.wait(dve, te, t_pss)
            t_kfo_new = sm.inc("dve", dve.tensor_tensor(
                kfo[:].rearrange("p (h f) -> p h f", h=NH)[:, 4 * rnd:4 * rnd + 4, :],
                pRc[:].rearrange("p (a f) -> p a f", a=4)[:, :, DE:2 * DE],
                repo[:].rearrange("p (a f) -> p a f", a=4), ALU.mult))
        t_kfo = t_kfo_new
        free("scRc")

        PSS1 = psum("PSS1", (128, 2048))
        PSS2 = psum("PSS2", (128, 512))
        for h in range(NH):
            sm.wait(pe, t_kfo, t_vones, t_u1, t_u2, t_num_ev, t_y)
            i1 = pe.matmul(PSS1[:, 128 * h:128 * h + EV],
                           kfo[:, DE * h:DE * h + 128],
                           v_sb[:, 1040 * c + EV * h:1040 * c + EV * (h + 1)],
                           start=True, stop=True)
            i2 = pe.matmul(PSS2[32 * (h % 4):32 * (h % 4) + 32,
                                EV * (h // 4):EV * (h // 4) + EV],
                           kfo[:, DE * h + 128:DE * h + DE],
                           v_sb[:, 1040 * c + EV * h:1040 * c + EV * (h + 1)],
                           start=True, stop=True, tile_position=(0, 32 * (h % 4)))
        t_pss = sm.inc("pe", i2)

        pQ1 = psum("scQ1", (128, 512))
        pQ2 = psum("scQ2", (128, 512))
        t_qf_new = t_qf
        for g in range(4):
            rhs = qt[:, 512 * g:512 * (g + 1)]
            sm.wait(pe, t_qt, t_consts, t_qf_new, t_num_ev, t_y)
            i1 = pe.matmul(pQ1[:], cst["R1QD1"][:], rhs, start=True, stop=True)
            i2 = pe.matmul(pQ2[:], cst["R2QD1"][:], rhs, start=True, stop=True)
            t_rep = sm.inc("pe", i2)
            sm.wait(act, t_rep)
            te = sm.inc("act", act.copy(qrep[:], pQ1[:]))
            sm.wait(dve, te)
            t_qf_new = sm.inc("dve", dve.tensor_tensor(
                qf1[:, 512 * g:512 * (g + 1)], pQ2[:], qrep[:], ALU.mult))
        for h in range(NH):
            rhs = qt[:, CH * h:CH * (h + 1)]
            jj = h % 4
            sm.wait(pe, t_qt, t_consts, t_qf_new)
            i1 = pe.matmul(pQ1[32 * jj:32 * jj + 32, 128 * (h // 4):128 * (h // 4) + 128],
                           cst["R1QD2"][:], rhs, start=True, stop=True,
                           tile_position=(0, 32 * jj))
            i2 = pe.matmul(pQ2[32 * jj:32 * jj + 32, 128 * (h // 4):128 * (h // 4) + 128],
                           cst["R2QD2"][:], rhs, start=True, stop=True,
                           tile_position=(0, 32 * jj))
        t_rep = sm.inc("pe", i2)
        sm.wait(act, t_rep)
        te = sm.inc("act", act.copy(qrep[:], pQ1[:]))
        sm.wait(dve, te)
        t_qf = sm.inc("dve", dve.tensor_tensor(qf2[:], pQ2[:], qrep[:], ALU.mult))
        free("scQ1", "scQ2")

        sm.wait(dve, t_pss, t_s1, t_s2, t_s1b, t_s2b)
        t_u1 = sm.inc("dve", dve.tensor_tensor(
            S1[:], S1[:],
            PSS1[:].rearrange("p (h e) -> p h e", h=NH)[:, :, 0:EV], ALU.add))
        sm.wait(dve, t_u1)
        t_u2 = sm.inc("dve", dve.tensor_tensor(S2[:], S2[:], PSS2[:, 0:4 * EV], ALU.add))
        free("PSS1", "PSS2")

        PS_S = psum("PS_S", (128, 2048))
        for h in range(NH):
            sm.wait(pe, t_qt, t_kt, t_at, t_u2)
            ii = pe.matmul(PS_S[:, CH * h:CH * (h + 1)],
                           kt[0:FEAT, CH * h:CH * (h + 1)],
                           qt[0:FEAT, CH * h:CH * (h + 1)], start=True, stop=True)
        t_smm = sm.inc("pe", ii)
        sm.wait(act, t_smm, t_at)
        t_sq = sm.inc("act", act.activation(sqf[:], PS_S[:], AF.Square,
                                            bias=1.0, scale=0.25))
        sm.wait(dve, t_sq, t_consts, t_dmm)
        t_at = sm.inc("dve", dve.scalar_tensor_tensor(
            at[:], sqf[:], 1.0, cst["MASKH"][:], ALU.add, ALU.mult))
        free("PS_S")

        PS_N = psum("PS_N", (EV, 2048))
        for h in range(NH):
            hs = slice(CH * h, CH * (h + 1))
            jj = h % 4
            sm.wait(pe, t_at, t_vones, t_s1b, t_s2b, t_qf, t_num_ev)
            i1 = pe.matmul(PS_N[:, hs],
                           v_sb[:, 1040 * c + EV * h:1040 * c + EV * (h + 1)],
                           at[:, hs], start=True, stop=False)
            i2 = pe.matmul(PS_N[:, hs], S1b[:, EV * h:EV * (h + 1)],
                           qf1[:, hs], start=False, stop=False)
            i3 = pe.matmul(PS_N[:, hs],
                           S2b[32 * jj:32 * jj + 32, EV * (h // 4):EV * (h // 4) + EV],
                           qf2[32 * jj:32 * jj + 32, 128 * (h // 4):128 * (h // 4) + 128],
                           start=False, stop=True, tile_position=(32 * jj, 0))
        t_num = sm.inc("pe", i3)

        PS_D = psum("PS_D", (128, 16))
        for h in range(NH):
            hs = slice(CH * h, CH * (h + 1))
            jj = h % 4
            sm.wait(pe, t_at, t_s1b, t_s2b, t_qf, t_recT)
            i1 = pe.matmul(PS_D[:, h:h + 1], at[:, hs], cst["ONECB"][:],
                           start=True, stop=False)
            i2 = pe.matmul(PS_D[:, h:h + 1], qf1[:, hs],
                           S1b[:, EV * h + 64:EV * h + 65], start=False, stop=False)
            i3 = pe.matmul(PS_D[:, h:h + 1],
                           qf2[32 * jj:32 * jj + 32, 128 * (h // 4):128 * (h // 4) + 128],
                           S2b[32 * jj:32 * jj + 32,
                               EV * (h // 4) + 64:EV * (h // 4) + 65],
                           start=False, stop=True, tile_position=(32 * jj, 0))
        t_dmm = sm.inc("pe", i3)
        sm.wait(dve, t_dmm, t_tr)
        td1 = sm.inc("dve", dve.tensor_scalar_add(dens[:], PS_D[:], ATT_EPS))
        sm.wait(dve, td1)
        td2 = sm.inc("dve", dve.reciprocal(recs[:], dens[:]))
        free("PS_D")
        PS_R = psum("PS_R", (16, 128))
        sm.wait(pe, td2, t_recT)
        t_tr = sm.inc("pe", pe.transpose(PS_R[:], recs[:], cst["IDENTF"][:]))
        sm.wait(act, t_tr, t_recT, t_bc)
        t_recT = sm.inc("act", act.copy(recT[:], PS_R[:]))
        free("PS_R")
        PS_B = psum("PS_B", (EV, 2048))
        for h in range(NH):
            sm.wait(pe, t_recT, t_consts, t_y)
            ii = pe.matmul(PS_B[:, CH * h:CH * (h + 1)],
                           cst["ESEL"][:, EV * h:EV * (h + 1)], recT[:],
                           start=True, stop=True)
        t_bc = sm.inc("pe", ii)
        sm.wait(act, t_num, t_num_ev, t_y)
        t_num_ev = sm.inc("act", act.copy(nums[:], PS_N[:]))
        t_y_new = t_y
        for h in range(NH):
            hs = slice(CH * h, CH * (h + 1))
            sm.wait(dve, t_num_ev, t_bc)
            t_y_new = sm.inc("dve", dve.tensor_tensor(
                y_sb[h // 2][64 * (h % 2):64 * (h % 2) + 64, tsl],
                nums[0:64, hs], PS_B[0:64, hs], ALU.mult))
        t_y = t_y_new
        free("PS_B", "PS_N")

        sm.wait(act, t_u1, t_u2, t_num, t_dmm)
        t_s1b = sm.inc("act", act.copy(S1b[:], S1[:]))
        t_s2b = sm.inc("act", act.copy(S2b[:], S2[:]))

    o = dbg_out("ysb", (128, 8 * NT), BF16)
    if o is not None:
        sm.wait(sq, t_y)
        for k in range(8):
            sm.inc("st", sq.dma_start(out=o[:, NT * k:NT * (k + 1)], in_=y_sb[k][:]), 16)
    phase_gate(4)

    free("recT", "recs", "dens", "nums", "at", "sqf", "qrep", "repo",
         "kfo", "qf2", "qf1", "kt", "qt")

    wo = sb("wo", (128, 8 * 1024), BF16)
    sm.wait(gp, *bar())
    t_wo = wload(wo, ins["WoT"])
    t_r1 = [None] * 8
    t_pwo_rd = None
    for m in range(8):
        for ct in range(NT // 512):
            cs = slice(512 * ct, 512 * (ct + 1))
            pw = psum("pwo", (128, 512))
            for k in range(8):
                sm.wait(pe, t_wo, t_y, t_pwo_rd)
                ii = pe.matmul(pw[:], wo[:, 1024 * k + 128 * m:1024 * k + 128 * (m + 1)],
                               y_sb[k][:, cs], start=(k == 0), stop=(k == 7))
            t_mm = sm.inc("pe", ii)
            sm.wait(dve, t_mm, t_r1[m])
            t_r1[m] = sm.inc("dve", dve.tensor_tensor(r[m][:, cs], r[m][:, cs],
                                                      pw[:], ALU.add))
            t_pwo_rd = t_r1[m]
            free("pwo")
    free(*(["wo"] + [f"y{k}" for k in range(8)] +
           ["S2b", "S1b", "S2", "S1", "v_sb"]))
    o = dbg_out("r1", (128, 8 * NT), F32)
    if o is not None:
        sm.wait(sq, *t_r1)
        for k in range(8):
            sm.inc("st", sq.dma_start(out=o[:, NT * k:NT * (k + 1)], in_=r[k][:]), 16)

    phase_gate(5)
    Bcol = sb("Bcol", (32, 1), F32)
    masksb = sb("masksb", (128, NCH * E_EXP), BF16)
    scrowb = sb("scrowb", (1, NT), BF16)
    rks = sb("rks", (128, E_EXP), F32)
    brow = sb("brow", (1, E_EXP), F32)
    browb = sb("browb", (1, E_EXP), BF16)
    slotf = sb("slotf", (128, NCH), F32)
    sloti = sb("sloti", (128, NCH), I32)
    tmp32 = sb("tmp32", (128, E_EXP), F32)
    tmpc = sb("tmpc", (128, 1), F32)
    zero_sb = sb("zero_sb", (128, D), BF16)
    xst = sb("xst", (128, D), BF16)
    rtt = sb("rtt", (128, D), BF16)
    rowM = sb("rowM", (1, NT), F32)
    rowR = sb("rowR", (1, NT), F32)
    ln_bf = [sb(f"ln{k}", (128, NT), BF16) for k in range(8)]

    t_ln2 = ln_exact("l2", [t_r1[7]] * 8, rowM, rowR)
    o = dbg_out("ln2", (128, 8 * NT), BF16)
    if o is not None:
        sm.wait(sq, t_ln2)
        for k in range(8):
            sm.inc("st", sq.dma_start(out=o[:, NT * k:NT * (k + 1)], in_=ln_bf[k][:]), 16)

    phase_gate(6)
    negC = sb("negC", (32, NT), F32)
    mnrow = sb("mnrow", (16, CH), F32)
    ctok = sb("ctok", (128, NCH * E_EXP), F32)
    mneg = sb("mneg", (128, NCH), F32)
    scr0 = sb("scr0", (1, NT), F32)
    scrV = sb("scrV", (1, NT), F32)

    pBc = psum("pBc", (32, 1))
    for k in range(8):
        sm.wait(pe, t_gate, t_consts, t_r1[7], t_ln2)
        ii = pe.matmul(pBc[:], gate[:, 32 * k:32 * (k + 1)], cst["ONECF"][:, 0:1],
                       start=(k == 0), stop=(k == 7))
    t_bm = sm.inc("pe", ii)
    sm.wait(dve, t_bm)
    t_bcol = sm.inc("dve", dve.tensor_copy(Bcol[:], pBc[:]))
    free("pBc")

    t_negC = None
    for ct in range(NT // 512):
        cs = slice(512 * ct, 512 * (ct + 1))
        pAg = psum("pAg", (32, 512))
        for k in range(8):
            sm.wait(pe, t_r1[k], t_gate, t_ln2, t_negC, t_bcol)
            ii = pe.matmul(pAg[:], gate[:, 32 * k:32 * (k + 1)], r[k][:, cs],
                           start=(k == 0), stop=(k == 7))
        t_ag = sm.inc("pe", ii)
        pMb = psum("pMb", (32, 512))
        sm.wait(pe, t_negC)
        t_mb = sm.inc("pe", pe.matmul(pMb[:], cst["ONESF"][0:1, 0:32], rowM[0:1, cs],
                                      start=True, stop=True))
        sm.wait(act, t_mb, t_negC)
        t_me = sm.inc("act", act.copy(negC[:, cs], pMb[:]))
        sm.wait(dve, t_me, t_ag, t_bcol)
        t_negC = sm.inc("dve", dve.scalar_tensor_tensor(
            negC[:, cs], negC[:, cs], Bcol[:, 0:1], pAg[:], ALU.mult, ALU.subtract))
        free("pAg", "pMb")

    pT = psum("pT", (128, 512))
    t_ct = None
    for tt in range(NCH):
        sm.wait(pe, t_negC, t_ct)
        ii = pe.transpose(pT[:, 32 * tt:32 * (tt + 1)],
                          negC[:, CH * tt:CH * (tt + 1)], cst["IDENTF"][0:32, 0:32])
        t_ct = sm.inc("pe", ii)
    sm.wait(dve, t_ct)
    t_ctok = sm.inc("dve", dve.tensor_copy(ctok[:], pT[:]))
    free("pT")
    sm.wait(dve, t_ctok)
    t_mn = sm.inc("dve", dve.tensor_reduce(
        mneg[:].rearrange("p (t o) -> p t o", o=1),
        ctok[:].rearrange("p (t e) -> p t e", e=E_EXP), axis=AX.X, op=ALU.min))
    t_mk = None
    for tt in range(NCH):
        sm.wait(dve, t_mn)
        t_mk = sm.inc("dve", dve.tensor_single_scalar(
            masksb[:, 32 * tt:32 * (tt + 1)], ctok[:, 32 * tt:32 * (tt + 1)],
            mneg[:, tt:tt + 1], ALU.is_equal))
    t_mkb = t_mk

    PS_MT = psum("PS_MT", (16, 128))
    sm.wait(pe, t_mn)
    t_mt = sm.inc("pe", pe.transpose(PS_MT[:], mneg[:], cst["IDENTF"][:]))
    sm.wait(act, t_mt)
    t_mr = sm.inc("act", act.copy(mnrow[:], PS_MT[:]))
    free("PS_MT")
    sm.wait(sq, t_mr)
    t_rw = sm.inc("st", sq.dma_start(
        out=ROWB.rearrange("(a b) -> a b", a=16), in_=mnrow[:]), 16)
    sm.wait(sq, t_rw)
    t_rr = sm.inc("ld", sq.dma_start(
        out=scr0[0:1, :], in_=ROWB.rearrange("(o t) -> o t", o=1)), 16)
    sm.wait(dve, t_rr, t_ln2)
    t_val = sm.inc("dve", dve.scalar_tensor_tensor(
        scrV[0:1, :], scr0[0:1, :], -1.0, rowR[0:1, :], ALU.mult, ALU.mult))
    sm.wait(act, t_val)
    t_sc = sm.inc("act", act.activation(scr0[0:1, :], scrV[0:1, :], AF.Sigmoid))
    sm.wait(dve, t_sc)
    t_scb = sm.inc("dve", dve.tensor_copy(scrowb[:], scr0[0:1, :]))
    o = dbg_out("scores", (1, NT), F32)
    if o is not None:
        sm.wait(sq, t_sc)
        sm.inc("st", sq.dma_start(out=o, in_=scr0[0:1, :]), 16)
    o = dbg_out("masks", (128, NCH * E_EXP), BF16)
    if o is not None:
        sm.wait(sq, t_mk)
        sm.inc("st", sq.dma_start(out=o, in_=masksb[:]), 16)
    sm.wait(sq, t_ctok, t_rw, t_scb, t_mk)
    free("scrV", "scr0", "mneg", "ctok", "mnrow", "negC")

    phase_gate(7)
    xs = [sb(f"xs{k}", (128, NT), BF16) for k in range(8)]
    t_xs = [None] * 8
    for ct in range(NT // 512):
        cs = slice(512 * ct, 512 * (ct + 1))
        pSc = psum("pSc", (128, 512))
        sm.wait(pe, t_scb, t_xs[7])
        t_sb_ = sm.inc("pe", pe.matmul(pSc[:], cst["ONESB"][0:1, 0:128], scrowb[0:1, cs],
                                       start=True, stop=True))
        for k in range(8):
            sm.wait(dve, t_sb_, t_ln2, ("st", 1))
            t_xs[k] = sm.inc("dve", dve.tensor_tensor(xs[k][:, cs], ln_bf[k][:, cs],
                                                      pSc[:], ALU.mult))
        free("pSc")

    sgws = [sb("sgws0", (128, 8 * 512), BF16)]
    suws = [sb("suws0", (128, 8 * 512), BF16)]
    sdws = [sb("sdws0", (128, 4 * 1024), BF16)]
    silu_sb = sb("silu_sb", (128, 256), BF16)
    tsh = sb("tsh", (128, 256), BF16)
    t_sw_used = [None]
    t_r2 = list(t_r1)

    sm.wait(gp, *bar())
    for isl in range(4):
        b = 0
        sm.wait(gp, t_sw_used[b])
        tg1 = sm.inc("gld", gp.dma_start(
            out=sgws[b][:].rearrange("p (k c) -> p k c", k=8),
            in_=ins["SGW"][:, 512 * isl:512 * (isl + 1)].rearrange("(k p) c -> p k c", p=128)), 16)
        tg2 = sm.inc("gld", gp.dma_start(
            out=suws[b][:].rearrange("p (k c) -> p k c", k=8),
            in_=ins["SUW"][:, 512 * isl:512 * (isl + 1)].rearrange("(k p) c -> p k c", p=128)), 16)
        tg3 = sm.inc("gld", gp.dma_start(
            out=sdws[b][:].rearrange("p (k c) -> p k c", k=4),
            in_=ins["SDW"][512 * isl:512 * (isl + 1), :].rearrange("(k p) c -> p k c", p=128)), 16)
        t_si = t_th = None
        for ct in range(NT // 256):
            cs = slice(256 * ct, 256 * (ct + 1))
            pDn = psum("pDn", (128, 2048))
            for it in range(4):
                pG = psum("pG", (128, 256))
                pU = psum("pU", (128, 256))
                for k in range(8):
                    sm.wait(pe, tg1, tg2, t_ln2, t_si, t_xs[7])
                    i1 = pe.matmul(pG[:], sgws[b][:, 512 * k + 128 * it:512 * k + 128 * (it + 1)],
                                   ln_bf[k][:, cs], start=(k == 0), stop=(k == 7))
                tgm = sm.inc("pe", i1)
                for k in range(8):
                    sm.wait(pe, tg2, t_th)
                    i2 = pe.matmul(pU[:], suws[b][:, 512 * k + 128 * it:512 * k + 128 * (it + 1)],
                                   ln_bf[k][:, cs], start=(k == 0), stop=(k == 7))
                tum = sm.inc("pe", i2)
                sm.wait(act, tgm, t_th)
                t_si = sm.inc("act", act.activation(silu_sb[:], pG[:], AF.Silu))
                sm.wait(dve, t_si, tum, t_th)
                t_th = sm.inc("dve", dve.tensor_tensor(tsh[:], silu_sb[:], pU[:],
                                                       ALU.mult))
                free("pG", "pU")
                for m in range(8):
                    sm.wait(pe, t_th, tg3, t_r2[7])
                    i3 = pe.matmul(pDn[:, 256 * m:256 * (m + 1)],
                                   sdws[b][:, 1024 * it + 128 * m:1024 * it + 128 * (m + 1)],
                                   tsh[:], start=(it == 0 and m % 2 == 0), stop=(it == 3),
                                   skip_group_check=True)
            t_dn = sm.inc("pe", i3)
            for m in range(8):
                sm.wait(dve, t_dn, t_r2[m])
                t_r2[m] = sm.inc("dve", dve.tensor_tensor(
                    r[m][:, cs], r[m][:, cs], pDn[:, 256 * m:256 * (m + 1)], ALU.add))
            free("pDn")
        t_sw_used[b] = t_r2[7]
    t_shared_done = t_dn

    free("tsh", "silu_sb", "sdws0", "suws0", "sgws0")

    phase_gate(8)
    sm.wait(dve, t_consts)
    t_br = sm.inc("dve", dve.memset(brow[:], 0.0))
    sm.wait(dve, t_br)
    t_brb = sm.inc("dve", dve.tensor_copy(browb[:], brow[:]))
    t_slot = None
    for tt in range(NCH):
        ms = slice(32 * tt, 32 * (tt + 1))
        pRk = psum("pRk", (128, 32))
        pBb = psum("pBb", (128, 32))
        pTt = psum("pTt", (1, 32))
        sm.wait(pe, t_mkb, t_consts, t_brb, t_r2[7], t_shared_done)
        i1 = pe.matmul(pRk[:], cst["UEXCL"][:], masksb[:, ms], start=True, stop=True)
        t_rk = sm.inc("pe", i1)
        i2 = pe.matmul(pBb[:], cst["ONESB"][0:1, 0:128],
                       browb[:], start=True, stop=True)
        t_bb = sm.inc("pe", i2)
        i3 = pe.matmul(pTt[0:1, :], cst["ONECB"][:], masksb[:, ms], start=True, stop=True)
        t_tt = sm.inc("pe", i3)
        sm.wait(dve, t_rk, t_bb, t_slot)
        t1 = sm.inc("dve", dve.tensor_copy(rks[:], pRk[:]))
        sm.wait(dve, t1)
        t2 = sm.inc("dve", dve.tensor_tensor(rks[:], rks[:], pBb[:], ALU.add))
        sm.wait(dve, t2, t_tt)
        t3 = sm.inc("dve", dve.tensor_tensor(brow[:], brow[:], pTt[0:1, :], ALU.add))
        sm.wait(dve, t3)
        t_brb = sm.inc("dve", dve.tensor_copy(browb[:], brow[:]))
        free("pRk", "pBb", "pTt")
        sm.wait(dve, t2, t_mk)
        t4 = sm.inc("dve", dve.tensor_tensor(tmp32[:], rks[:], cst["EIDX"][:], ALU.add))
        sm.wait(dve, t4)
        t5 = sm.inc("dve", dve.tensor_tensor(tmp32[:], tmp32[:], masksb[:, ms], ALU.mult))
        sm.wait(dve, t5)
        t6 = sm.inc("dve", dve.tensor_reduce(
            slotf[:, tt:tt + 1], tmp32[:], axis=AX.X, op=ALU.add))
        sm.wait(dve, t6)
        t7 = sm.inc("dve", dve.tensor_tensor(tmp32[:], rks[:], masksb[:, ms], ALU.mult))
        sm.wait(dve, t7)
        t8 = sm.inc("dve", dve.tensor_reduce(tmpc[:], tmp32[:], axis=AX.X, op=ALU.add))
        sm.wait(dve, t8)
        t9 = sm.inc("dve", dve.tensor_single_scalar(tmpc[:], tmpc[:], float(CAP - 0.5),
                                                    ALU.is_gt))
        sm.wait(dve, t6, t9)
        ta = sm.inc("dve", dve.tensor_scalar(
            tmp32[:, 0:1], slotf[:, tt:tt + 1], -1.0, float(TRASH), ALU.mult, ALU.add))
        sm.wait(dve, ta)
        tb = sm.inc("dve", dve.tensor_tensor(tmp32[:, 0:1], tmp32[:, 0:1], tmpc[:],
                                             ALU.mult))
        sm.wait(dve, tb)
        tcp = sm.inc("dve", dve.tensor_tensor(slotf[:, tt:tt + 1], slotf[:, tt:tt + 1],
                                              tmp32[:, 0:1], ALU.add))
        sm.wait(dve, tcp)
        t_slot = sm.inc("dve", dve.tensor_copy(sloti[:, tt:tt + 1], slotf[:, tt:tt + 1]))
    o = dbg_out("sloti", (128, NCH), I32)
    if o is not None:
        sm.wait(sq, t_slot)
        sm.inc("st", sq.dma_start(out=o, in_=sloti[:]), 16)

    sm.wait(dve, t_slot)
    t_z = sm.inc("dve", dve.memset(zero_sb[:], 0.0))
    t_zs = None
    sm.wait(sq, t_z)
    for blk in range(NSLOT // 128):
        t_zs = sm.inc("st", sq.dma_start(out=XSG[128 * blk:128 * (blk + 1), :],
                                         in_=zero_sb[:]), 16)
    t_zs = sm.inc("st", sq.dma_start(out=XSG[NSLOT - 16:NSLOT, :], in_=zero_sb[0:16, :]), 16)
    t_zr = sm.inc("st", sq.dma_start(out=RTD[TRASH:TRASH + 16, :], in_=zero_sb[0:16, :]), 16)

    t_xst = None
    t_scat = None
    for tt in range(NCH):
        tsl = slice(CH * tt, CH * (tt + 1))
        pX = psum("pX", (128, 128), BF16)
        t_tp = None
        for k in range(8):
            sm.wait(pe, t_xs[k], t_consts, t_xst, t_slot)
            t_tp = sm.inc("pe", pe.transpose(pX[:], xs[k][:, tsl], cst["IDENTB"][:]))
            sm.wait(act, t_tp, t_scat)
            t_xst = sm.inc("act", act.copy(xst[:, 128 * k:128 * (k + 1)], pX[:]))
        free("pX")
        sm.wait(gp, t_xst, t_zs, t_slot)
        t_scat = sm.inc("sct", gp.indirect_dma_start(
            out=XSG, out_offset=IndirectOffsetOnAxis(ap=sloti[:, tt:tt + 1], axis=0),
            in_=xst[:], in_offset=None), 16)
    t_scat_all = t_scat

    free(*[f"xs{k}" for k in range(8)])
    free(*[f"ln{k}" for k in range(8)] + ["rowR", "rowM"])

    xg = [sb(f"xg{k}", (128, E_EXP * CAP), BF16) for k in range(8)]
    t_xg = [None] * 8
    for k in range(8):
        sm.wait(sq, t_scat_all, t_r2[7], t_shared_done)
        t_xg[k] = sm.inc("ld", sq.dma_start(
            out=xg[k][:], in_=XSG[0:E_EXP * CAP, 128 * k:128 * (k + 1)],
            transpose=True), 16)

    NSL = 8
    rgws = [sb(f"rgws{b}", (128, 8 * 256), BF16) for b in range(2)]
    ruws = [sb(f"ruws{b}", (128, 8 * 256), BF16) for b in range(2)]
    rdws = [sb(f"rdws{b}", (128, 2 * 1024), BF16) for b in range(2)]
    te_sb = sb("te_sb", (128, 2 * CAP), BF16)
    sil2 = sb("sil2", (128, 2 * CAP), BF16)
    rts = sb("rts", (128, 1024), BF16)

    t_buf_used = [None, None]
    t_rts = None
    t_rt_store = None
    t_si2 = t_te = t_dn_pe = None
    nsl_ctr = 0

    for e in range(E_EXP):
        PRo = psum("PRo", (128, 1024))
        for isl in range(NSL):
            b = nsl_ctr % 2
            nsl_ctr += 1
            wsem = "wt%d" % b
            sm.wait(gp, t_buf_used[b], t_r2[7], t_shared_done, ("sct", 1))
            tg1 = sm.inc(wsem, gp.dma_start(
                out=rgws[b][:].rearrange("p (k c) -> p k c", k=8),
                in_=ins["RGW"][e, :, 256 * isl:256 * (isl + 1)].rearrange("(k p) c -> p k c", p=128)), 16)
            tg2 = sm.inc(wsem, gp.dma_start(
                out=ruws[b][:].rearrange("p (k c) -> p k c", k=8),
                in_=ins["RUW"][e, :, 256 * isl:256 * (isl + 1)].rearrange("(k p) c -> p k c", p=128)), 16)
            tg3 = sm.inc(wsem, gp.dma_start(
                out=rdws[b][:].rearrange("p (k c) -> p k c", k=2),
                in_=ins["RDW"][e, 256 * isl:256 * (isl + 1), :].rearrange("(k p) c -> p k c", p=128)), 16)
            t_te_it = [None, None]
            pGs = [psum(f"pG2{x}", (128, CAP)) for x in range(2)]
            pUs = [psum(f"pU2{x}", (128, CAP)) for x in range(2)]
            for it in range(2):
                pG, pU = pGs[it], pUs[it]
                for k in range(8):
                    sm.wait(pe, tg1, t_xg[k], t_si2)
                    i1 = pe.matmul(pG[:], rgws[b][:, 256 * k + 128 * it:256 * k + 128 * (it + 1)],
                                   xg[k][:, CAP * e:CAP * (e + 1)],
                                   start=(k == 0), stop=(k == 7))
                tgm = sm.inc("pe", i1)
                for k in range(8):
                    sm.wait(pe, tg2, t_te_it[it] or t_te)
                    i2 = pe.matmul(pU[:], ruws[b][:, 256 * k + 128 * it:256 * k + 128 * (it + 1)],
                                   xg[k][:, CAP * e:CAP * (e + 1)],
                                   start=(k == 0), stop=(k == 7))
                tum = sm.inc("pe", i2)
                sm.wait(act, tgm, t_te)
                t_si2 = sm.inc("act", act.activation(sil2[:, CAP * it:CAP * (it + 1)], pG[:], AF.Silu))
                sm.wait(dve, t_si2, tum, t_te, t_dn_pe)
                t_te = sm.inc("dve", dve.tensor_tensor(
                    te_sb[:, CAP * it:CAP * (it + 1)],
                    sil2[:, CAP * it:CAP * (it + 1)], pU[:], ALU.mult))
                t_te_it[it] = t_te
            for it in range(2):
                for nn in range(2):
                    sm.wait(pe, t_te_it[it], tg3, t_rts, t_r2[7])
                    i3 = pe.matmul(PRo[:, 512 * nn:512 * (nn + 1)],
                                   te_sb[0:128, CAP * it:CAP * (it + 1)],
                                   rdws[b][:, 1024 * it + 512 * nn:1024 * it + 512 * (nn + 1)],
                                   start=(isl == 0 and it == 0), stop=(isl == NSL - 1 and it == 1),
                                   skip_group_check=True)
            t_dn = sm.inc("pe", i3)
            free("pG20", "pG21", "pU20", "pU21")
            t_buf_used[b] = t_dn
            t_dn_pe = t_dn
        sm.wait(act, t_dn, t_rt_store)
        t_rts = sm.inc("act", act.copy(rts[:], PRo[:]))
        free("PRo")
        sm.wait(sq, t_rts, t_zr)
        t_rt_store = sm.inc("st", sq.dma_start(
            out=RTD[CAP * e:CAP * (e + 1), :], in_=rts[:]), 16)

    free(*(["rts", "sil2", "te_sb", "rdws0", "rdws1", "ruws0", "ruws1",
            "rgws0", "rgws1"] + [f"xg{k}" for k in range(8)]))

    phase_gate(9)
    t_rtt = None
    t_pX2_rd = None
    t_out = [None] * 8
    for tt in range(NCH):
        tsl = slice(CH * tt, CH * (tt + 1))
        sm.wait(gp, t_rt_store, t_slot, t_rtt)
        t_g = sm.inc("sct", gp.indirect_dma_start(
            out=rtt[:], out_offset=None,
            in_=RTD, in_offset=IndirectOffsetOnAxis(ap=sloti[:, tt:tt + 1], axis=0)), 16)
        for k in range(8):
            pX = psum("pX2", (128, 128), BF16)
            sm.wait(pe, t_g, t_consts, t_pX2_rd, t_rts, t_te)
            t_tp = sm.inc("pe", pe.transpose(pX[:], rtt[:, 128 * k:128 * (k + 1)],
                                             cst["IDENTB"][:]))
            sm.wait(dve, t_tp, t_r2[k], t_out[k])
            t_out[k] = sm.inc("dve", dve.tensor_tensor(r[k][:, tsl], r[k][:, tsl],
                                                       pX[:], ALU.add))
            t_pX2_rd = t_out[k]
            free("pX2")
        t_rtt = t_out[7]

    t_fin = None
    for k in range(8):
        sm.wait(sq, t_out[k])
        t_fin = sm.inc("st", sq.dma_start(out=OUT[128 * k:128 * (k + 1), :], in_=r[k][:]), 16)
    sm.wait(sq, t_fin)

    stack.close()
    return nc, dbg


def _noop():
    pass


def kernel(L0_attn_hidden_states, L0_attn_Wq, L0_attn_Wk, L0_attn_Wv,
           L0_attn_Wo, L0_ffn_gate_w, L0_ffn_shared_gate_w, L0_ffn_shared_up_w,
           L0_ffn_shared_down_w, L0_ffn_routed_gate_w, L0_ffn_routed_up_w,
           L0_ffn_routed_down_w):
    x = np.asarray(L0_attn_hidden_states, np.float32).reshape(T_FULL, D)
    XT = np.ascontiguousarray(x.T)
    shared = {
        "WqT": np.ascontiguousarray(np.asarray(L0_attn_Wq, np.float32).T),
        "WkT": np.ascontiguousarray(np.asarray(L0_attn_Wk, np.float32).T),
        "WvT": np.ascontiguousarray(np.asarray(L0_attn_Wv, np.float32).T),
        "WoT": np.ascontiguousarray(np.asarray(L0_attn_Wo, np.float32).T),
        "GATE": np.ascontiguousarray(np.asarray(L0_ffn_gate_w, np.float32)),
        "SGW": np.ascontiguousarray(np.asarray(L0_ffn_shared_gate_w, np.float32)),
        "SUW": np.ascontiguousarray(np.asarray(L0_ffn_shared_up_w, np.float32)),
        "SDW": np.ascontiguousarray(np.asarray(L0_ffn_shared_down_w, np.float32)),
        "RGW": np.ascontiguousarray(np.asarray(L0_ffn_routed_gate_w, np.float32)),
        "RUW": np.ascontiguousarray(np.asarray(L0_ffn_routed_up_w, np.float32)),
        "RDW": np.ascontiguousarray(np.asarray(L0_ffn_routed_down_w, np.float32)),
    }
    shared.update(CONSTS)
    in_maps = []
    for c in range(NCORES):
        t0 = NT * c
        xpre = np.zeros((D, PRE), np.float32)
        if t0 > 0:
            xpre[:, :t0] = XT[:, :t0]
        pmt = np.zeros((128, PRE // CH), np.float32)
        pmt[:, :t0 // CH] = 1.0
        m = dict(shared)
        m["XS_T"] = np.ascontiguousarray(XT[:, t0:t0 + NT])
        m["XPRE_T"] = xpre
        m["PMT"] = pmt
        in_maps.append(m)
    nc, _ = build(tuple(DEBUG_OUTS))
    res = run_bass_kernel_spmd(nc, in_maps, core_ids=list(range(NCORES)))
    out = np.concatenate(
        [np.asarray(res.results[c]["OUT"], np.float32).T for c in range(NCORES)],
        axis=0)
    return out.reshape(1, T_FULL, D)


# version 07
# speedup vs baseline: 1.2503x; 1.2503x over previous
import contextlib
import numpy as np
import ml_dtypes
import concourse.bass as bass
import concourse.mybir as mybir
from concourse.bass_utils import run_bass_kernel_spmd
from concourse.bass_core_rust import IndirectOffsetOnAxis

F32 = mybir.dt.float32
BF16 = mybir.dt.bfloat16
I32 = mybir.dt.int32
AF = mybir.ActivationFunctionType
ALU = mybir.AluOpType
AX = mybir.AxisListType
BF = ml_dtypes.bfloat16

NCORES = 8
D = 1024
T_FULL = 16384
NT = T_FULL // NCORES
NH, FEAT, HD = 16, 16, 64
CH = 128
NCH = NT // CH
PRE = T_FULL - NT
PT = 512
NPT = PRE // PT
DE = 160
EV = 65
E_EXP = 32
I_FFN = 2048
CAP = 128
TRASH = E_EXP * CAP
NSLOT = E_EXP * CAP + 16
LN_EPS = 1e-5
ATT_EPS = 1e-12

DEBUG_OUTS = []
PHASE_LIMIT = 99


def _tri_consts():
    pairs = [(i, j) for i in range(FEAT) for j in range(i, FEAT)]
    R1 = np.zeros((DE, FEAT + 1), np.float32)
    R2k = np.zeros((DE, FEAT + 1), np.float32)
    R2q = np.zeros((DE, FEAT + 1), np.float32)
    for f, (i, j) in enumerate(pairs):
        R1[f, i] = 1.0
        R2k[f, j] = 1.0
        R2q[f, j] = (1.0 / 32.0) if i == j else (1.0 / 16.0)
    for i in range(FEAT):
        R1[136 + i, i] = 1.0
        R2k[136 + i, FEAT] = 1.0
        R2q[136 + i, FEAT] = 0.25
    R1[152, FEAT] = 1.0
    R2k[152, FEAT] = 1.0
    R2q[152, FEAT] = 1.0
    return R1, R2k, R2q


def _consts():
    c = {}
    R1, R2k, R2q = _tri_consts()
    c["R1KT"] = np.ascontiguousarray(R1.T).astype(BF)
    c["R2KT"] = np.ascontiguousarray(R2k.T).astype(BF)
    c["R12KT"] = np.ascontiguousarray(np.concatenate([R1.T, R2k.T], axis=1)).astype(BF)
    c["R1QD1"] = np.ascontiguousarray(R1[0:128].T).astype(BF)
    c["R2QD1"] = np.ascontiguousarray(R2q[0:128].T).astype(BF)
    c["R1QD2"] = np.ascontiguousarray(R1[128:160].T).astype(BF)
    c["R2QD2"] = np.ascontiguousarray(R2q[128:160].T).astype(BF)
    c["IDENTF"] = np.eye(128, dtype=np.float32)
    c["IDENTB"] = np.eye(128, dtype=BF)
    mh = 0.5 * np.triu(np.ones((CH, CH), np.float32))
    c["MASKH"] = np.tile(mh, (1, NH)).astype(BF)
    c["UEXCL"] = np.triu(np.ones((CH, CH), np.float32), 1).astype(BF)
    c["EIDX"] = np.tile(np.arange(E_EXP, dtype=np.float32)[None, :] * CAP,
                        (128, 1)).astype(np.float32)
    es = np.zeros((FEAT, NH * EV), np.float32)
    for h in range(NH):
        es[h, h * EV:(h + 1) * EV] = 1.0
    c["ESEL"] = es.astype(BF)
    c["ONESB"] = np.ones((128, 256), BF)
    c["ONECB"] = np.ones((128, 1), BF)
    c["ONECF"] = np.ones((128, 1), np.float32)
    c["ONESF"] = np.ones((1, 128), np.float32)
    return c


CONSTS = _consts()


class S:
    def __init__(self, nc, stack):
        self.h = {}
        self.cnt = {}
        for name in ("pe", "dve", "act", "gp", "ld", "gld", "st", "sct", "wt0", "wt1"):
            self.h[name] = stack.enter_context(nc.semaphore(name))
            self.cnt[name] = 0

    def inc(self, name, inst, n=1):
        inst.then_inc(self.h[name], n)
        self.cnt[name] += n
        return (name, self.cnt[name])

    DMA_SEMS = ("ld", "gld", "st", "sct", "wt0", "wt1")

    def wait(self, eng, *toks):
        seen = {}
        for t in toks:
            if t is None:
                continue
            name, v = t
            if name in self.DMA_SEMS:
                v = self.cnt[name]
            if v > 0:
                seen[name] = max(seen.get(name, 0), v)
        for name, v in seen.items():
            eng.wait_ge(self.h[name], v)


class _EarlyFinishDone(Exception):
    pass


def build(debug_outs=()):
    try:
        return _build(debug_outs)
    except _EarlyFinishDone as ex:
        return ex.args[0]


def _build(debug_outs=()):
    nc = bass.Bass("TRN2", target_bir_lowering=False)
    stack = contextlib.ExitStack()

    def dram_in(name, shape, dtype=F32):
        return nc.dram_tensor(name, list(shape), dtype, kind="ExternalInput").ap()

    ins = {}
    ins["XS_T"] = dram_in("XS_T", (D, NT))
    ins["XPRE_T"] = dram_in("XPRE_T", (D, PRE))
    ins["PMT"] = dram_in("PMT", (128, PRE // CH))
    for w, shp in (("WqT", (D, 256)), ("WkT", (D, 256)), ("WvT", (D, D)),
                   ("WoT", (D, D)), ("GATE", (D, E_EXP)),
                   ("SGW", (D, I_FFN)), ("SUW", (D, I_FFN)), ("SDW", (I_FFN, D)),
                   ("RGW", (E_EXP, D, I_FFN)), ("RUW", (E_EXP, D, I_FFN)),
                   ("RDW", (E_EXP, I_FFN, D))):
        ins[w] = dram_in(w, shp)
    for k, v in CONSTS.items():
        ins[k] = dram_in(k, v.shape, BF16 if v.dtype == BF else F32)

    OUT = nc.dram_tensor("OUT", [D, NT], F32, kind="ExternalOutput").ap()
    dbg = {}

    def dbg_out(name, shape, dtype=F32):
        if name in debug_outs:
            dbg[name] = nc.dram_tensor("dbg_" + name, list(shape), dtype,
                                       kind="ExternalOutput").ap()
        return dbg.get(name)

    KBOWN = nc.dram_tensor("KBOWN", [2, FEAT + 1, NH, NT], BF16, kind="Internal").ap()
    KBPRE = nc.dram_tensor("KBPRE", [2, FEAT + 1, NH, PT], BF16, kind="Internal").ap()
    ROWB = nc.dram_tensor("ROWB", [NT], F32, kind="Internal").ap()
    XSG = nc.dram_tensor("XSG", [NSLOT, D], BF16, kind="Internal").ap()
    RTD = nc.dram_tensor("RTD", [NSLOT, D], BF16, kind="Internal").ap()

    pe, dve, act, gp, sq = nc.tensor, nc.vector, nc.scalar, nc.gpsimd, nc.sync
    sm = S(nc, stack)

    def bar():
        return (("pe", sm.cnt["pe"]), ("dve", sm.cnt["dve"]),
                ("act", sm.cnt["act"]), ("gp", sm.cnt["gp"]), ("st", 1),
                ("ld", 1), ("gld", 1))

    def phase_gate(n):
        if PHASE_LIMIT > n:
            return
        for name in ("pe", "dve", "act", "gp", "ld", "gld", "st", "sct", "wt0", "wt1"):
            if sm.cnt[name] > 0:
                sq.wait_ge(sm.h[name], sm.cnt[name])
        t_fin = None
        for k in range(8):
            t_fin = sm.inc("st", sq.dma_start(
                out=OUT[128 * k:128 * (k + 1), :], in_=r[k][:]), 16)
        sm.wait(sq, t_fin)
        stack.close()
        raise _EarlyFinishDone((nc, dbg))
    frees = {}
    _seq = [0]

    def _uname(name):
        _seq[0] += 1
        return f"{name}_{_seq[0]}"

    def sb(name, shape, dtype):
        assert name not in frees, name
        cm = nc.sbuf_tensor(_uname(name), list(shape), dtype)
        t = cm.__enter__()
        frees[name] = cm
        return t

    def psum(name, shape, dtype=F32):
        assert name not in frees, name
        cm = nc.psum_tensor(_uname(name), list(shape), dtype)
        t = cm.__enter__()
        frees[name] = cm
        return t

    def free(*names):
        order = list(frees.keys())
        for n in sorted(names, key=lambda n: -order.index(n)):
            frees.pop(n).__exit__(None, None, None)

    r = [sb(f"r{k}", (128, NT), F32) for k in range(8)]
    cst = {}
    for k, v in CONSTS.items():
        cst[k] = sb("c_" + k, v.shape, BF16 if v.dtype == BF else F32)
    gate = sb("gate", (128, 8 * E_EXP), F32)
    pmt = sb("pmt", (128, PRE // CH), F32)

    t_r_in = []
    for k in range(8):
        t_r_in.append(sm.inc("ld", sq.dma_start(
            out=r[k][:], in_=ins["XS_T"][128 * k:128 * (k + 1), :]), 16))
    for k in CONSTS:
        sm.inc("ld", sq.dma_start(out=cst[k][:], in_=ins[k]), 16)
    sm.inc("ld", sq.dma_start(out=pmt[:], in_=ins["PMT"]), 16)
    t_gate = sm.inc("ld", sq.dma_start(
        out=gate[:].rearrange("p (k c) -> p k c", k=8),
        in_=ins["GATE"].rearrange("(k p) c -> p k c", p=128)), 16)
    t_consts = ("ld", sm.cnt["ld"])

    def wload(dst, src, ktiles=8):
        return sm.inc("gld", gp.dma_start(
            out=dst[:].rearrange("p (k c) -> p k c", k=ktiles),
            in_=src.rearrange("(k p) c -> p k c", p=128)), 16)


    v_sb = sb("v_sb", (128, NCH * NH * EV), BF16)
    S1 = sb("S1", (128, NH * EV), F32)
    S2 = sb("S2", (128, 4 * EV), F32)
    wk = sb("wk", (128, 8 * 256), BF16)
    wv = sb("wv", (128, 8 * 1024), BF16)
    wq = sb("wq", (128, 8 * 256), BF16)
    t_wq = wload(wq, ins["WqT"])
    t_wk = wload(wk, ins["WkT"])
    t_wv = wload(wv, ins["WvT"])

    ln_bf = [sb(f"ln{k}", (128, NT), BF16) for k in range(8)]

    def ln_exact(tag, t_src, rowM, rowR):
        sqs = sb("sqs" + tag, (128, 512), F32)
        rowS = sb("rowS" + tag, (1, NT), F32)
        rowMb = sb("rowMb" + tag, (1, NT), BF16)
        rowRb = sb("rowRb" + tag, (1, NT), BF16)
        last = None
        t_sqs = None
        for ct in range(NT // 512):
            cs = slice(512 * ct, 512 * (ct + 1))
            pA = psum(f"{tag}A", (1, 512))
            pB = psum(f"{tag}B", (1, 512))
            for k in range(8):
                sm.wait(pe, t_src[k], t_consts, last)
                i1 = pe.matmul(pA[0:1, :], cst["ONECF"][:], r[k][:, cs],
                               start=(k == 0), stop=(k == 7))
                sm.wait(act, t_src[k], t_sqs)
                t2 = sm.inc("act", act.activation(sqs[:], r[k][:, cs], AF.Square))
                sm.wait(pe, t2)
                i3 = pe.matmul(pB[0:1, :], cst["ONECF"][:], sqs[:],
                               start=(k == 0), stop=(k == 7))
                t_sqs = sm.inc("pe", i3)
            t_mm = sm.inc("pe", i1)
            sm.wait(dve, t_mm)
            t1 = sm.inc("dve", dve.tensor_scalar_mul(rowM[0:1, cs], pA[0:1, :], 1.0 / D))
            sm.wait(dve, t_sqs, t1)
            t2 = sm.inc("dve", dve.tensor_scalar_mul(rowS[0:1, cs], pB[0:1, :], 1.0 / D))
            sm.wait(dve, t1)
            t3 = sm.inc("dve", dve.tensor_tensor(rowR[0:1, cs], rowM[0:1, cs],
                                                 rowM[0:1, cs], ALU.mult))
            sm.wait(dve, t2, t3)
            t4 = sm.inc("dve", dve.tensor_tensor(rowS[0:1, cs], rowS[0:1, cs],
                                                 rowR[0:1, cs], ALU.subtract))
            sm.wait(dve, t4)
            t5 = sm.inc("dve", dve.tensor_scalar_add(rowS[0:1, cs], rowS[0:1, cs], LN_EPS))
            sm.wait(dve, t5)
            t6 = sm.inc("dve", dve.reciprocal(rowS[0:1, cs], rowS[0:1, cs]))
            sm.wait(act, t6)
            t7 = sm.inc("act", act.activation(rowR[0:1, cs], rowS[0:1, cs], AF.Sqrt))
            sm.wait(dve, t1, t7)
            t8 = sm.inc("dve", dve.tensor_copy(rowMb[0:1, cs], rowM[0:1, cs]))
            sm.wait(dve, t8)
            t9 = sm.inc("dve", dve.tensor_copy(rowRb[0:1, cs], rowR[0:1, cs]))
            free(f"{tag}A", f"{tag}B")
            pM = psum(f"{tag}M", (128, 512))
            pR = psum(f"{tag}R", (128, 512))
            sm.wait(pe, t8)
            t_mb = sm.inc("pe", pe.matmul(pM[:], cst["ONESB"][0:1, 0:128], rowMb[0:1, cs],
                                          start=True, stop=True))
            sm.wait(pe, t9)
            t_rb = sm.inc("pe", pe.matmul(pR[:], cst["ONESB"][0:1, 0:128], rowRb[0:1, cs],
                                          start=True, stop=True))
            for k in range(8):
                sm.wait(dve, t_mb, t_rb, t_sqs)
                ta = sm.inc("dve", dve.tensor_tensor(sqs[:], r[k][:, cs], pM[:],
                                                     ALU.subtract))
                t_sqs = ta
                sm.wait(dve, ta)
                last = sm.inc("dve", dve.tensor_tensor(ln_bf[k][:, cs], sqs[:], pR[:],
                                                       ALU.mult))
                t_sqs = last
            free(f"{tag}M", f"{tag}R")
        free("rowRb" + tag, "rowMb" + tag, "rowS" + tag, "sqs" + tag)
        return last

    rowM1 = sb("rowM1", (1, NT), F32)
    rowR1 = sb("rowR1", (1, NT), F32)
    t_ln1 = ln_exact("l1", t_r_in, rowM1, rowR1)
    o = dbg_out("ln1", (128, 8 * NT), BF16)
    if o is not None:
        sm.wait(sq, t_ln1)
        for k in range(8):
            sm.inc("st", sq.dma_start(out=o[:, NT * k:NT * (k + 1)], in_=ln_bf[k][:]), 16)
    phase_gate(1)

    qknat = sb("qknat", (128, NT), BF16)

    t_kb = []
    tg = None
    for iqk, wmat, t_w in ((0, wq, t_wq), (1, wk, t_wk)):
        for grp in range(2):
            for ct in range(NT // 512):
                cs = slice(512 * ct, 512 * (ct + 1))
                pg = psum("pj", (128, 512))
                for k in range(8):
                    sm.wait(pe, t_w, t_ln1, tg)
                    ii = pe.matmul(pg[:], wmat[:, 256 * k + 128 * grp:256 * k + 128 * (grp + 1)],
                                   ln_bf[k][:, cs], start=(k == 0), stop=(k == 7))
                t_mm = sm.inc("pe", ii)
                sm.wait(act, t_mm, tg, t_kb[-1] if t_kb else None)
                tg = sm.inc("act", act.copy(qknat[:, cs], pg[:]))
                free("pj")
            sm.wait(sq, tg)
            t_kb.append(sm.inc("st", sq.dma_start(
                out=KBOWN[iqk, 0:FEAT, 8 * grp:8 * (grp + 1), :].rearrange("f h t -> h f t"),
                in_=qknat[:]), 16))
    for iqk in range(2):
        sm.wait(sq, t_consts)
        t_kb.append(sm.inc("st", sq.dma_start(
            out=KBOWN[iqk, FEAT, :, :].rearrange("h t -> (h t)").rearrange("(a b) -> a b", a=128),
            in_=cst["ONESB"][:, 0:256]), 16))
    t_kb_all = t_kb[-1]
    free("qknat")

    t_vown = tg
    for tt in range(NCH):
        ts_ = slice(128 * tt, 128 * (tt + 1))
        for half in range(2):
            pv = psum("pv", (128, 512))
            for k in range(8):
                sm.wait(pe, t_wv, t_ln1, t_vown)
                ii = pe.matmul(pv[:], ln_bf[k][:, ts_],
                               wv[:, 1024 * k + 512 * half:1024 * k + 512 * (half + 1)],
                               start=(k == 0), stop=(k == 7))
            t_mm = sm.inc("pe", ii)
            sm.wait(act, t_mm, t_vown)
            t_vown = sm.inc("act", act.copy(
                v_sb[:].rearrange("p (t h e) -> p t h e", t=NCH, h=NH)[:, tt, 8 * half:8 * (half + 1), 0:64],
                pv[:].rearrange("p (h e) -> p h e", h=8)))
            free("pv")
    sm.wait(dve, t_vown)
    t_vones = sm.inc("dve", dve.memset(
        v_sb[:].rearrange("p (t h e) -> p t h e", t=NCH, h=NH)[:, :, :, 64:65], 1.0))
    o = dbg_out("vsb", (128, NCH * NH * EV), BF16)
    if o is not None:
        sm.wait(sq, t_vones)
        sm.inc("st", sq.dma_start(out=o, in_=v_sb[:]), 16)
    phase_gate(2)
    free(*[f"ln{k}" for k in range(8)] + ["rowR1", "rowM1", "wq"])

    t_sz1 = sm.inc("dve", dve.memset(S1[:], 0.0))
    t_sz2 = sm.inc("dve", dve.memset(S2[:], 0.0))

    xbf = [[sb(f"xbf{p}_{k}", (128, PT), BF16) for k in range(8)] for p in range(2)]
    xh = [[sb(f"xh{p}_{k}", (128, PT), BF16) for k in range(8)] for p in range(2)]
    sqb = sb("sqb", (128, PT), BF16)
    prowM = sb("prowM", (1, PT), F32)
    prowR = sb("prowR", (1, PT), F32)
    prowS = sb("prowS", (1, PT), F32)
    prowMb = sb("prowMb", (1, PT), BF16)
    prowRb = sb("prowRb", (1, PT), BF16)
    knat = sb("knat", (128, 2 * PT), BF16)
    ktp = sb("ktp", (FEAT + 1, NH * PT), BF16)
    kf = sb("kf", (128, 2048), BF16)
    rep_sb = sb("rep_sb", (128, 4 * DE), BF16)
    vpre = sb("vpre", (128, 4 * NH * EV), BF16)
    pmb = sb("pmb", (128, 4), F32)

    t_pm = None
    t_pm_rd = None
    t_row_rd = None
    t_rep_act = None
    t_kf_last = [None]
    t_xbf = [None] * 8
    t_xh = [None] * 8
    t_xh_war = [[None] * 8, [None] * 8]
    t_knat = None
    t_ktp_w = [None, None]
    t_ktp_r = [None, None]
    t_kf = None
    t_vpre = None
    t_state = None
    t_u1p = t_u2p = None

    t_pre_ones = None
    for ib in range(2):
        for hhalf in range(2):
            sm.wait(sq, t_consts)
            t_pre_ones = sm.inc("st", sq.dma_start(
                out=KBPRE[ib, FEAT, 8 * hhalf:8 * (hhalf + 1), :]
                    .rearrange("h t -> (h t)").rearrange("(a b) -> a b", a=16),
                in_=cst["ONESB"][0:16, 0:256]), 16)

    t_bar3 = bar()
    t_rep_rd = None
    for j in range(NPT):
        js = slice(PT * j, PT * (j + 1))
        buf = j % 2
        xb, xhh = xbf[buf], xh[buf]
        t_xb_prev = t_xbf[buf * 0:8] if False else None
        for k in range(8):
            sm.wait(gp, t_xh_war[buf][k], *(t_bar3 if j == 0 else ()))
            t_xbf[k] = sm.inc("wt%d" % buf, gp.dma_start(
                out=xb[k][:], in_=ins["XPRE_T"][128 * k:128 * (k + 1), js]), 16)
        sm.wait(dve, t_consts, t_pm_rd, *(t_bar3 if j == 0 else ()))
        t_pm = sm.inc("dve", dve.tensor_copy(pmb[:], pmt[:, 4 * j:4 * (j + 1)]))
        pA = psum("ppA", (1, PT))
        pB = psum("ppB", (1, PT))
        t_sqb = None
        for k in range(8):
            sm.wait(pe, t_xbf[k], t_consts, t_pm, t_vown, t_row_rd)
            i1 = pe.matmul(pA[0:1, :], cst["ONECB"][:], xb[k][:],
                           start=(k == 0), stop=(k == 7))
            sm.wait(act, t_xbf[k], t_sqb)
            t2 = sm.inc("act", act.activation(sqb[:], xb[k][:], AF.Square))
            sm.wait(pe, t2)
            i3 = pe.matmul(pB[0:1, :], cst["ONECB"][:], sqb[:],
                           start=(k == 0), stop=(k == 7))
            t_sqb = sm.inc("pe", i3)
        t_mm = sm.inc("pe", i1)
        sm.wait(dve, t_mm, t_pm)
        t1 = sm.inc("dve", dve.tensor_scalar_mul(prowM[0:1, :], pA[0:1, :], 1.0 / D))
        sm.wait(dve, t_sqb, t1)
        t2 = sm.inc("dve", dve.tensor_scalar_mul(prowS[0:1, :], pB[0:1, :], 1.0 / D))
        sm.wait(dve, t1)
        t3 = sm.inc("dve", dve.tensor_tensor(prowR[0:1, :], prowM[0:1, :],
                                             prowM[0:1, :], ALU.mult))
        sm.wait(dve, t2, t3)
        t4 = sm.inc("dve", dve.tensor_tensor(prowS[0:1, :], prowS[0:1, :],
                                             prowR[0:1, :], ALU.subtract))
        sm.wait(dve, t4)
        t5 = sm.inc("dve", dve.tensor_scalar_add(prowS[0:1, :], prowS[0:1, :], LN_EPS))
        sm.wait(dve, t5)
        t6 = sm.inc("dve", dve.reciprocal(prowS[0:1, :], prowS[0:1, :]))
        sm.wait(act, t6)
        t7 = sm.inc("act", act.activation(prowR[0:1, :], prowS[0:1, :], AF.Sqrt))
        sm.wait(dve, t1, t7)
        t8 = sm.inc("dve", dve.tensor_copy(prowMb[0:1, :], prowM[0:1, :]))
        sm.wait(dve, t8)
        t9 = sm.inc("dve", dve.tensor_copy(prowRb[0:1, :], prowR[0:1, :]))
        free("ppA", "ppB")
        pM = psum("ppM", (128, PT))
        pR = psum("ppR", (128, PT))
        sm.wait(pe, t8)
        t_mb = sm.inc("pe", pe.matmul(pM[:], cst["ONESB"][0:1, 0:128], prowMb[0:1, :],
                                      start=True, stop=True))
        sm.wait(pe, t9)
        t_rb = sm.inc("pe", pe.matmul(pR[:], cst["ONESB"][0:1, 0:128], prowRb[0:1, :],
                                      start=True, stop=True))
        t_row_rd = t_rb
        t_xh_new = [None] * 8
        for k in range(8):
            sm.wait(dve, t_mb, t_rb, t_xbf[k], t_knat, t_xh_war[buf][k])
            ta = sm.inc("dve", dve.tensor_tensor(xb[k][:], xb[k][:], pM[:],
                                                 ALU.subtract))
            sm.wait(dve, ta)
            t_xh_new[k] = sm.inc("dve", dve.tensor_tensor(xhh[k][:], xb[k][:], pR[:],
                                                          ALU.mult))
        t_xh = t_xh_new
        free("ppM", "ppR")
        pK = psum("ppK", (128, PT))
        tk_ev = t_knat
        for grp in range(2):
            for k in range(8):
                sm.wait(pe, t_wk, t_xh[k], t_xh[7], tk_ev)
                ii = pe.matmul(pK[:], wk[:, 256 * k + 128 * grp:256 * k + 128 * (grp + 1)],
                               xhh[k][:], start=(k == 0), stop=(k == 7))
            t_mm = sm.inc("pe", ii)
            sm.wait(act, t_mm, tk_ev)
            tk_ev = sm.inc("act", act.copy(knat[:, PT * grp:PT * (grp + 1)], pK[:]))
        t_knat = tk_ev
        free("ppK")
        sm.wait(sq, t_knat, t_ktp_r[buf], t_pre_ones)
        tw = sm.inc("st", sq.dma_start(
            out=KBPRE[buf, 0:FEAT, :, :].rearrange("f (g h) t -> h f g t", g=2),
            in_=knat[:].rearrange("p (g t) -> p g t", g=2)), 16)
        t_ktp_w[buf] = tw
        sm.wait(sq, tw)
        tr = sm.inc("ld", sq.dma_start(
            out=ktp[:].rearrange("f (h t) -> f h t", h=NH),
            in_=KBPRE[buf, :, :, :]), 16)
        t_ktp_r[buf] = tr
        t_vev = t_vown
        for cc in range(4):
            tsl = slice(128 * cc, 128 * (cc + 1))
            pV = psum("ppV", (128, 512))
            for half in range(2):
                for k in range(8):
                    sm.wait(pe, t_wv, t_xh[k], t_vev, t_knat, t_state)
                    ii = pe.matmul(pV[:], xhh[k][:, tsl],
                                   wv[:, 1024 * k + 512 * half:1024 * k + 512 * (half + 1)],
                                   start=(k == 0), stop=(k == 7))
                t_mm = sm.inc("pe", ii)
                sm.wait(act, t_mm, t_vev)
                t_vev = sm.inc("act", act.copy(
                    vpre[:].rearrange("p (c h e) -> p c h e", c=4, h=NH)
                        [:, cc, 8 * half:8 * (half + 1), 0:64],
                    pV[:].rearrange("p (h e) -> p h e", h=8)))
            free("ppV")
            sm.wait(dve, t_vev, t_pm)
            t_vm = sm.inc("dve", dve.tensor_scalar_mul(
                vpre[:].rearrange("p (c h e) -> p c h e", c=4, h=NH)[:, cc, :, 64:65]
                    .rearrange("p h o -> p (h o)"),
                cst["ONESB"][:, 0:16], pmb[:, cc:cc + 1]))
        t_vpre = t_vm
        t_pm_rd = t_vm
        PSS1p = psum("PSS1p", (128, 1024))
        PSS2p = psum("PSS2p", (128, 512))
        pR1 = [psum("ppR1a", (128, 1024)), psum("ppR1b", (128, 1024))]
        t_kf_h = [None] * (NH + 1)
        kf2 = kf

        def emit_reps(h):
            p = h % 2
            t_sub_rd = t_reprd[p]
            for sub in range(2):
                pR = pR1[p]
                for tt2 in range(2):
                    cc = 2 * sub + tt2
                    lhs = ktp[:, PT * h + 128 * cc:PT * h + 128 * (cc + 1)]
                    sm.wait(pe, tr, t_consts, t_sub_rd)
                    i1 = pe.matmul(pR[:, 512 * tt2:512 * tt2 + 2 * DE], lhs,
                                   cst["R12KT"][:], start=True, stop=True)
                t_rep = sm.inc("pe", i1)
                sm.wait(act, t_rep, t_repact[0])
                te = sm.inc("act", act.copy(
                    rep_sb[:].rearrange("p (q a f) -> p q a f", q=2, a=2)[:, p],
                    pR[:].rearrange("p (a f) -> p a f", a=2)[:, :, 0:DE]))
                t_repact[0] = te
                sm.wait(dve, te, t_kfd[0])
                t_kfd[0] = sm.inc("dve", dve.tensor_tensor(
                    kf2[:].rearrange("p (q c f) -> p q c f", q=2, c=4)
                        [:, p, 2 * sub:2 * sub + 2, 0:DE],
                    pR[:].rearrange("p (a f) -> p a f", a=2)[:, :, DE:2 * DE],
                    rep_sb[:].rearrange("p (q a f) -> p q a f", q=2, a=2)[:, p],
                    ALU.mult))
                t_sub_rd = t_kfd[0]
            t_kf_h[h] = t_kfd[0]
            t_reprd[p] = t_kfd[0]

        def emit_states(h):
            p = h % 2
            hg = h % 8
            for cc in range(4):
                sm.wait(pe, t_kf_h[h], t_vpre, t_u2p)
                i1 = pe.matmul(PSS1p[:, 128 * hg:128 * hg + EV],
                               kf2[:, 1024 * p + 256 * cc:1024 * p + 256 * cc + 128],
                               vpre[:, 1040 * cc + EV * h:1040 * cc + EV * (h + 1)],
                               start=(cc == 0), stop=(cc == 3),
                               skip_group_check=True)
                i2 = pe.matmul(PSS2p[32 * (h % 4):32 * (h % 4) + 32,
                                     EV * (h // 4):EV * (h // 4) + EV],
                               kf2[:, 1024 * p + 256 * cc + 128:1024 * p + 256 * cc + DE],
                               vpre[:, 1040 * cc + EV * h:1040 * cc + EV * (h + 1)],
                               start=(cc == 0), stop=(cc == 3),
                               tile_position=(0, 32 * (h % 4)),
                               skip_group_check=True)
            return sm.inc("pe", i2)

        t_reprd = [t_rep_rd, t_rep_rd]
        t_repact = [t_rep_act]
        t_kfd = [t_kf_last[0]]
        for half in range(2):
            for hh in range(9):
                h = 8 * half + hh
                if hh < 8 and h < NH:
                    emit_reps(h)
                if hh > 0:
                    t_state = emit_states(8 * half + hh - 1)
            sm.wait(dve, t_state, t_sz1, t_sz2, t_u1p, t_u2p)
            t_u1p = sm.inc("dve", dve.tensor_tensor(
                S1[:].rearrange("p (h e) -> p h e", h=NH)[:, 8 * half:8 * half + 8],
                S1[:].rearrange("p (h e) -> p h e", h=NH)[:, 8 * half:8 * half + 8],
                PSS1p[:].rearrange("p (h e) -> p h e", h=8)[:, :, 0:EV], ALU.add))
            if half == 1:
                sm.wait(dve, t_u1p)
                t_u2p = sm.inc("dve", dve.tensor_tensor(S2[:], S2[:],
                                                        PSS2p[:, 0:4 * EV], ALU.add))
        t_rep_rd = t_reprd[1]
        t_rep_act = t_repact[0]
        t_kf_last[0] = t_kfd[0]
        free("ppR1a", "ppR1b", "PSS1p", "PSS2p")
        for k in range(8):
            t_xh_war[buf][k] = t_state

    free(*([f"xbf{p}_{k}" for p in range(2) for k in range(8)] +
           [f"xh{p}_{k}" for p in range(2) for k in range(8)] +
           ["sqb", "prowM", "prowR", "prowS", "prowMb", "prowRb", "knat", "ktp",
            "kf", "rep_sb", "vpre", "pmb", "wv", "wk"]))
    phase_gate(3)

    S1b = sb("S1b", (128, NH * EV), BF16)
    S2b = sb("S2b", (128, 4 * EV), BF16)
    t_s1 = t_u1p
    t_s2 = t_u2p
    sm.wait(act, t_u1p, t_u2p, t_sz1, t_sz2)
    t_s1b = sm.inc("act", act.copy(S1b[:], S1[:]))
    t_s2b = sm.inc("act", act.copy(S2b[:], S2[:]))
    o = dbg_out("s1", (128, NH * EV), F32)
    if o is not None:
        sm.wait(sq, t_s1)
        sm.inc("st", sq.dma_start(out=o, in_=S1[:]), 16)

    y_sb = [sb(f"y{k}", (128, NT), BF16) for k in range(8)]
    qt = sb("qt", (FEAT + 1, NH * CH), BF16)
    kt = sb("kt", (FEAT + 1, NH * CH), BF16)
    qf1 = sb("qf1", (128, NH * CH), BF16)
    qf2 = sb("qf2", (128, 4 * CH), BF16)
    kfo = sb("kfo", (128, NH * DE), BF16)
    repo = sb("repo", (128, 4 * DE), BF16)
    qrep = sb("qrep", (128, 512), BF16)
    sqf = sb("sqf", (128, NH * CH), BF16)
    at = sb("at", (128, NH * CH), BF16)
    nums = sb("nums", (EV, NH * CH), BF16)
    dens = sb("dens", (128, NH), F32)
    recs = sb("recs", (128, NH), F32)
    recT = sb("recT", (16, CH), BF16)

    t_qt = t_kt = None
    t_kfo = t_qf = t_at = t_num_ev = t_y = t_recT = None
    t_u1 = t_u2 = None
    t_pss = t_dmm = t_bc = t_tr = None
    t_bar4 = bar()

    for c in range(NCH):
        tsl = slice(CH * c, CH * (c + 1))
        sm.wait(sq, t_kb_all, t_qt, t_dmm, *(t_bar4 if c == 0 else ()))
        t_qt = sm.inc("ld", sq.dma_start(
            out=qt[:].rearrange("f (h t) -> f h t", h=NH),
            in_=KBOWN[0, :, :, tsl]), 16)
        sm.wait(sq, t_kt, t_dmm)
        t_kt = sm.inc("ld", sq.dma_start(
            out=kt[:].rearrange("f (h t) -> f h t", h=NH),
            in_=KBOWN[1, :, :, tsl]), 16)

        pRc = psum("scRc", (128, 2048))
        t_kfo_new = t_kfo
        for rnd in range(4):
            for hh in range(4):
                h = 4 * rnd + hh
                lhs = kt[:, CH * h:CH * (h + 1)]
                sm.wait(pe, t_kt, t_consts, t_kfo_new, t_num_ev, t_y)
                i1 = pe.matmul(pRc[:, 512 * hh:512 * hh + 2 * DE], lhs,
                               cst["R12KT"][:], start=True, stop=True)
            t_rep = sm.inc("pe", i1)
            sm.wait(act, t_rep)
            te = sm.inc("act", act.copy(
                repo[:].rearrange("p (a f) -> p a f", a=4),
                pRc[:].rearrange("p (a f) -> p a f", a=4)[:, :, 0:DE]))
            sm.wait(dve, te, t_pss)
            t_kfo_new = sm.inc("dve", dve.tensor_tensor(
                kfo[:].rearrange("p (h f) -> p h f", h=NH)[:, 4 * rnd:4 * rnd + 4, :],
                pRc[:].rearrange("p (a f) -> p a f", a=4)[:, :, DE:2 * DE],
                repo[:].rearrange("p (a f) -> p a f", a=4), ALU.mult))
        t_kfo = t_kfo_new
        free("scRc")

        PSS1 = psum("PSS1", (128, 2048))
        PSS2 = psum("PSS2", (128, 512))
        for h in range(NH):
            sm.wait(pe, t_kfo, t_vones, t_u1, t_u2, t_num_ev, t_y)
            i1 = pe.matmul(PSS1[:, 128 * h:128 * h + EV],
                           kfo[:, DE * h:DE * h + 128],
                           v_sb[:, 1040 * c + EV * h:1040 * c + EV * (h + 1)],
                           start=True, stop=True)
            i2 = pe.matmul(PSS2[32 * (h % 4):32 * (h % 4) + 32,
                                EV * (h // 4):EV * (h // 4) + EV],
                           kfo[:, DE * h + 128:DE * h + DE],
                           v_sb[:, 1040 * c + EV * h:1040 * c + EV * (h + 1)],
                           start=True, stop=True, tile_position=(0, 32 * (h % 4)))
        t_pss = sm.inc("pe", i2)

        pQ1 = psum("scQ1", (128, 512))
        pQ2 = psum("scQ2", (128, 512))
        t_qf_new = t_qf
        for g in range(4):
            rhs = qt[:, 512 * g:512 * (g + 1)]
            sm.wait(pe, t_qt, t_consts, t_qf_new, t_num_ev, t_y)
            i1 = pe.matmul(pQ1[:], cst["R1QD1"][:], rhs, start=True, stop=True)
            i2 = pe.matmul(pQ2[:], cst["R2QD1"][:], rhs, start=True, stop=True)
            t_rep = sm.inc("pe", i2)
            sm.wait(act, t_rep)
            te = sm.inc("act", act.copy(qrep[:], pQ1[:]))
            sm.wait(dve, te)
            t_qf_new = sm.inc("dve", dve.tensor_tensor(
                qf1[:, 512 * g:512 * (g + 1)], pQ2[:], qrep[:], ALU.mult))
        for h in range(NH):
            rhs = qt[:, CH * h:CH * (h + 1)]
            jj = h % 4
            sm.wait(pe, t_qt, t_consts, t_qf_new)
            i1 = pe.matmul(pQ1[32 * jj:32 * jj + 32, 128 * (h // 4):128 * (h // 4) + 128],
                           cst["R1QD2"][:], rhs, start=True, stop=True,
                           tile_position=(0, 32 * jj))
            i2 = pe.matmul(pQ2[32 * jj:32 * jj + 32, 128 * (h // 4):128 * (h // 4) + 128],
                           cst["R2QD2"][:], rhs, start=True, stop=True,
                           tile_position=(0, 32 * jj))
        t_rep = sm.inc("pe", i2)
        sm.wait(act, t_rep)
        te = sm.inc("act", act.copy(qrep[:], pQ1[:]))
        sm.wait(dve, te)
        t_qf = sm.inc("dve", dve.tensor_tensor(qf2[:], pQ2[:], qrep[:], ALU.mult))
        free("scQ1", "scQ2")

        sm.wait(dve, t_pss, t_s1, t_s2, t_s1b, t_s2b)
        t_u1 = sm.inc("dve", dve.tensor_tensor(
            S1[:], S1[:],
            PSS1[:].rearrange("p (h e) -> p h e", h=NH)[:, :, 0:EV], ALU.add))
        sm.wait(dve, t_u1)
        t_u2 = sm.inc("dve", dve.tensor_tensor(S2[:], S2[:], PSS2[:, 0:4 * EV], ALU.add))
        free("PSS1", "PSS2")

        PS_S = psum("PS_S", (128, 2048))
        for h in range(NH):
            sm.wait(pe, t_qt, t_kt, t_at, t_u2)
            ii = pe.matmul(PS_S[:, CH * h:CH * (h + 1)],
                           kt[0:FEAT, CH * h:CH * (h + 1)],
                           qt[0:FEAT, CH * h:CH * (h + 1)], start=True, stop=True)
        t_smm = sm.inc("pe", ii)
        sm.wait(act, t_smm, t_at)
        t_sq = sm.inc("act", act.activation(sqf[:], PS_S[:], AF.Square,
                                            bias=1.0, scale=0.25))
        sm.wait(dve, t_sq, t_consts, t_dmm)
        t_at = sm.inc("dve", dve.scalar_tensor_tensor(
            at[:], sqf[:], 1.0, cst["MASKH"][:], ALU.add, ALU.mult))
        free("PS_S")

        PS_N = psum("PS_N", (EV, 2048))
        for h in range(NH):
            hs = slice(CH * h, CH * (h + 1))
            jj = h % 4
            sm.wait(pe, t_at, t_vones, t_s1b, t_s2b, t_qf, t_num_ev)
            i1 = pe.matmul(PS_N[:, hs],
                           v_sb[:, 1040 * c + EV * h:1040 * c + EV * (h + 1)],
                           at[:, hs], start=True, stop=False)
            i2 = pe.matmul(PS_N[:, hs], S1b[:, EV * h:EV * (h + 1)],
                           qf1[:, hs], start=False, stop=False)
            i3 = pe.matmul(PS_N[:, hs],
                           S2b[32 * jj:32 * jj + 32, EV * (h // 4):EV * (h // 4) + EV],
                           qf2[32 * jj:32 * jj + 32, 128 * (h // 4):128 * (h // 4) + 128],
                           start=False, stop=True, tile_position=(32 * jj, 0))
        t_num = sm.inc("pe", i3)

        PS_D = psum("PS_D", (128, 16))
        for h in range(NH):
            hs = slice(CH * h, CH * (h + 1))
            jj = h % 4
            sm.wait(pe, t_at, t_s1b, t_s2b, t_qf, t_recT)
            i1 = pe.matmul(PS_D[:, h:h + 1], at[:, hs], cst["ONECB"][:],
                           start=True, stop=False)
            i2 = pe.matmul(PS_D[:, h:h + 1], qf1[:, hs],
                           S1b[:, EV * h + 64:EV * h + 65], start=False, stop=False)
            i3 = pe.matmul(PS_D[:, h:h + 1],
                           qf2[32 * jj:32 * jj + 32, 128 * (h // 4):128 * (h // 4) + 128],
                           S2b[32 * jj:32 * jj + 32,
                               EV * (h // 4) + 64:EV * (h // 4) + 65],
                           start=False, stop=True, tile_position=(32 * jj, 0))
        t_dmm = sm.inc("pe", i3)
        sm.wait(dve, t_dmm, t_tr)
        td1 = sm.inc("dve", dve.tensor_scalar_add(dens[:], PS_D[:], ATT_EPS))
        sm.wait(dve, td1)
        td2 = sm.inc("dve", dve.reciprocal(recs[:], dens[:]))
        free("PS_D")
        PS_R = psum("PS_R", (16, 128))
        sm.wait(pe, td2, t_recT)
        t_tr = sm.inc("pe", pe.transpose(PS_R[:], recs[:], cst["IDENTF"][:]))
        sm.wait(act, t_tr, t_recT, t_bc)
        t_recT = sm.inc("act", act.copy(recT[:], PS_R[:]))
        free("PS_R")
        PS_B = psum("PS_B", (EV, 2048))
        for h in range(NH):
            sm.wait(pe, t_recT, t_consts, t_y)
            ii = pe.matmul(PS_B[:, CH * h:CH * (h + 1)],
                           cst["ESEL"][:, EV * h:EV * (h + 1)], recT[:],
                           start=True, stop=True)
        t_bc = sm.inc("pe", ii)
        sm.wait(act, t_num, t_num_ev, t_y)
        t_num_ev = sm.inc("act", act.copy(nums[:], PS_N[:]))
        t_y_new = t_y
        for h in range(NH):
            hs = slice(CH * h, CH * (h + 1))
            sm.wait(dve, t_num_ev, t_bc)
            t_y_new = sm.inc("dve", dve.tensor_tensor(
                y_sb[h // 2][64 * (h % 2):64 * (h % 2) + 64, tsl],
                nums[0:64, hs], PS_B[0:64, hs], ALU.mult))
        t_y = t_y_new
        free("PS_B", "PS_N")

        sm.wait(act, t_u1, t_u2, t_num, t_dmm)
        t_s1b = sm.inc("act", act.copy(S1b[:], S1[:]))
        t_s2b = sm.inc("act", act.copy(S2b[:], S2[:]))

    o = dbg_out("ysb", (128, 8 * NT), BF16)
    if o is not None:
        sm.wait(sq, t_y)
        for k in range(8):
            sm.inc("st", sq.dma_start(out=o[:, NT * k:NT * (k + 1)], in_=y_sb[k][:]), 16)
    phase_gate(4)

    free("recT", "recs", "dens", "nums", "at", "sqf", "qrep", "repo",
         "kfo", "qf2", "qf1", "kt", "qt")

    wo = sb("wo", (128, 8 * 1024), BF16)
    sm.wait(gp, *bar())
    t_wo = wload(wo, ins["WoT"])
    t_r1 = [None] * 8
    t_pwo_rd = None
    for m in range(8):
        for ct in range(NT // 512):
            cs = slice(512 * ct, 512 * (ct + 1))
            pw = psum("pwo", (128, 512))
            for k in range(8):
                sm.wait(pe, t_wo, t_y, t_pwo_rd)
                ii = pe.matmul(pw[:], wo[:, 1024 * k + 128 * m:1024 * k + 128 * (m + 1)],
                               y_sb[k][:, cs], start=(k == 0), stop=(k == 7))
            t_mm = sm.inc("pe", ii)
            sm.wait(dve, t_mm, t_r1[m])
            t_r1[m] = sm.inc("dve", dve.tensor_tensor(r[m][:, cs], r[m][:, cs],
                                                      pw[:], ALU.add))
            t_pwo_rd = t_r1[m]
            free("pwo")
    free(*(["wo"] + [f"y{k}" for k in range(8)] +
           ["S2b", "S1b", "S2", "S1", "v_sb"]))
    o = dbg_out("r1", (128, 8 * NT), F32)
    if o is not None:
        sm.wait(sq, *t_r1)
        for k in range(8):
            sm.inc("st", sq.dma_start(out=o[:, NT * k:NT * (k + 1)], in_=r[k][:]), 16)

    phase_gate(5)
    Bcol = sb("Bcol", (32, 1), F32)
    masksb = sb("masksb", (128, NCH * E_EXP), BF16)
    scrowb = sb("scrowb", (1, NT), BF16)
    rks = sb("rks", (128, E_EXP), F32)
    brow = sb("brow", (1, E_EXP), F32)
    browb = sb("browb", (1, E_EXP), BF16)
    slotf = sb("slotf", (128, NCH), F32)
    sloti = sb("sloti", (128, NCH), I32)
    tmp32 = sb("tmp32", (128, E_EXP), F32)
    tmpc = sb("tmpc", (128, 1), F32)
    zero_sb = sb("zero_sb", (128, D), BF16)
    xst = sb("xst", (128, D), BF16)
    rtt = sb("rtt", (128, D), BF16)
    rowM = sb("rowM", (1, NT), F32)
    rowR = sb("rowR", (1, NT), F32)
    ln_bf = [sb(f"ln{k}", (128, NT), BF16) for k in range(8)]

    t_ln2 = ln_exact("l2", [t_r1[7]] * 8, rowM, rowR)
    o = dbg_out("ln2", (128, 8 * NT), BF16)
    if o is not None:
        sm.wait(sq, t_ln2)
        for k in range(8):
            sm.inc("st", sq.dma_start(out=o[:, NT * k:NT * (k + 1)], in_=ln_bf[k][:]), 16)

    phase_gate(6)
    negC = sb("negC", (32, NT), F32)
    mnrow = sb("mnrow", (16, CH), F32)
    ctok = sb("ctok", (128, NCH * E_EXP), F32)
    mneg = sb("mneg", (128, NCH), F32)
    scr0 = sb("scr0", (1, NT), F32)
    scrV = sb("scrV", (1, NT), F32)

    pBc = psum("pBc", (32, 1))
    for k in range(8):
        sm.wait(pe, t_gate, t_consts, t_r1[7], t_ln2)
        ii = pe.matmul(pBc[:], gate[:, 32 * k:32 * (k + 1)], cst["ONECF"][:, 0:1],
                       start=(k == 0), stop=(k == 7))
    t_bm = sm.inc("pe", ii)
    sm.wait(dve, t_bm)
    t_bcol = sm.inc("dve", dve.tensor_copy(Bcol[:], pBc[:]))
    free("pBc")

    t_negC = None
    for ct in range(NT // 512):
        cs = slice(512 * ct, 512 * (ct + 1))
        pAg = psum("pAg", (32, 512))
        for k in range(8):
            sm.wait(pe, t_r1[k], t_gate, t_ln2, t_negC, t_bcol)
            ii = pe.matmul(pAg[:], gate[:, 32 * k:32 * (k + 1)], r[k][:, cs],
                           start=(k == 0), stop=(k == 7))
        t_ag = sm.inc("pe", ii)
        pMb = psum("pMb", (32, 512))
        sm.wait(pe, t_negC)
        t_mb = sm.inc("pe", pe.matmul(pMb[:], cst["ONESF"][0:1, 0:32], rowM[0:1, cs],
                                      start=True, stop=True))
        sm.wait(act, t_mb, t_negC)
        t_me = sm.inc("act", act.copy(negC[:, cs], pMb[:]))
        sm.wait(dve, t_me, t_ag, t_bcol)
        t_negC = sm.inc("dve", dve.scalar_tensor_tensor(
            negC[:, cs], negC[:, cs], Bcol[:, 0:1], pAg[:], ALU.mult, ALU.subtract))
        free("pAg", "pMb")

    pT = psum("pT", (128, 512))
    t_ct = None
    for tt in range(NCH):
        sm.wait(pe, t_negC, t_ct)
        ii = pe.transpose(pT[:, 32 * tt:32 * (tt + 1)],
                          negC[:, CH * tt:CH * (tt + 1)], cst["IDENTF"][0:32, 0:32])
        t_ct = sm.inc("pe", ii)
    sm.wait(dve, t_ct)
    t_ctok = sm.inc("dve", dve.tensor_copy(ctok[:], pT[:]))
    free("pT")
    sm.wait(dve, t_ctok)
    t_mn = sm.inc("dve", dve.tensor_reduce(
        mneg[:].rearrange("p (t o) -> p t o", o=1),
        ctok[:].rearrange("p (t e) -> p t e", e=E_EXP), axis=AX.X, op=ALU.min))
    t_mk = None
    for tt in range(NCH):
        sm.wait(dve, t_mn)
        t_mk = sm.inc("dve", dve.tensor_single_scalar(
            masksb[:, 32 * tt:32 * (tt + 1)], ctok[:, 32 * tt:32 * (tt + 1)],
            mneg[:, tt:tt + 1], ALU.is_equal))
    t_mkb = t_mk

    PS_MT = psum("PS_MT", (16, 128))
    sm.wait(pe, t_mn)
    t_mt = sm.inc("pe", pe.transpose(PS_MT[:], mneg[:], cst["IDENTF"][:]))
    sm.wait(act, t_mt)
    t_mr = sm.inc("act", act.copy(mnrow[:], PS_MT[:]))
    free("PS_MT")
    sm.wait(sq, t_mr)
    t_rw = sm.inc("st", sq.dma_start(
        out=ROWB.rearrange("(a b) -> a b", a=16), in_=mnrow[:]), 16)
    sm.wait(sq, t_rw)
    t_rr = sm.inc("ld", sq.dma_start(
        out=scr0[0:1, :], in_=ROWB.rearrange("(o t) -> o t", o=1)), 16)
    sm.wait(dve, t_rr, t_ln2)
    t_val = sm.inc("dve", dve.scalar_tensor_tensor(
        scrV[0:1, :], scr0[0:1, :], -1.0, rowR[0:1, :], ALU.mult, ALU.mult))
    sm.wait(act, t_val)
    t_sc = sm.inc("act", act.activation(scr0[0:1, :], scrV[0:1, :], AF.Sigmoid))
    sm.wait(dve, t_sc)
    t_scb = sm.inc("dve", dve.tensor_copy(scrowb[:], scr0[0:1, :]))
    o = dbg_out("scores", (1, NT), F32)
    if o is not None:
        sm.wait(sq, t_sc)
        sm.inc("st", sq.dma_start(out=o, in_=scr0[0:1, :]), 16)
    o = dbg_out("masks", (128, NCH * E_EXP), BF16)
    if o is not None:
        sm.wait(sq, t_mk)
        sm.inc("st", sq.dma_start(out=o, in_=masksb[:]), 16)
    sm.wait(sq, t_ctok, t_rw, t_scb, t_mk)
    free("scrV", "scr0", "mneg", "ctok", "mnrow", "negC")

    phase_gate(7)
    xs = [sb(f"xs{k}", (128, NT), BF16) for k in range(8)]
    t_xs = [None] * 8
    for ct in range(NT // 512):
        cs = slice(512 * ct, 512 * (ct + 1))
        pSc = psum("pSc", (128, 512))
        sm.wait(pe, t_scb, t_xs[7])
        t_sb_ = sm.inc("pe", pe.matmul(pSc[:], cst["ONESB"][0:1, 0:128], scrowb[0:1, cs],
                                       start=True, stop=True))
        for k in range(8):
            sm.wait(dve, t_sb_, t_ln2, ("st", 1))
            t_xs[k] = sm.inc("dve", dve.tensor_tensor(xs[k][:, cs], ln_bf[k][:, cs],
                                                      pSc[:], ALU.mult))
        free("pSc")

    sgws = [sb("sgws0", (128, 8 * 512), BF16)]
    suws = [sb("suws0", (128, 8 * 512), BF16)]
    sdws = [sb("sdws0", (128, 4 * 1024), BF16)]
    silu_sb = sb("silu_sb", (128, 256), BF16)
    tsh = sb("tsh", (128, 256), BF16)
    t_sw_used = [None]
    t_r2 = list(t_r1)

    sm.wait(gp, *bar())
    for isl in range(4):
        b = 0
        sm.wait(gp, t_sw_used[b])
        tg1 = sm.inc("gld", gp.dma_start(
            out=sgws[b][:].rearrange("p (k c) -> p k c", k=8),
            in_=ins["SGW"][:, 512 * isl:512 * (isl + 1)].rearrange("(k p) c -> p k c", p=128)), 16)
        tg2 = sm.inc("gld", gp.dma_start(
            out=suws[b][:].rearrange("p (k c) -> p k c", k=8),
            in_=ins["SUW"][:, 512 * isl:512 * (isl + 1)].rearrange("(k p) c -> p k c", p=128)), 16)
        tg3 = sm.inc("gld", gp.dma_start(
            out=sdws[b][:].rearrange("p (k c) -> p k c", k=4),
            in_=ins["SDW"][512 * isl:512 * (isl + 1), :].rearrange("(k p) c -> p k c", p=128)), 16)
        t_si = t_th = None
        for ct in range(NT // 256):
            cs = slice(256 * ct, 256 * (ct + 1))
            pDn = psum("pDn", (128, 2048))
            for it in range(4):
                pG = psum("pG", (128, 256))
                pU = psum("pU", (128, 256))
                for k in range(8):
                    sm.wait(pe, tg1, tg2, t_ln2, t_si, t_xs[7])
                    i1 = pe.matmul(pG[:], sgws[b][:, 512 * k + 128 * it:512 * k + 128 * (it + 1)],
                                   ln_bf[k][:, cs], start=(k == 0), stop=(k == 7))
                tgm = sm.inc("pe", i1)
                for k in range(8):
                    sm.wait(pe, tg2, t_th)
                    i2 = pe.matmul(pU[:], suws[b][:, 512 * k + 128 * it:512 * k + 128 * (it + 1)],
                                   ln_bf[k][:, cs], start=(k == 0), stop=(k == 7))
                tum = sm.inc("pe", i2)
                sm.wait(act, tgm, t_th)
                t_si = sm.inc("act", act.activation(silu_sb[:], pG[:], AF.Silu))
                sm.wait(dve, t_si, tum, t_th)
                t_th = sm.inc("dve", dve.tensor_tensor(tsh[:], silu_sb[:], pU[:],
                                                       ALU.mult))
                free("pG", "pU")
                for m in range(8):
                    sm.wait(pe, t_th, tg3, t_r2[7])
                    i3 = pe.matmul(pDn[:, 256 * m:256 * (m + 1)],
                                   sdws[b][:, 1024 * it + 128 * m:1024 * it + 128 * (m + 1)],
                                   tsh[:], start=(it == 0 and m % 2 == 0), stop=(it == 3),
                                   skip_group_check=True)
            t_dn = sm.inc("pe", i3)
            for m in range(8):
                sm.wait(dve, t_dn, t_r2[m])
                t_r2[m] = sm.inc("dve", dve.tensor_tensor(
                    r[m][:, cs], r[m][:, cs], pDn[:, 256 * m:256 * (m + 1)], ALU.add))
            free("pDn")
        t_sw_used[b] = t_r2[7]
    t_shared_done = t_dn

    free("tsh", "silu_sb", "sdws0", "suws0", "sgws0")

    phase_gate(8)
    sm.wait(dve, t_consts)
    t_br = sm.inc("dve", dve.memset(brow[:], 0.0))
    sm.wait(dve, t_br)
    t_brb = sm.inc("dve", dve.tensor_copy(browb[:], brow[:]))
    t_slot = None
    for tt in range(NCH):
        ms = slice(32 * tt, 32 * (tt + 1))
        pRk = psum("pRk", (128, 32))
        pBb = psum("pBb", (128, 32))
        pTt = psum("pTt", (1, 32))
        sm.wait(pe, t_mkb, t_consts, t_brb, t_r2[7], t_shared_done)
        i1 = pe.matmul(pRk[:], cst["UEXCL"][:], masksb[:, ms], start=True, stop=True)
        t_rk = sm.inc("pe", i1)
        i2 = pe.matmul(pBb[:], cst["ONESB"][0:1, 0:128],
                       browb[:], start=True, stop=True)
        t_bb = sm.inc("pe", i2)
        i3 = pe.matmul(pTt[0:1, :], cst["ONECB"][:], masksb[:, ms], start=True, stop=True)
        t_tt = sm.inc("pe", i3)
        sm.wait(dve, t_rk, t_bb, t_slot)
        t1 = sm.inc("dve", dve.tensor_copy(rks[:], pRk[:]))
        sm.wait(dve, t1)
        t2 = sm.inc("dve", dve.tensor_tensor(rks[:], rks[:], pBb[:], ALU.add))
        sm.wait(dve, t2, t_tt)
        t3 = sm.inc("dve", dve.tensor_tensor(brow[:], brow[:], pTt[0:1, :], ALU.add))
        sm.wait(dve, t3)
        t_brb = sm.inc("dve", dve.tensor_copy(browb[:], brow[:]))
        free("pRk", "pBb", "pTt")
        sm.wait(dve, t2, t_mk)
        t4 = sm.inc("dve", dve.tensor_tensor(tmp32[:], rks[:], cst["EIDX"][:], ALU.add))
        sm.wait(dve, t4)
        t5 = sm.inc("dve", dve.tensor_tensor(tmp32[:], tmp32[:], masksb[:, ms], ALU.mult))
        sm.wait(dve, t5)
        t6 = sm.inc("dve", dve.tensor_reduce(
            slotf[:, tt:tt + 1], tmp32[:], axis=AX.X, op=ALU.add))
        sm.wait(dve, t6)
        t7 = sm.inc("dve", dve.tensor_tensor(tmp32[:], rks[:], masksb[:, ms], ALU.mult))
        sm.wait(dve, t7)
        t8 = sm.inc("dve", dve.tensor_reduce(tmpc[:], tmp32[:], axis=AX.X, op=ALU.add))
        sm.wait(dve, t8)
        t9 = sm.inc("dve", dve.tensor_single_scalar(tmpc[:], tmpc[:], float(CAP - 0.5),
                                                    ALU.is_gt))
        sm.wait(dve, t6, t9)
        ta = sm.inc("dve", dve.tensor_scalar(
            tmp32[:, 0:1], slotf[:, tt:tt + 1], -1.0, float(TRASH), ALU.mult, ALU.add))
        sm.wait(dve, ta)
        tb = sm.inc("dve", dve.tensor_tensor(tmp32[:, 0:1], tmp32[:, 0:1], tmpc[:],
                                             ALU.mult))
        sm.wait(dve, tb)
        tcp = sm.inc("dve", dve.tensor_tensor(slotf[:, tt:tt + 1], slotf[:, tt:tt + 1],
                                              tmp32[:, 0:1], ALU.add))
        sm.wait(dve, tcp)
        t_slot = sm.inc("dve", dve.tensor_copy(sloti[:, tt:tt + 1], slotf[:, tt:tt + 1]))
    o = dbg_out("sloti", (128, NCH), I32)
    if o is not None:
        sm.wait(sq, t_slot)
        sm.inc("st", sq.dma_start(out=o, in_=sloti[:]), 16)

    sm.wait(dve, t_slot)
    t_z = sm.inc("dve", dve.memset(zero_sb[:], 0.0))
    t_zs = None
    sm.wait(sq, t_z)
    for blk in range(NSLOT // 128):
        t_zs = sm.inc("st", sq.dma_start(out=XSG[128 * blk:128 * (blk + 1), :],
                                         in_=zero_sb[:]), 16)
    t_zs = sm.inc("st", sq.dma_start(out=XSG[NSLOT - 16:NSLOT, :], in_=zero_sb[0:16, :]), 16)
    t_zr = sm.inc("st", sq.dma_start(out=RTD[TRASH:TRASH + 16, :], in_=zero_sb[0:16, :]), 16)

    t_xst = None
    t_scat = None
    for tt in range(NCH):
        tsl = slice(CH * tt, CH * (tt + 1))
        pX = psum("pX", (128, 128), BF16)
        t_tp = None
        for k in range(8):
            sm.wait(pe, t_xs[k], t_consts, t_xst, t_slot)
            t_tp = sm.inc("pe", pe.transpose(pX[:], xs[k][:, tsl], cst["IDENTB"][:]))
            sm.wait(act, t_tp, t_scat)
            t_xst = sm.inc("act", act.copy(xst[:, 128 * k:128 * (k + 1)], pX[:]))
        free("pX")
        sm.wait(gp, t_xst, t_zs, t_slot)
        t_scat = sm.inc("sct", gp.indirect_dma_start(
            out=XSG, out_offset=IndirectOffsetOnAxis(ap=sloti[:, tt:tt + 1], axis=0),
            in_=xst[:], in_offset=None), 16)
    t_scat_all = t_scat

    free(*[f"xs{k}" for k in range(8)])
    free(*[f"ln{k}" for k in range(8)] + ["rowR", "rowM"])

    xg = [sb(f"xg{k}", (128, E_EXP * CAP), BF16) for k in range(8)]
    t_xg = [None] * 8
    for k in range(8):
        sm.wait(sq, t_scat_all, t_r2[7], t_shared_done)
        t_xg[k] = sm.inc("ld", sq.dma_start(
            out=xg[k][:], in_=XSG[0:E_EXP * CAP, 128 * k:128 * (k + 1)],
            transpose=True), 16)

    NSL = 8
    rgws = [sb(f"rgws{b}", (128, 8 * 256), BF16) for b in range(2)]
    ruws = [sb(f"ruws{b}", (128, 8 * 256), BF16) for b in range(2)]
    rdws = [sb(f"rdws{b}", (128, 2 * 1024), BF16) for b in range(2)]
    te_sb = sb("te_sb", (128, 2 * CAP), BF16)
    sil2 = sb("sil2", (128, 2 * CAP), BF16)
    rts = sb("rts", (128, 1024), BF16)

    t_buf_used = [None, None]
    t_rts = None
    t_rt_store = None
    t_si2 = t_te = t_dn_pe = None
    nsl_ctr = 0

    for e in range(E_EXP):
        PRo = psum("PRo", (128, 1024))
        for isl in range(NSL):
            b = nsl_ctr % 2
            nsl_ctr += 1
            wsem = "wt%d" % b
            sm.wait(gp, t_buf_used[b], t_r2[7], t_shared_done, ("sct", 1))
            tg1 = sm.inc(wsem, gp.dma_start(
                out=rgws[b][:].rearrange("p (k c) -> p k c", k=8),
                in_=ins["RGW"][e, :, 256 * isl:256 * (isl + 1)].rearrange("(k p) c -> p k c", p=128)), 16)
            tg2 = sm.inc(wsem, gp.dma_start(
                out=ruws[b][:].rearrange("p (k c) -> p k c", k=8),
                in_=ins["RUW"][e, :, 256 * isl:256 * (isl + 1)].rearrange("(k p) c -> p k c", p=128)), 16)
            tg3 = sm.inc(wsem, gp.dma_start(
                out=rdws[b][:].rearrange("p (k c) -> p k c", k=2),
                in_=ins["RDW"][e, 256 * isl:256 * (isl + 1), :].rearrange("(k p) c -> p k c", p=128)), 16)
            t_te_it = [None, None]
            pGs = [psum(f"pG2{x}", (128, CAP)) for x in range(2)]
            pUs = [psum(f"pU2{x}", (128, CAP)) for x in range(2)]
            for it in range(2):
                pG, pU = pGs[it], pUs[it]
                for k in range(8):
                    sm.wait(pe, tg1, t_xg[k], t_si2)
                    i1 = pe.matmul(pG[:], rgws[b][:, 256 * k + 128 * it:256 * k + 128 * (it + 1)],
                                   xg[k][:, CAP * e:CAP * (e + 1)],
                                   start=(k == 0), stop=(k == 7))
                tgm = sm.inc("pe", i1)
                for k in range(8):
                    sm.wait(pe, tg2, t_te_it[it] or t_te)
                    i2 = pe.matmul(pU[:], ruws[b][:, 256 * k + 128 * it:256 * k + 128 * (it + 1)],
                                   xg[k][:, CAP * e:CAP * (e + 1)],
                                   start=(k == 0), stop=(k == 7))
                tum = sm.inc("pe", i2)
                sm.wait(act, tgm, t_te)
                t_si2 = sm.inc("act", act.activation(sil2[:, CAP * it:CAP * (it + 1)], pG[:], AF.Silu))
                sm.wait(dve, t_si2, tum, t_te, t_dn_pe)
                t_te = sm.inc("dve", dve.tensor_tensor(
                    te_sb[:, CAP * it:CAP * (it + 1)],
                    sil2[:, CAP * it:CAP * (it + 1)], pU[:], ALU.mult))
                t_te_it[it] = t_te
            for it in range(2):
                for nn in range(2):
                    sm.wait(pe, t_te_it[it], tg3, t_rts, t_r2[7])
                    i3 = pe.matmul(PRo[:, 512 * nn:512 * (nn + 1)],
                                   te_sb[0:128, CAP * it:CAP * (it + 1)],
                                   rdws[b][:, 1024 * it + 512 * nn:1024 * it + 512 * (nn + 1)],
                                   start=(isl == 0 and it == 0), stop=(isl == NSL - 1 and it == 1),
                                   skip_group_check=True)
            t_dn = sm.inc("pe", i3)
            free("pG20", "pG21", "pU20", "pU21")
            t_buf_used[b] = t_dn
            t_dn_pe = t_dn
        sm.wait(act, t_dn, t_rt_store)
        t_rts = sm.inc("act", act.copy(rts[:], PRo[:]))
        free("PRo")
        sm.wait(sq, t_rts, t_zr)
        t_rt_store = sm.inc("st", sq.dma_start(
            out=RTD[CAP * e:CAP * (e + 1), :], in_=rts[:]), 16)

    free(*(["rts", "sil2", "te_sb", "rdws0", "rdws1", "ruws0", "ruws1",
            "rgws0", "rgws1"] + [f"xg{k}" for k in range(8)]))

    phase_gate(9)
    t_rtt = None
    t_pX2_rd = None
    t_out = [None] * 8
    for tt in range(NCH):
        tsl = slice(CH * tt, CH * (tt + 1))
        sm.wait(gp, t_rt_store, t_slot, t_rtt)
        t_g = sm.inc("sct", gp.indirect_dma_start(
            out=rtt[:], out_offset=None,
            in_=RTD, in_offset=IndirectOffsetOnAxis(ap=sloti[:, tt:tt + 1], axis=0)), 16)
        for k in range(8):
            pX = psum("pX2", (128, 128), BF16)
            sm.wait(pe, t_g, t_consts, t_pX2_rd, t_rts, t_te)
            t_tp = sm.inc("pe", pe.transpose(pX[:], rtt[:, 128 * k:128 * (k + 1)],
                                             cst["IDENTB"][:]))
            sm.wait(dve, t_tp, t_r2[k], t_out[k])
            t_out[k] = sm.inc("dve", dve.tensor_tensor(r[k][:, tsl], r[k][:, tsl],
                                                       pX[:], ALU.add))
            t_pX2_rd = t_out[k]
            free("pX2")
        t_rtt = t_out[7]

    t_fin = None
    for k in range(8):
        sm.wait(sq, t_out[k])
        t_fin = sm.inc("st", sq.dma_start(out=OUT[128 * k:128 * (k + 1), :], in_=r[k][:]), 16)
    sm.wait(sq, t_fin)

    stack.close()
    return nc, dbg


def _noop():
    pass


def kernel(L0_attn_hidden_states, L0_attn_Wq, L0_attn_Wk, L0_attn_Wv,
           L0_attn_Wo, L0_ffn_gate_w, L0_ffn_shared_gate_w, L0_ffn_shared_up_w,
           L0_ffn_shared_down_w, L0_ffn_routed_gate_w, L0_ffn_routed_up_w,
           L0_ffn_routed_down_w):
    x = np.asarray(L0_attn_hidden_states, np.float32).reshape(T_FULL, D)
    XT = np.ascontiguousarray(x.T)
    shared = {
        "WqT": np.ascontiguousarray(np.asarray(L0_attn_Wq, np.float32).T),
        "WkT": np.ascontiguousarray(np.asarray(L0_attn_Wk, np.float32).T),
        "WvT": np.ascontiguousarray(np.asarray(L0_attn_Wv, np.float32).T),
        "WoT": np.ascontiguousarray(np.asarray(L0_attn_Wo, np.float32).T),
        "GATE": np.ascontiguousarray(np.asarray(L0_ffn_gate_w, np.float32)),
        "SGW": np.ascontiguousarray(np.asarray(L0_ffn_shared_gate_w, np.float32)),
        "SUW": np.ascontiguousarray(np.asarray(L0_ffn_shared_up_w, np.float32)),
        "SDW": np.ascontiguousarray(np.asarray(L0_ffn_shared_down_w, np.float32)),
        "RGW": np.ascontiguousarray(np.asarray(L0_ffn_routed_gate_w, np.float32)),
        "RUW": np.ascontiguousarray(np.asarray(L0_ffn_routed_up_w, np.float32)),
        "RDW": np.ascontiguousarray(np.asarray(L0_ffn_routed_down_w, np.float32)),
    }
    shared.update(CONSTS)
    in_maps = []
    for c in range(NCORES):
        t0 = NT * c
        xpre = np.zeros((D, PRE), np.float32)
        if t0 > 0:
            xpre[:, :t0] = XT[:, :t0]
        pmt = np.zeros((128, PRE // CH), np.float32)
        pmt[:, :t0 // CH] = 1.0
        m = dict(shared)
        m["XS_T"] = np.ascontiguousarray(XT[:, t0:t0 + NT])
        m["XPRE_T"] = xpre
        m["PMT"] = pmt
        in_maps.append(m)
    nc, _ = build(tuple(DEBUG_OUTS))
    res = run_bass_kernel_spmd(nc, in_maps, core_ids=list(range(NCORES)))
    out = np.concatenate(
        [np.asarray(res.results[c]["OUT"], np.float32).T for c in range(NCORES)],
        axis=0)
    return out.reshape(1, T_FULL, D)


# version 08
# speedup vs baseline: 1.5617x; 1.2491x over previous
import contextlib
import numpy as np
import ml_dtypes
import concourse.bass as bass
import concourse.mybir as mybir
from concourse.bass_utils import run_bass_kernel_spmd
from concourse.bass_core_rust import IndirectOffsetOnAxis

F32 = mybir.dt.float32
BF16 = mybir.dt.bfloat16
I32 = mybir.dt.int32
AF = mybir.ActivationFunctionType
ALU = mybir.AluOpType
AX = mybir.AxisListType
BF = ml_dtypes.bfloat16

NCORES = 8
D = 1024
T_FULL = 16384
NT = T_FULL // NCORES
NH, FEAT, HD = 16, 16, 64
CH = 128
NCH = NT // CH
PRE = T_FULL - NT
PT = 512
NPT = PRE // PT
DE = 160
EV = 65
E_EXP = 32
I_FFN = 2048
CAP = 128
TRASH = E_EXP * CAP
NSLOT = E_EXP * CAP + 16
LN_EPS = 1e-5
ATT_EPS = 1e-12

DEBUG_OUTS = []
PHASE_LIMIT = 99


def _tri_consts():
    pairs = [(i, j) for i in range(FEAT) for j in range(i, FEAT)]
    R1 = np.zeros((DE, FEAT + 1), np.float32)
    R2k = np.zeros((DE, FEAT + 1), np.float32)
    R2q = np.zeros((DE, FEAT + 1), np.float32)
    for f, (i, j) in enumerate(pairs):
        R1[f, i] = 1.0
        R2k[f, j] = 1.0
        R2q[f, j] = (1.0 / 32.0) if i == j else (1.0 / 16.0)
    for i in range(FEAT):
        R1[136 + i, i] = 1.0
        R2k[136 + i, FEAT] = 1.0
        R2q[136 + i, FEAT] = 0.25
    R1[152, FEAT] = 1.0
    R2k[152, FEAT] = 1.0
    R2q[152, FEAT] = 1.0
    return R1, R2k, R2q


def _consts():
    c = {}
    R1, R2k, R2q = _tri_consts()
    c["R1KT"] = np.ascontiguousarray(R1.T).astype(BF)
    c["R2KT"] = np.ascontiguousarray(R2k.T).astype(BF)
    c["R12KT"] = np.ascontiguousarray(np.concatenate([R1.T, R2k.T], axis=1)).astype(BF)
    c["R1QD1"] = np.ascontiguousarray(R1[0:128].T).astype(BF)
    c["R2QD1"] = np.ascontiguousarray(R2q[0:128].T).astype(BF)
    c["R1QD2"] = np.ascontiguousarray(R1[128:160].T).astype(BF)
    c["R2QD2"] = np.ascontiguousarray(R2q[128:160].T).astype(BF)
    c["IDENTF"] = np.eye(128, dtype=np.float32)
    c["IDENTB"] = np.eye(128, dtype=BF)
    mh = 0.5 * np.triu(np.ones((CH, CH), np.float32))
    c["MASKH"] = np.tile(mh, (1, NH)).astype(BF)
    c["UEXCL"] = np.triu(np.ones((CH, CH), np.float32), 1).astype(BF)
    c["EIDX"] = np.tile(np.arange(E_EXP, dtype=np.float32)[None, :] * CAP,
                        (128, 1)).astype(np.float32)
    es = np.zeros((FEAT, NH * EV), np.float32)
    for h in range(NH):
        es[h, h * EV:(h + 1) * EV] = 1.0
    c["ESEL"] = es.astype(BF)
    c["ONESB"] = np.ones((128, 256), BF)
    c["ONECB"] = np.ones((128, 1), BF)
    c["ONECF"] = np.ones((128, 1), np.float32)
    c["ONESF"] = np.ones((1, 128), np.float32)
    return c


CONSTS = _consts()


class S:
    def __init__(self, nc, stack):
        self.h = {}
        self.cnt = {}
        self.seen = {}
        for name in ("pe", "dve", "act", "gp", "ld", "gld", "st", "sct", "wt0", "wt1"):
            self.h[name] = stack.enter_context(nc.semaphore(name))
            self.cnt[name] = 0

    def inc(self, name, inst, n=1):
        inst.then_inc(self.h[name], n)
        self.cnt[name] += n
        return (name, self.cnt[name])

    DMA_SEMS = ("ld", "gld", "st", "sct", "wt0", "wt1")

    def wait(self, eng, *toks):
        seen = {}
        for t in toks:
            if t is None:
                continue
            name, v = t
            if name in self.DMA_SEMS:
                v = self.cnt[name]
            if v > 0:
                seen[name] = max(seen.get(name, 0), v)
        done = self.seen.setdefault(id(eng), {})
        for name, v in seen.items():
            if done.get(name, -1) >= v:
                continue
            done[name] = v
            eng.wait_ge(self.h[name], v)


class _EarlyFinishDone(Exception):
    pass


def build(debug_outs=()):
    try:
        return _build(debug_outs)
    except _EarlyFinishDone as ex:
        return ex.args[0]


def _build(debug_outs=()):
    nc = bass.Bass("TRN2", target_bir_lowering=False)
    stack = contextlib.ExitStack()

    def dram_in(name, shape, dtype=F32):
        return nc.dram_tensor(name, list(shape), dtype, kind="ExternalInput").ap()

    ins = {}
    ins["XS_T"] = dram_in("XS_T", (D, NT))
    ins["XPRE_T"] = dram_in("XPRE_T", (D, PRE))
    ins["PMT"] = dram_in("PMT", (128, PRE // CH))
    for w, shp in (("WqT", (D, 256)), ("WkT", (D, 256)), ("WvT", (D, D)),
                   ("WoT", (D, D)), ("GATE", (D, E_EXP)),
                   ("SGW", (D, I_FFN)), ("SUW", (D, I_FFN)), ("SDW", (I_FFN, D)),
                   ("RGW", (E_EXP, D, I_FFN)), ("RUW", (E_EXP, D, I_FFN)),
                   ("RDW", (E_EXP, I_FFN, D))):
        ins[w] = dram_in(w, shp)
    for k, v in CONSTS.items():
        ins[k] = dram_in(k, v.shape, BF16 if v.dtype == BF else F32)

    OUT = nc.dram_tensor("OUT", [D, NT], F32, kind="ExternalOutput").ap()
    dbg = {}

    def dbg_out(name, shape, dtype=F32):
        if name in debug_outs:
            dbg[name] = nc.dram_tensor("dbg_" + name, list(shape), dtype,
                                       kind="ExternalOutput").ap()
        return dbg.get(name)

    KBOWN = nc.dram_tensor("KBOWN", [2, FEAT + 1, NH, NT], BF16, kind="Internal").ap()
    KBPRE = nc.dram_tensor("KBPRE", [2, FEAT + 1, NH, PT], BF16, kind="Internal").ap()
    ROWB = nc.dram_tensor("ROWB", [NT], F32, kind="Internal").ap()
    XSG = nc.dram_tensor("XSG", [NSLOT, D], BF16, kind="Internal").ap()
    RTD = nc.dram_tensor("RTD", [NSLOT, D], BF16, kind="Internal").ap()

    pe, dve, act, gp, sq = nc.tensor, nc.vector, nc.scalar, nc.gpsimd, nc.sync
    sm = S(nc, stack)

    def bar():
        return (("pe", sm.cnt["pe"]), ("dve", sm.cnt["dve"]),
                ("act", sm.cnt["act"]), ("gp", sm.cnt["gp"]), ("st", 1),
                ("ld", 1), ("gld", 1))

    def phase_gate(n):
        if PHASE_LIMIT > n:
            return
        for name in ("pe", "dve", "act", "gp", "ld", "gld", "st", "sct", "wt0", "wt1"):
            if sm.cnt[name] > 0:
                sq.wait_ge(sm.h[name], sm.cnt[name])
        t_fin = None
        for k in range(8):
            t_fin = sm.inc("st", sq.dma_start(
                out=OUT[128 * k:128 * (k + 1), :], in_=r[k][:]), 16)
        sm.wait(sq, t_fin)
        stack.close()
        raise _EarlyFinishDone((nc, dbg))
    frees = {}
    _seq = [0]

    def _uname(name):
        _seq[0] += 1
        return f"{name}_{_seq[0]}"

    def sb(name, shape, dtype):
        assert name not in frees, name
        cm = nc.sbuf_tensor(_uname(name), list(shape), dtype)
        t = cm.__enter__()
        frees[name] = cm
        return t

    def psum(name, shape, dtype=F32):
        assert name not in frees, name
        cm = nc.psum_tensor(_uname(name), list(shape), dtype)
        t = cm.__enter__()
        frees[name] = cm
        return t

    def free(*names):
        order = list(frees.keys())
        for n in sorted(names, key=lambda n: -order.index(n)):
            frees.pop(n).__exit__(None, None, None)

    r = [sb(f"r{k}", (128, NT), F32) for k in range(8)]
    cst = {}
    for k, v in CONSTS.items():
        cst[k] = sb("c_" + k, v.shape, BF16 if v.dtype == BF else F32)
    gate = sb("gate", (128, 8 * E_EXP), F32)
    pmt = sb("pmt", (128, PRE // CH), F32)

    t_r_in = []
    for k in range(8):
        t_r_in.append(sm.inc("ld", sq.dma_start(
            out=r[k][:], in_=ins["XS_T"][128 * k:128 * (k + 1), :]), 16))
    for k in CONSTS:
        sm.inc("ld", sq.dma_start(out=cst[k][:], in_=ins[k]), 16)
    sm.inc("ld", sq.dma_start(out=pmt[:], in_=ins["PMT"]), 16)
    t_gate = sm.inc("ld", sq.dma_start(
        out=gate[:].rearrange("p (k c) -> p k c", k=8),
        in_=ins["GATE"].rearrange("(k p) c -> p k c", p=128)), 16)
    t_consts = ("ld", sm.cnt["ld"])

    def wload(dst, src, ktiles=8):
        return sm.inc("gld", gp.dma_start(
            out=dst[:].rearrange("p (k c) -> p k c", k=ktiles),
            in_=src.rearrange("(k p) c -> p k c", p=128)), 16)


    v_sb = sb("v_sb", (128, NCH * NH * EV), BF16)
    S1 = sb("S1", (128, NH * EV), F32)
    S2 = sb("S2", (128, 4 * EV), F32)
    wk = sb("wk", (128, 8 * 256), BF16)
    wv = sb("wv", (128, 8 * 1024), BF16)
    wq = sb("wq", (128, 8 * 256), BF16)
    t_wq = wload(wq, ins["WqT"])
    t_wk = wload(wk, ins["WkT"])
    t_wv = wload(wv, ins["WvT"])

    ln_bf = [sb(f"ln{k}", (128, NT), BF16) for k in range(8)]

    def ln_exact(tag, t_src, rowM, rowR):
        sqs = sb("sqs" + tag, (128, 512), F32)
        rowS = sb("rowS" + tag, (1, NT), F32)
        rowMb = sb("rowMb" + tag, (1, NT), BF16)
        rowRb = sb("rowRb" + tag, (1, NT), BF16)
        last = None
        t_sqs = None
        for ct in range(NT // 512):
            cs = slice(512 * ct, 512 * (ct + 1))
            pA = psum(f"{tag}A", (1, 512))
            pB = psum(f"{tag}B", (1, 512))
            for k in range(8):
                sm.wait(pe, t_src[k], t_consts, last)
                i1 = pe.matmul(pA[0:1, :], cst["ONECF"][:], r[k][:, cs],
                               start=(k == 0), stop=(k == 7))
                sm.wait(act, t_src[k], t_sqs)
                t2 = sm.inc("act", act.activation(sqs[:], r[k][:, cs], AF.Square))
                sm.wait(pe, t2)
                i3 = pe.matmul(pB[0:1, :], cst["ONECF"][:], sqs[:],
                               start=(k == 0), stop=(k == 7))
                t_sqs = sm.inc("pe", i3)
            t_mm = sm.inc("pe", i1)
            sm.wait(dve, t_mm)
            t1 = sm.inc("dve", dve.tensor_scalar_mul(rowM[0:1, cs], pA[0:1, :], 1.0 / D))
            sm.wait(dve, t_sqs, t1)
            t2 = sm.inc("dve", dve.tensor_scalar_mul(rowS[0:1, cs], pB[0:1, :], 1.0 / D))
            sm.wait(dve, t1)
            t3 = sm.inc("dve", dve.tensor_tensor(rowR[0:1, cs], rowM[0:1, cs],
                                                 rowM[0:1, cs], ALU.mult))
            sm.wait(dve, t2, t3)
            t4 = sm.inc("dve", dve.tensor_tensor(rowS[0:1, cs], rowS[0:1, cs],
                                                 rowR[0:1, cs], ALU.subtract))
            sm.wait(dve, t4)
            t5 = sm.inc("dve", dve.tensor_scalar_add(rowS[0:1, cs], rowS[0:1, cs], LN_EPS))
            sm.wait(dve, t5)
            t6 = sm.inc("dve", dve.reciprocal(rowS[0:1, cs], rowS[0:1, cs]))
            sm.wait(act, t6)
            t7 = sm.inc("act", act.activation(rowR[0:1, cs], rowS[0:1, cs], AF.Sqrt))
            sm.wait(dve, t1, t7)
            t8 = sm.inc("dve", dve.tensor_copy(rowMb[0:1, cs], rowM[0:1, cs]))
            sm.wait(dve, t8)
            t9 = sm.inc("dve", dve.tensor_copy(rowRb[0:1, cs], rowR[0:1, cs]))
            free(f"{tag}A", f"{tag}B")
            pM = psum(f"{tag}M", (128, 512))
            pR = psum(f"{tag}R", (128, 512))
            sm.wait(pe, t8)
            t_mb = sm.inc("pe", pe.matmul(pM[:], cst["ONESB"][0:1, 0:128], rowMb[0:1, cs],
                                          start=True, stop=True))
            sm.wait(pe, t9)
            t_rb = sm.inc("pe", pe.matmul(pR[:], cst["ONESB"][0:1, 0:128], rowRb[0:1, cs],
                                          start=True, stop=True))
            for k in range(8):
                sm.wait(dve, t_mb, t_rb, t_sqs)
                ta = sm.inc("dve", dve.tensor_tensor(sqs[:], r[k][:, cs], pM[:],
                                                     ALU.subtract))
                t_sqs = ta
                sm.wait(dve, ta)
                last = sm.inc("dve", dve.tensor_tensor(ln_bf[k][:, cs], sqs[:], pR[:],
                                                       ALU.mult))
                t_sqs = last
            free(f"{tag}M", f"{tag}R")
        free("rowRb" + tag, "rowMb" + tag, "rowS" + tag, "sqs" + tag)
        return last

    rowM1 = sb("rowM1", (1, NT), F32)
    rowR1 = sb("rowR1", (1, NT), F32)
    t_ln1 = ln_exact("l1", t_r_in, rowM1, rowR1)
    o = dbg_out("ln1", (128, 8 * NT), BF16)
    if o is not None:
        sm.wait(sq, t_ln1)
        for k in range(8):
            sm.inc("st", sq.dma_start(out=o[:, NT * k:NT * (k + 1)], in_=ln_bf[k][:]), 16)
    phase_gate(1)

    qknat = sb("qknat", (128, NT), BF16)

    t_kb = []
    tg = None
    for iqk, wmat, t_w in ((0, wq, t_wq), (1, wk, t_wk)):
        for grp in range(2):
            for ct in range(NT // 512):
                cs = slice(512 * ct, 512 * (ct + 1))
                pg = psum("pj", (128, 512))
                for k in range(8):
                    sm.wait(pe, t_w, t_ln1, tg)
                    ii = pe.matmul(pg[:], wmat[:, 256 * k + 128 * grp:256 * k + 128 * (grp + 1)],
                                   ln_bf[k][:, cs], start=(k == 0), stop=(k == 7))
                t_mm = sm.inc("pe", ii)
                sm.wait(act, t_mm, tg, t_kb[-1] if t_kb else None)
                tg = sm.inc("act", act.copy(qknat[:, cs], pg[:]))
                free("pj")
            sm.wait(sq, tg)
            t_kb.append(sm.inc("st", sq.dma_start(
                out=KBOWN[iqk, 0:FEAT, 8 * grp:8 * (grp + 1), :].rearrange("f h t -> h f t"),
                in_=qknat[:]), 16))
    for iqk in range(2):
        sm.wait(sq, t_consts)
        t_kb.append(sm.inc("st", sq.dma_start(
            out=KBOWN[iqk, FEAT, :, :].rearrange("h t -> (h t)").rearrange("(a b) -> a b", a=128),
            in_=cst["ONESB"][:, 0:256]), 16))
    t_kb_all = t_kb[-1]
    free("qknat")

    t_vown = tg
    for tt in range(NCH):
        ts_ = slice(128 * tt, 128 * (tt + 1))
        for half in range(2):
            pv = psum("pv", (128, 512))
            for k in range(8):
                sm.wait(pe, t_wv, t_ln1, t_vown)
                ii = pe.matmul(pv[:], ln_bf[k][:, ts_],
                               wv[:, 1024 * k + 512 * half:1024 * k + 512 * (half + 1)],
                               start=(k == 0), stop=(k == 7))
            t_mm = sm.inc("pe", ii)
            sm.wait(act, t_mm, t_vown)
            t_vown = sm.inc("act", act.copy(
                v_sb[:].rearrange("p (t h e) -> p t h e", t=NCH, h=NH)[:, tt, 8 * half:8 * (half + 1), 0:64],
                pv[:].rearrange("p (h e) -> p h e", h=8)))
            free("pv")
    sm.wait(dve, t_vown)
    t_vones = sm.inc("dve", dve.memset(
        v_sb[:].rearrange("p (t h e) -> p t h e", t=NCH, h=NH)[:, :, :, 64:65], 1.0))
    o = dbg_out("vsb", (128, NCH * NH * EV), BF16)
    if o is not None:
        sm.wait(sq, t_vones)
        sm.inc("st", sq.dma_start(out=o, in_=v_sb[:]), 16)
    phase_gate(2)
    free(*[f"ln{k}" for k in range(8)] + ["rowR1", "rowM1", "wq"])

    t_sz1 = sm.inc("dve", dve.memset(S1[:], 0.0))
    t_sz2 = sm.inc("dve", dve.memset(S2[:], 0.0))

    xbf = [[sb(f"xbf{p}_{k}", (128, PT), BF16) for k in range(8)] for p in range(2)]
    xh = [[sb(f"xh{p}_{k}", (128, PT), BF16) for k in range(8)] for p in range(2)]
    sqb = sb("sqb", (128, PT), BF16)
    prowM = sb("prowM", (1, PT), F32)
    prowR = sb("prowR", (1, PT), F32)
    prowS = sb("prowS", (1, PT), F32)
    prowMb = sb("prowMb", (1, PT), BF16)
    prowRb = sb("prowRb", (1, PT), BF16)
    knat = sb("knat", (128, 2 * PT), BF16)
    ktp = sb("ktp", (FEAT + 1, NH * PT), BF16)
    kf = sb("kf", (128, 2048), BF16)
    rep_sb = sb("rep_sb", (128, 4 * DE), BF16)
    vpre = sb("vpre", (128, 4 * NH * EV), BF16)
    pmb = sb("pmb", (128, 4), F32)

    t_pm = None
    t_pm_rd = None
    t_row_rd = None
    t_rep_act = None
    t_kf_last = [None]
    t_xbf = [None] * 8
    t_xh = [None] * 8
    t_xh_war = [[None] * 8, [None] * 8]
    t_knat = None
    t_ktp_w = [None, None]
    t_ktp_r = [None, None]
    t_kf = None
    t_vpre = None
    t_state = None
    t_u1p = t_u2p = None

    t_pre_ones = None
    for ib in range(2):
        for hhalf in range(2):
            sm.wait(sq, t_consts)
            t_pre_ones = sm.inc("st", sq.dma_start(
                out=KBPRE[ib, FEAT, 8 * hhalf:8 * (hhalf + 1), :]
                    .rearrange("h t -> (h t)").rearrange("(a b) -> a b", a=16),
                in_=cst["ONESB"][0:16, 0:256]), 16)

    t_bar3 = bar()
    t_rep_rd = None
    for j in range(NPT):
        js = slice(PT * j, PT * (j + 1))
        buf = j % 2
        xb, xhh = xbf[buf], xh[buf]
        t_xb_prev = t_xbf[buf * 0:8] if False else None
        for k in range(8):
            sm.wait(gp, t_xh_war[buf][k], *(t_bar3 if j == 0 else ()))
            t_xbf[k] = sm.inc("wt%d" % buf, gp.dma_start(
                out=xb[k][:], in_=ins["XPRE_T"][128 * k:128 * (k + 1), js]), 16)
        sm.wait(dve, t_consts, t_pm_rd, *(t_bar3 if j == 0 else ()))
        t_pm = sm.inc("dve", dve.tensor_copy(pmb[:], pmt[:, 4 * j:4 * (j + 1)]))
        pA = psum("ppA", (1, PT))
        pB = psum("ppB", (1, PT))
        t_sqb = None
        for k in range(8):
            sm.wait(pe, t_xbf[k], t_consts, t_pm, t_vown, t_row_rd)
            i1 = pe.matmul(pA[0:1, :], cst["ONECB"][:], xb[k][:],
                           start=(k == 0), stop=(k == 7))
            sm.wait(act, t_xbf[k], t_sqb)
            t2 = sm.inc("act", act.activation(sqb[:], xb[k][:], AF.Square))
            sm.wait(pe, t2)
            i3 = pe.matmul(pB[0:1, :], cst["ONECB"][:], sqb[:],
                           start=(k == 0), stop=(k == 7))
            t_sqb = sm.inc("pe", i3)
        t_mm = sm.inc("pe", i1)
        sm.wait(dve, t_mm, t_pm)
        t1 = sm.inc("dve", dve.tensor_scalar_mul(prowM[0:1, :], pA[0:1, :], 1.0 / D))
        sm.wait(dve, t_sqb, t1)
        t2 = sm.inc("dve", dve.tensor_scalar_mul(prowS[0:1, :], pB[0:1, :], 1.0 / D))
        sm.wait(dve, t1)
        t3 = sm.inc("dve", dve.tensor_tensor(prowR[0:1, :], prowM[0:1, :],
                                             prowM[0:1, :], ALU.mult))
        sm.wait(dve, t2, t3)
        t4 = sm.inc("dve", dve.tensor_tensor(prowS[0:1, :], prowS[0:1, :],
                                             prowR[0:1, :], ALU.subtract))
        sm.wait(dve, t4)
        t5 = sm.inc("dve", dve.tensor_scalar_add(prowS[0:1, :], prowS[0:1, :], LN_EPS))
        sm.wait(dve, t5)
        t6 = sm.inc("dve", dve.reciprocal(prowS[0:1, :], prowS[0:1, :]))
        sm.wait(act, t6)
        t7 = sm.inc("act", act.activation(prowR[0:1, :], prowS[0:1, :], AF.Sqrt))
        sm.wait(dve, t1, t7)
        t8 = sm.inc("dve", dve.tensor_copy(prowMb[0:1, :], prowM[0:1, :]))
        sm.wait(dve, t8)
        t9 = sm.inc("dve", dve.tensor_copy(prowRb[0:1, :], prowR[0:1, :]))
        free("ppA", "ppB")
        pM = psum("ppM", (128, PT))
        pR = psum("ppR", (128, PT))
        sm.wait(pe, t8)
        t_mb = sm.inc("pe", pe.matmul(pM[:], cst["ONESB"][0:1, 0:128], prowMb[0:1, :],
                                      start=True, stop=True))
        sm.wait(pe, t9)
        t_rb = sm.inc("pe", pe.matmul(pR[:], cst["ONESB"][0:1, 0:128], prowRb[0:1, :],
                                      start=True, stop=True))
        t_row_rd = t_rb
        t_xh_new = [None] * 8
        for k in range(8):
            sm.wait(dve, t_mb, t_rb, t_xbf[k], t_knat, t_xh_war[buf][k])
            ta = sm.inc("dve", dve.tensor_tensor(xb[k][:], xb[k][:], pM[:],
                                                 ALU.subtract))
            sm.wait(dve, ta)
            t_xh_new[k] = sm.inc("dve", dve.tensor_tensor(xhh[k][:], xb[k][:], pR[:],
                                                          ALU.mult))
        t_xh = t_xh_new
        free("ppM", "ppR")
        pK = psum("ppK", (128, PT))
        tk_ev = t_knat
        for grp in range(2):
            for k in range(8):
                sm.wait(pe, t_wk, t_xh[k], t_xh[7], tk_ev)
                ii = pe.matmul(pK[:], wk[:, 256 * k + 128 * grp:256 * k + 128 * (grp + 1)],
                               xhh[k][:], start=(k == 0), stop=(k == 7))
            t_mm = sm.inc("pe", ii)
            sm.wait(act, t_mm, tk_ev)
            tk_ev = sm.inc("act", act.copy(knat[:, PT * grp:PT * (grp + 1)], pK[:]))
        t_knat = tk_ev
        free("ppK")
        sm.wait(sq, t_knat, t_ktp_r[buf], t_pre_ones)
        tw = sm.inc("st", sq.dma_start(
            out=KBPRE[buf, 0:FEAT, :, :].rearrange("f (g h) t -> h f g t", g=2),
            in_=knat[:].rearrange("p (g t) -> p g t", g=2)), 16)
        t_ktp_w[buf] = tw
        sm.wait(sq, tw)
        tr = sm.inc("ld", sq.dma_start(
            out=ktp[:].rearrange("f (h t) -> f h t", h=NH),
            in_=KBPRE[buf, :, :, :]), 16)
        t_ktp_r[buf] = tr
        t_vev = t_vown
        for cc in range(4):
            tsl = slice(128 * cc, 128 * (cc + 1))
            pV = psum("ppV", (128, 512))
            for half in range(2):
                for k in range(8):
                    sm.wait(pe, t_wv, t_xh[k], t_vev, t_knat, t_state)
                    ii = pe.matmul(pV[:], xhh[k][:, tsl],
                                   wv[:, 1024 * k + 512 * half:1024 * k + 512 * (half + 1)],
                                   start=(k == 0), stop=(k == 7))
                t_mm = sm.inc("pe", ii)
                sm.wait(act, t_mm, t_vev)
                t_vev = sm.inc("act", act.copy(
                    vpre[:].rearrange("p (c h e) -> p c h e", c=4, h=NH)
                        [:, cc, 8 * half:8 * (half + 1), 0:64],
                    pV[:].rearrange("p (h e) -> p h e", h=8)))
            free("ppV")
            sm.wait(dve, t_vev, t_pm)
            t_vm = sm.inc("dve", dve.tensor_scalar_mul(
                vpre[:].rearrange("p (c h e) -> p c h e", c=4, h=NH)[:, cc, :, 64:65]
                    .rearrange("p h o -> p (h o)"),
                cst["ONESB"][:, 0:16], pmb[:, cc:cc + 1]))
        t_vpre = t_vm
        t_pm_rd = t_vm
        PSS1p = psum("PSS1p", (128, 1024))
        PSS2p = psum("PSS2p", (128, 512))
        pR1 = [psum("ppR1a", (128, 1024)), psum("ppR1b", (128, 1024))]
        t_kf_h = [None] * (NH + 1)
        kf2 = kf

        def emit_reps(h):
            p = h % 2
            t_sub_rd = t_reprd[p]
            for sub in range(2):
                pR = pR1[p]
                for tt2 in range(2):
                    cc = 2 * sub + tt2
                    lhs = ktp[:, PT * h + 128 * cc:PT * h + 128 * (cc + 1)]
                    sm.wait(pe, tr, t_consts, t_sub_rd)
                    i1 = pe.matmul(pR[:, 512 * tt2:512 * tt2 + 2 * DE], lhs,
                                   cst["R12KT"][:], start=True, stop=True)
                t_rep = sm.inc("pe", i1)
                sm.wait(act, t_rep, t_repact[0])
                te = sm.inc("act", act.copy(
                    rep_sb[:].rearrange("p (q a f) -> p q a f", q=2, a=2)[:, p],
                    pR[:].rearrange("p (a f) -> p a f", a=2)[:, :, 0:DE]))
                t_repact[0] = te
                sm.wait(dve, te, t_kfd[0])
                t_kfd[0] = sm.inc("dve", dve.tensor_tensor(
                    kf2[:].rearrange("p (q c f) -> p q c f", q=2, c=4)
                        [:, p, 2 * sub:2 * sub + 2, 0:DE],
                    pR[:].rearrange("p (a f) -> p a f", a=2)[:, :, DE:2 * DE],
                    rep_sb[:].rearrange("p (q a f) -> p q a f", q=2, a=2)[:, p],
                    ALU.mult))
                t_sub_rd = t_kfd[0]
            t_kf_h[h] = t_kfd[0]
            t_reprd[p] = t_kfd[0]

        def emit_states(h):
            p = h % 2
            hg = h % 8
            for cc in range(4):
                sm.wait(pe, t_kf_h[h], t_vpre, t_u2p)
                i1 = pe.matmul(PSS1p[:, 128 * hg:128 * hg + EV],
                               kf2[:, 1024 * p + 256 * cc:1024 * p + 256 * cc + 128],
                               vpre[:, 1040 * cc + EV * h:1040 * cc + EV * (h + 1)],
                               start=(cc == 0), stop=(cc == 3),
                               skip_group_check=True)
                i2 = pe.matmul(PSS2p[32 * (h % 4):32 * (h % 4) + 32,
                                     EV * (h // 4):EV * (h // 4) + EV],
                               kf2[:, 1024 * p + 256 * cc + 128:1024 * p + 256 * cc + DE],
                               vpre[:, 1040 * cc + EV * h:1040 * cc + EV * (h + 1)],
                               start=(cc == 0), stop=(cc == 3),
                               tile_position=(0, 32 * (h % 4)),
                               skip_group_check=True)
            return sm.inc("pe", i2)

        t_reprd = [t_rep_rd, t_rep_rd]
        t_repact = [t_rep_act]
        t_kfd = [t_kf_last[0]]
        for half in range(2):
            for hh in range(9):
                h = 8 * half + hh
                if hh < 8 and h < NH:
                    emit_reps(h)
                if hh > 0:
                    t_state = emit_states(8 * half + hh - 1)
            sm.wait(dve, t_state, t_sz1, t_sz2, t_u1p, t_u2p)
            t_u1p = sm.inc("dve", dve.tensor_tensor(
                S1[:].rearrange("p (h e) -> p h e", h=NH)[:, 8 * half:8 * half + 8],
                S1[:].rearrange("p (h e) -> p h e", h=NH)[:, 8 * half:8 * half + 8],
                PSS1p[:].rearrange("p (h e) -> p h e", h=8)[:, :, 0:EV], ALU.add))
            if half == 1:
                sm.wait(dve, t_u1p)
                t_u2p = sm.inc("dve", dve.tensor_tensor(S2[:], S2[:],
                                                        PSS2p[:, 0:4 * EV], ALU.add))
        t_rep_rd = t_reprd[1]
        t_rep_act = t_repact[0]
        t_kf_last[0] = t_kfd[0]
        free("ppR1a", "ppR1b", "PSS1p", "PSS2p")
        for k in range(8):
            t_xh_war[buf][k] = t_state

    free(*([f"xbf{p}_{k}" for p in range(2) for k in range(8)] +
           [f"xh{p}_{k}" for p in range(2) for k in range(8)] +
           ["sqb", "prowM", "prowR", "prowS", "prowMb", "prowRb", "knat", "ktp",
            "kf", "rep_sb", "vpre", "pmb", "wv", "wk"]))
    phase_gate(3)

    S1b = sb("S1b", (128, NH * EV), BF16)
    S2b = sb("S2b", (128, 4 * EV), BF16)
    t_s1 = t_u1p
    t_s2 = t_u2p
    sm.wait(act, t_u1p, t_u2p, t_sz1, t_sz2)
    t_s1b = sm.inc("act", act.copy(S1b[:], S1[:]))
    t_s2b = sm.inc("act", act.copy(S2b[:], S2[:]))
    o = dbg_out("s1", (128, NH * EV), F32)
    if o is not None:
        sm.wait(sq, t_s1)
        sm.inc("st", sq.dma_start(out=o, in_=S1[:]), 16)

    y_sb = [sb(f"y{k}", (128, NT), BF16) for k in range(8)]
    qt = sb("qt", (FEAT + 1, NH * CH), BF16)
    kt = sb("kt", (FEAT + 1, NH * CH), BF16)
    qf1 = sb("qf1", (128, NH * CH), BF16)
    qf2 = sb("qf2", (128, 4 * CH), BF16)
    kfo = sb("kfo", (128, NH * DE), BF16)
    repo = sb("repo", (128, 4 * DE), BF16)
    qrep = sb("qrep", (128, 512), BF16)
    sqf = sb("sqf", (128, NH * CH), BF16)
    at = sb("at", (128, NH * CH), BF16)
    nums = sb("nums", (EV, NH * CH), BF16)
    dens = sb("dens", (128, NH), F32)
    recs = sb("recs", (128, NH), F32)
    recT = sb("recT", (16, CH), BF16)

    t_qt = t_kt = None
    t_kfo = t_qf = t_at = t_num_ev = t_y = t_recT = None
    t_u1 = t_u2 = None
    t_pss = t_dmm = t_bc = t_tr = None
    t_bar4 = bar()

    for c in range(NCH):
        tsl = slice(CH * c, CH * (c + 1))
        sm.wait(sq, t_kb_all, t_qt, t_dmm, *(t_bar4 if c == 0 else ()))
        t_qt = sm.inc("ld", sq.dma_start(
            out=qt[:].rearrange("f (h t) -> f h t", h=NH),
            in_=KBOWN[0, :, :, tsl]), 16)
        sm.wait(sq, t_kt, t_dmm)
        t_kt = sm.inc("ld", sq.dma_start(
            out=kt[:].rearrange("f (h t) -> f h t", h=NH),
            in_=KBOWN[1, :, :, tsl]), 16)

        pRc = psum("scRc", (128, 2048))
        t_kfo_new = t_kfo
        for rnd in range(4):
            for hh in range(4):
                h = 4 * rnd + hh
                lhs = kt[:, CH * h:CH * (h + 1)]
                sm.wait(pe, t_kt, t_consts, t_kfo_new, t_num_ev, t_y)
                i1 = pe.matmul(pRc[:, 512 * hh:512 * hh + 2 * DE], lhs,
                               cst["R12KT"][:], start=True, stop=True)
            t_rep = sm.inc("pe", i1)
            sm.wait(act, t_rep)
            te = sm.inc("act", act.copy(
                repo[:].rearrange("p (a f) -> p a f", a=4),
                pRc[:].rearrange("p (a f) -> p a f", a=4)[:, :, 0:DE]))
            sm.wait(dve, te, t_pss)
            t_kfo_new = sm.inc("dve", dve.tensor_tensor(
                kfo[:].rearrange("p (h f) -> p h f", h=NH)[:, 4 * rnd:4 * rnd + 4, :],
                pRc[:].rearrange("p (a f) -> p a f", a=4)[:, :, DE:2 * DE],
                repo[:].rearrange("p (a f) -> p a f", a=4), ALU.mult))
        t_kfo = t_kfo_new
        free("scRc")

        PSS1 = psum("PSS1", (128, 2048))
        PSS2 = psum("PSS2", (128, 512))
        for h in range(NH):
            sm.wait(pe, t_kfo, t_vones, t_u1, t_u2, t_num_ev, t_y)
            i1 = pe.matmul(PSS1[:, 128 * h:128 * h + EV],
                           kfo[:, DE * h:DE * h + 128],
                           v_sb[:, 1040 * c + EV * h:1040 * c + EV * (h + 1)],
                           start=True, stop=True)
            i2 = pe.matmul(PSS2[32 * (h % 4):32 * (h % 4) + 32,
                                EV * (h // 4):EV * (h // 4) + EV],
                           kfo[:, DE * h + 128:DE * h + DE],
                           v_sb[:, 1040 * c + EV * h:1040 * c + EV * (h + 1)],
                           start=True, stop=True, tile_position=(0, 32 * (h % 4)))
        t_pss = sm.inc("pe", i2)

        pQ1 = psum("scQ1", (128, 512))
        pQ2 = psum("scQ2", (128, 512))
        t_qf_new = t_qf
        for g in range(4):
            rhs = qt[:, 512 * g:512 * (g + 1)]
            sm.wait(pe, t_qt, t_consts, t_qf_new, t_num_ev, t_y)
            i1 = pe.matmul(pQ1[:], cst["R1QD1"][:], rhs, start=True, stop=True)
            i2 = pe.matmul(pQ2[:], cst["R2QD1"][:], rhs, start=True, stop=True)
            t_rep = sm.inc("pe", i2)
            sm.wait(act, t_rep)
            te = sm.inc("act", act.copy(qrep[:], pQ1[:]))
            sm.wait(dve, te)
            t_qf_new = sm.inc("dve", dve.tensor_tensor(
                qf1[:, 512 * g:512 * (g + 1)], pQ2[:], qrep[:], ALU.mult))
        for h in range(NH):
            rhs = qt[:, CH * h:CH * (h + 1)]
            jj = h % 4
            sm.wait(pe, t_qt, t_consts, t_qf_new)
            i1 = pe.matmul(pQ1[32 * jj:32 * jj + 32, 128 * (h // 4):128 * (h // 4) + 128],
                           cst["R1QD2"][:], rhs, start=True, stop=True,
                           tile_position=(0, 32 * jj))
            i2 = pe.matmul(pQ2[32 * jj:32 * jj + 32, 128 * (h // 4):128 * (h // 4) + 128],
                           cst["R2QD2"][:], rhs, start=True, stop=True,
                           tile_position=(0, 32 * jj))
        t_rep = sm.inc("pe", i2)
        sm.wait(act, t_rep)
        te = sm.inc("act", act.copy(qrep[:], pQ1[:]))
        sm.wait(dve, te)
        t_qf = sm.inc("dve", dve.tensor_tensor(qf2[:], pQ2[:], qrep[:], ALU.mult))
        free("scQ1", "scQ2")

        sm.wait(dve, t_pss, t_s1, t_s2, t_s1b, t_s2b)
        t_u1 = sm.inc("dve", dve.tensor_tensor(
            S1[:], S1[:],
            PSS1[:].rearrange("p (h e) -> p h e", h=NH)[:, :, 0:EV], ALU.add))
        sm.wait(dve, t_u1)
        t_u2 = sm.inc("dve", dve.tensor_tensor(S2[:], S2[:], PSS2[:, 0:4 * EV], ALU.add))
        free("PSS1", "PSS2")

        PS_S = psum("PS_S", (128, 2048))
        for h in range(NH):
            sm.wait(pe, t_qt, t_kt, t_at, t_u2)
            ii = pe.matmul(PS_S[:, CH * h:CH * (h + 1)],
                           kt[0:FEAT, CH * h:CH * (h + 1)],
                           qt[0:FEAT, CH * h:CH * (h + 1)], start=True, stop=True)
        t_smm = sm.inc("pe", ii)
        sm.wait(act, t_smm, t_at)
        t_sq = sm.inc("act", act.activation(sqf[:], PS_S[:], AF.Square,
                                            bias=1.0, scale=0.25))
        sm.wait(dve, t_sq, t_consts, t_dmm)
        t_at = sm.inc("dve", dve.scalar_tensor_tensor(
            at[:], sqf[:], 1.0, cst["MASKH"][:], ALU.add, ALU.mult))
        free("PS_S")

        PS_N = psum("PS_N", (EV, 2048))
        for h in range(NH):
            hs = slice(CH * h, CH * (h + 1))
            jj = h % 4
            sm.wait(pe, t_at, t_vones, t_s1b, t_s2b, t_qf, t_num_ev)
            i1 = pe.matmul(PS_N[:, hs],
                           v_sb[:, 1040 * c + EV * h:1040 * c + EV * (h + 1)],
                           at[:, hs], start=True, stop=False)
            i2 = pe.matmul(PS_N[:, hs], S1b[:, EV * h:EV * (h + 1)],
                           qf1[:, hs], start=False, stop=False)
            i3 = pe.matmul(PS_N[:, hs],
                           S2b[32 * jj:32 * jj + 32, EV * (h // 4):EV * (h // 4) + EV],
                           qf2[32 * jj:32 * jj + 32, 128 * (h // 4):128 * (h // 4) + 128],
                           start=False, stop=True, tile_position=(32 * jj, 0))
        t_num = sm.inc("pe", i3)

        PS_D = psum("PS_D", (128, 16))
        for h in range(NH):
            hs = slice(CH * h, CH * (h + 1))
            jj = h % 4
            sm.wait(pe, t_at, t_s1b, t_s2b, t_qf, t_recT)
            i1 = pe.matmul(PS_D[:, h:h + 1], at[:, hs], cst["ONECB"][:],
                           start=True, stop=False)
            i2 = pe.matmul(PS_D[:, h:h + 1], qf1[:, hs],
                           S1b[:, EV * h + 64:EV * h + 65], start=False, stop=False)
            i3 = pe.matmul(PS_D[:, h:h + 1],
                           qf2[32 * jj:32 * jj + 32, 128 * (h // 4):128 * (h // 4) + 128],
                           S2b[32 * jj:32 * jj + 32,
                               EV * (h // 4) + 64:EV * (h // 4) + 65],
                           start=False, stop=True, tile_position=(32 * jj, 0))
        t_dmm = sm.inc("pe", i3)
        sm.wait(dve, t_dmm, t_tr)
        td1 = sm.inc("dve", dve.tensor_scalar_add(dens[:], PS_D[:], ATT_EPS))
        sm.wait(dve, td1)
        td2 = sm.inc("dve", dve.reciprocal(recs[:], dens[:]))
        free("PS_D")
        PS_R = psum("PS_R", (16, 128))
        sm.wait(pe, td2, t_recT)
        t_tr = sm.inc("pe", pe.transpose(PS_R[:], recs[:], cst["IDENTF"][:]))
        sm.wait(act, t_tr, t_recT, t_bc)
        t_recT = sm.inc("act", act.copy(recT[:], PS_R[:]))
        free("PS_R")
        PS_B = psum("PS_B", (EV, 2048))
        for h in range(NH):
            sm.wait(pe, t_recT, t_consts, t_y)
            ii = pe.matmul(PS_B[:, CH * h:CH * (h + 1)],
                           cst["ESEL"][:, EV * h:EV * (h + 1)], recT[:],
                           start=True, stop=True)
        t_bc = sm.inc("pe", ii)
        sm.wait(act, t_num, t_num_ev, t_y)
        t_num_ev = sm.inc("act", act.copy(nums[:], PS_N[:]))
        t_y_new = t_y
        for h in range(NH):
            hs = slice(CH * h, CH * (h + 1))
            sm.wait(dve, t_num_ev, t_bc)
            t_y_new = sm.inc("dve", dve.tensor_tensor(
                y_sb[h // 2][64 * (h % 2):64 * (h % 2) + 64, tsl],
                nums[0:64, hs], PS_B[0:64, hs], ALU.mult))
        t_y = t_y_new
        free("PS_B", "PS_N")

        sm.wait(act, t_u1, t_u2, t_num, t_dmm)
        t_s1b = sm.inc("act", act.copy(S1b[:], S1[:]))
        t_s2b = sm.inc("act", act.copy(S2b[:], S2[:]))

    o = dbg_out("ysb", (128, 8 * NT), BF16)
    if o is not None:
        sm.wait(sq, t_y)
        for k in range(8):
            sm.inc("st", sq.dma_start(out=o[:, NT * k:NT * (k + 1)], in_=y_sb[k][:]), 16)
    phase_gate(4)

    free("recT", "recs", "dens", "nums", "at", "sqf", "qrep", "repo",
         "kfo", "qf2", "qf1", "kt", "qt")

    wo = sb("wo", (128, 8 * 1024), BF16)
    sm.wait(gp, *bar())
    t_wo = wload(wo, ins["WoT"])
    t_r1 = [None] * 8
    t_pwo_rd = None
    for m in range(8):
        for ct in range(NT // 512):
            cs = slice(512 * ct, 512 * (ct + 1))
            pw = psum("pwo", (128, 512))
            for k in range(8):
                sm.wait(pe, t_wo, t_y, t_pwo_rd)
                ii = pe.matmul(pw[:], wo[:, 1024 * k + 128 * m:1024 * k + 128 * (m + 1)],
                               y_sb[k][:, cs], start=(k == 0), stop=(k == 7))
            t_mm = sm.inc("pe", ii)
            sm.wait(dve, t_mm, t_r1[m])
            t_r1[m] = sm.inc("dve", dve.tensor_tensor(r[m][:, cs], r[m][:, cs],
                                                      pw[:], ALU.add))
            t_pwo_rd = t_r1[m]
            free("pwo")
    free(*(["wo"] + [f"y{k}" for k in range(8)] +
           ["S2b", "S1b", "S2", "S1", "v_sb"]))
    o = dbg_out("r1", (128, 8 * NT), F32)
    if o is not None:
        sm.wait(sq, *t_r1)
        for k in range(8):
            sm.inc("st", sq.dma_start(out=o[:, NT * k:NT * (k + 1)], in_=r[k][:]), 16)

    phase_gate(5)
    Bcol = sb("Bcol", (32, 1), F32)
    masksb = sb("masksb", (128, NCH * E_EXP), BF16)
    scrowb = sb("scrowb", (1, NT), BF16)
    rks = sb("rks", (128, E_EXP), F32)
    brow = sb("brow", (1, E_EXP), F32)
    browb = sb("browb", (1, E_EXP), BF16)
    slotf = sb("slotf", (128, NCH), F32)
    sloti = sb("sloti", (128, NCH), I32)
    tmp32 = sb("tmp32", (128, E_EXP), F32)
    tmpc = sb("tmpc", (128, 1), F32)
    zero_sb = sb("zero_sb", (128, D), BF16)
    xst = sb("xst", (128, D), BF16)
    rtt = sb("rtt", (128, D), BF16)
    rowM = sb("rowM", (1, NT), F32)
    rowR = sb("rowR", (1, NT), F32)
    ln_bf = [sb(f"ln{k}", (128, NT), BF16) for k in range(8)]

    t_ln2 = ln_exact("l2", [t_r1[7]] * 8, rowM, rowR)
    o = dbg_out("ln2", (128, 8 * NT), BF16)
    if o is not None:
        sm.wait(sq, t_ln2)
        for k in range(8):
            sm.inc("st", sq.dma_start(out=o[:, NT * k:NT * (k + 1)], in_=ln_bf[k][:]), 16)

    phase_gate(6)
    negC = sb("negC", (32, NT), F32)
    mnrow = sb("mnrow", (16, CH), F32)
    ctok = sb("ctok", (128, NCH * E_EXP), F32)
    mneg = sb("mneg", (128, NCH), F32)
    scr0 = sb("scr0", (1, NT), F32)
    scrV = sb("scrV", (1, NT), F32)

    pBc = psum("pBc", (32, 1))
    for k in range(8):
        sm.wait(pe, t_gate, t_consts, t_r1[7], t_ln2)
        ii = pe.matmul(pBc[:], gate[:, 32 * k:32 * (k + 1)], cst["ONECF"][:, 0:1],
                       start=(k == 0), stop=(k == 7))
    t_bm = sm.inc("pe", ii)
    sm.wait(dve, t_bm)
    t_bcol = sm.inc("dve", dve.tensor_copy(Bcol[:], pBc[:]))
    free("pBc")

    t_negC = None
    for ct in range(NT // 512):
        cs = slice(512 * ct, 512 * (ct + 1))
        pAg = psum("pAg", (32, 512))
        for k in range(8):
            sm.wait(pe, t_r1[k], t_gate, t_ln2, t_negC, t_bcol)
            ii = pe.matmul(pAg[:], gate[:, 32 * k:32 * (k + 1)], r[k][:, cs],
                           start=(k == 0), stop=(k == 7))
        t_ag = sm.inc("pe", ii)
        pMb = psum("pMb", (32, 512))
        sm.wait(pe, t_negC)
        t_mb = sm.inc("pe", pe.matmul(pMb[:], cst["ONESF"][0:1, 0:32], rowM[0:1, cs],
                                      start=True, stop=True))
        sm.wait(act, t_mb, t_negC)
        t_me = sm.inc("act", act.copy(negC[:, cs], pMb[:]))
        sm.wait(dve, t_me, t_ag, t_bcol)
        t_negC = sm.inc("dve", dve.scalar_tensor_tensor(
            negC[:, cs], negC[:, cs], Bcol[:, 0:1], pAg[:], ALU.mult, ALU.subtract))
        free("pAg", "pMb")

    pT = psum("pT", (128, 512))
    t_ct = None
    for tt in range(NCH):
        sm.wait(pe, t_negC, t_ct)
        ii = pe.transpose(pT[:, 32 * tt:32 * (tt + 1)],
                          negC[:, CH * tt:CH * (tt + 1)], cst["IDENTF"][0:32, 0:32])
        t_ct = sm.inc("pe", ii)
    sm.wait(dve, t_ct)
    t_ctok = sm.inc("dve", dve.tensor_copy(ctok[:], pT[:]))
    free("pT")
    sm.wait(dve, t_ctok)
    t_mn = sm.inc("dve", dve.tensor_reduce(
        mneg[:].rearrange("p (t o) -> p t o", o=1),
        ctok[:].rearrange("p (t e) -> p t e", e=E_EXP), axis=AX.X, op=ALU.min))
    t_mk = None
    for tt in range(NCH):
        sm.wait(dve, t_mn)
        t_mk = sm.inc("dve", dve.tensor_single_scalar(
            masksb[:, 32 * tt:32 * (tt + 1)], ctok[:, 32 * tt:32 * (tt + 1)],
            mneg[:, tt:tt + 1], ALU.is_equal))
    t_mkb = t_mk

    PS_MT = psum("PS_MT", (16, 128))
    sm.wait(pe, t_mn)
    t_mt = sm.inc("pe", pe.transpose(PS_MT[:], mneg[:], cst["IDENTF"][:]))
    sm.wait(act, t_mt)
    t_mr = sm.inc("act", act.copy(mnrow[:], PS_MT[:]))
    free("PS_MT")
    sm.wait(sq, t_mr)
    t_rw = sm.inc("st", sq.dma_start(
        out=ROWB.rearrange("(a b) -> a b", a=16), in_=mnrow[:]), 16)
    sm.wait(sq, t_rw)
    t_rr = sm.inc("ld", sq.dma_start(
        out=scr0[0:1, :], in_=ROWB.rearrange("(o t) -> o t", o=1)), 16)
    sm.wait(dve, t_rr, t_ln2)
    t_val = sm.inc("dve", dve.scalar_tensor_tensor(
        scrV[0:1, :], scr0[0:1, :], -1.0, rowR[0:1, :], ALU.mult, ALU.mult))
    sm.wait(act, t_val)
    t_sc = sm.inc("act", act.activation(scr0[0:1, :], scrV[0:1, :], AF.Sigmoid))
    sm.wait(dve, t_sc)
    t_scb = sm.inc("dve", dve.tensor_copy(scrowb[:], scr0[0:1, :]))
    o = dbg_out("scores", (1, NT), F32)
    if o is not None:
        sm.wait(sq, t_sc)
        sm.inc("st", sq.dma_start(out=o, in_=scr0[0:1, :]), 16)
    o = dbg_out("masks", (128, NCH * E_EXP), BF16)
    if o is not None:
        sm.wait(sq, t_mk)
        sm.inc("st", sq.dma_start(out=o, in_=masksb[:]), 16)
    sm.wait(sq, t_ctok, t_rw, t_scb, t_mk)
    free("scrV", "scr0", "mneg", "ctok", "mnrow", "negC")

    phase_gate(7)
    xs = [sb(f"xs{k}", (128, NT), BF16) for k in range(8)]
    t_xs = [None] * 8
    for ct in range(NT // 512):
        cs = slice(512 * ct, 512 * (ct + 1))
        pSc = psum("pSc", (128, 512))
        sm.wait(pe, t_scb, t_xs[7])
        t_sb_ = sm.inc("pe", pe.matmul(pSc[:], cst["ONESB"][0:1, 0:128], scrowb[0:1, cs],
                                       start=True, stop=True))
        for k in range(8):
            sm.wait(dve, t_sb_, t_ln2, ("st", 1))
            t_xs[k] = sm.inc("dve", dve.tensor_tensor(xs[k][:, cs], ln_bf[k][:, cs],
                                                      pSc[:], ALU.mult))
        free("pSc")

    sgws = [sb("sgws0", (128, 8 * 512), BF16)]
    suws = [sb("suws0", (128, 8 * 512), BF16)]
    sdws = [sb("sdws0", (128, 4 * 1024), BF16)]
    silu_sb = sb("silu_sb", (128, 256), BF16)
    tsh = sb("tsh", (128, 256), BF16)
    t_sw_used = [None]
    t_r2 = list(t_r1)

    sm.wait(gp, *bar())
    for isl in range(4):
        b = 0
        sm.wait(gp, t_sw_used[b])
        tg1 = sm.inc("gld", gp.dma_start(
            out=sgws[b][:].rearrange("p (k c) -> p k c", k=8),
            in_=ins["SGW"][:, 512 * isl:512 * (isl + 1)].rearrange("(k p) c -> p k c", p=128)), 16)
        tg2 = sm.inc("gld", gp.dma_start(
            out=suws[b][:].rearrange("p (k c) -> p k c", k=8),
            in_=ins["SUW"][:, 512 * isl:512 * (isl + 1)].rearrange("(k p) c -> p k c", p=128)), 16)
        tg3 = sm.inc("gld", gp.dma_start(
            out=sdws[b][:].rearrange("p (k c) -> p k c", k=4),
            in_=ins["SDW"][512 * isl:512 * (isl + 1), :].rearrange("(k p) c -> p k c", p=128)), 16)
        t_si = t_th = None
        for ct in range(NT // 256):
            cs = slice(256 * ct, 256 * (ct + 1))
            pDn = psum("pDn", (128, 2048))
            for it in range(4):
                pG = psum("pG", (128, 256))
                pU = psum("pU", (128, 256))
                for k in range(8):
                    sm.wait(pe, tg1, tg2, t_ln2, t_si, t_xs[7])
                    i1 = pe.matmul(pG[:], sgws[b][:, 512 * k + 128 * it:512 * k + 128 * (it + 1)],
                                   ln_bf[k][:, cs], start=(k == 0), stop=(k == 7))
                tgm = sm.inc("pe", i1)
                for k in range(8):
                    sm.wait(pe, tg2, t_th)
                    i2 = pe.matmul(pU[:], suws[b][:, 512 * k + 128 * it:512 * k + 128 * (it + 1)],
                                   ln_bf[k][:, cs], start=(k == 0), stop=(k == 7))
                tum = sm.inc("pe", i2)
                sm.wait(act, tgm, t_th)
                t_si = sm.inc("act", act.activation(silu_sb[:], pG[:], AF.Silu))
                sm.wait(dve, t_si, tum, t_th)
                t_th = sm.inc("dve", dve.tensor_tensor(tsh[:], silu_sb[:], pU[:],
                                                       ALU.mult))
                free("pG", "pU")
                for m in range(8):
                    sm.wait(pe, t_th, tg3, t_r2[7])
                    i3 = pe.matmul(pDn[:, 256 * m:256 * (m + 1)],
                                   sdws[b][:, 1024 * it + 128 * m:1024 * it + 128 * (m + 1)],
                                   tsh[:], start=(it == 0 and m % 2 == 0), stop=(it == 3),
                                   skip_group_check=True)
            t_dn = sm.inc("pe", i3)
            for m in range(8):
                sm.wait(dve, t_dn, t_r2[m])
                t_r2[m] = sm.inc("dve", dve.tensor_tensor(
                    r[m][:, cs], r[m][:, cs], pDn[:, 256 * m:256 * (m + 1)], ALU.add))
            free("pDn")
        t_sw_used[b] = t_r2[7]
    t_shared_done = t_dn

    free("tsh", "silu_sb", "sdws0", "suws0", "sgws0")

    phase_gate(8)
    sm.wait(dve, t_consts)
    t_br = sm.inc("dve", dve.memset(brow[:], 0.0))
    sm.wait(dve, t_br)
    t_brb = sm.inc("dve", dve.tensor_copy(browb[:], brow[:]))
    t_slot = None
    for tt in range(NCH):
        ms = slice(32 * tt, 32 * (tt + 1))
        pRk = psum("pRk", (128, 32))
        pBb = psum("pBb", (128, 32))
        pTt = psum("pTt", (1, 32))
        sm.wait(pe, t_mkb, t_consts, t_brb, t_r2[7], t_shared_done)
        i1 = pe.matmul(pRk[:], cst["UEXCL"][:], masksb[:, ms], start=True, stop=True)
        t_rk = sm.inc("pe", i1)
        i2 = pe.matmul(pBb[:], cst["ONESB"][0:1, 0:128],
                       browb[:], start=True, stop=True)
        t_bb = sm.inc("pe", i2)
        i3 = pe.matmul(pTt[0:1, :], cst["ONECB"][:], masksb[:, ms], start=True, stop=True)
        t_tt = sm.inc("pe", i3)
        sm.wait(dve, t_rk, t_bb, t_slot)
        t1 = sm.inc("dve", dve.tensor_copy(rks[:], pRk[:]))
        sm.wait(dve, t1)
        t2 = sm.inc("dve", dve.tensor_tensor(rks[:], rks[:], pBb[:], ALU.add))
        sm.wait(dve, t2, t_tt)
        t3 = sm.inc("dve", dve.tensor_tensor(brow[:], brow[:], pTt[0:1, :], ALU.add))
        sm.wait(dve, t3)
        t_brb = sm.inc("dve", dve.tensor_copy(browb[:], brow[:]))
        free("pRk", "pBb", "pTt")
        sm.wait(dve, t2, t_mk)
        t4 = sm.inc("dve", dve.tensor_tensor(tmp32[:], rks[:], cst["EIDX"][:], ALU.add))
        sm.wait(dve, t4)
        t5 = sm.inc("dve", dve.tensor_tensor(tmp32[:], tmp32[:], masksb[:, ms], ALU.mult))
        sm.wait(dve, t5)
        t6 = sm.inc("dve", dve.tensor_reduce(
            slotf[:, tt:tt + 1], tmp32[:], axis=AX.X, op=ALU.add))
        sm.wait(dve, t6)
        t7 = sm.inc("dve", dve.tensor_tensor(tmp32[:], rks[:], masksb[:, ms], ALU.mult))
        sm.wait(dve, t7)
        t8 = sm.inc("dve", dve.tensor_reduce(tmpc[:], tmp32[:], axis=AX.X, op=ALU.add))
        sm.wait(dve, t8)
        t9 = sm.inc("dve", dve.tensor_single_scalar(tmpc[:], tmpc[:], float(CAP - 0.5),
                                                    ALU.is_gt))
        sm.wait(dve, t6, t9)
        ta = sm.inc("dve", dve.tensor_scalar(
            tmp32[:, 0:1], slotf[:, tt:tt + 1], -1.0, float(TRASH), ALU.mult, ALU.add))
        sm.wait(dve, ta)
        tb = sm.inc("dve", dve.tensor_tensor(tmp32[:, 0:1], tmp32[:, 0:1], tmpc[:],
                                             ALU.mult))
        sm.wait(dve, tb)
        tcp = sm.inc("dve", dve.tensor_tensor(slotf[:, tt:tt + 1], slotf[:, tt:tt + 1],
                                              tmp32[:, 0:1], ALU.add))
        sm.wait(dve, tcp)
        t_slot = sm.inc("dve", dve.tensor_copy(sloti[:, tt:tt + 1], slotf[:, tt:tt + 1]))
    o = dbg_out("sloti", (128, NCH), I32)
    if o is not None:
        sm.wait(sq, t_slot)
        sm.inc("st", sq.dma_start(out=o, in_=sloti[:]), 16)

    sm.wait(dve, t_slot)
    t_z = sm.inc("dve", dve.memset(zero_sb[:], 0.0))
    t_zs = None
    sm.wait(sq, t_z)
    for blk in range(NSLOT // 128):
        t_zs = sm.inc("st", sq.dma_start(out=XSG[128 * blk:128 * (blk + 1), :],
                                         in_=zero_sb[:]), 16)
    t_zs = sm.inc("st", sq.dma_start(out=XSG[NSLOT - 16:NSLOT, :], in_=zero_sb[0:16, :]), 16)
    t_zr = sm.inc("st", sq.dma_start(out=RTD[TRASH:TRASH + 16, :], in_=zero_sb[0:16, :]), 16)

    t_xst = None
    t_scat = None
    for tt in range(NCH):
        tsl = slice(CH * tt, CH * (tt + 1))
        pX = psum("pX", (128, 128), BF16)
        t_tp = None
        for k in range(8):
            sm.wait(pe, t_xs[k], t_consts, t_xst, t_slot)
            t_tp = sm.inc("pe", pe.transpose(pX[:], xs[k][:, tsl], cst["IDENTB"][:]))
            sm.wait(act, t_tp, t_scat)
            t_xst = sm.inc("act", act.copy(xst[:, 128 * k:128 * (k + 1)], pX[:]))
        free("pX")
        sm.wait(gp, t_xst, t_zs, t_slot)
        t_scat = sm.inc("sct", gp.indirect_dma_start(
            out=XSG, out_offset=IndirectOffsetOnAxis(ap=sloti[:, tt:tt + 1], axis=0),
            in_=xst[:], in_offset=None), 16)
    t_scat_all = t_scat

    free(*[f"xs{k}" for k in range(8)])
    free(*[f"ln{k}" for k in range(8)] + ["rowR", "rowM"])

    xg = [sb(f"xg{k}", (128, E_EXP * CAP), BF16) for k in range(8)]
    t_xg = [None] * 8
    for k in range(8):
        sm.wait(sq, t_scat_all, t_r2[7], t_shared_done)
        t_xg[k] = sm.inc("ld", sq.dma_start(
            out=xg[k][:], in_=XSG[0:E_EXP * CAP, 128 * k:128 * (k + 1)],
            transpose=True), 16)

    NSL = 8
    rgws = [sb(f"rgws{b}", (128, 8 * 256), BF16) for b in range(2)]
    ruws = [sb(f"ruws{b}", (128, 8 * 256), BF16) for b in range(2)]
    rdws = [sb(f"rdws{b}", (128, 2 * 1024), BF16) for b in range(2)]
    te_sb = sb("te_sb", (128, 2 * CAP), BF16)
    sil2 = sb("sil2", (128, 2 * CAP), BF16)
    rts = sb("rts", (128, 1024), BF16)

    t_buf_used = [None, None]
    t_rts = None
    t_rt_store = None
    t_si2 = t_te = t_dn_pe = None
    nsl_ctr = 0

    for e in range(E_EXP):
        PRo = psum("PRo", (128, 1024))
        for isl in range(NSL):
            b = nsl_ctr % 2
            nsl_ctr += 1
            wsem = "wt%d" % b
            sm.wait(gp, t_buf_used[b], t_r2[7], t_shared_done, ("sct", 1))
            tg1 = sm.inc(wsem, gp.dma_start(
                out=rgws[b][:].rearrange("p (k c) -> p k c", k=8),
                in_=ins["RGW"][e, :, 256 * isl:256 * (isl + 1)].rearrange("(k p) c -> p k c", p=128)), 16)
            tg2 = sm.inc(wsem, gp.dma_start(
                out=ruws[b][:].rearrange("p (k c) -> p k c", k=8),
                in_=ins["RUW"][e, :, 256 * isl:256 * (isl + 1)].rearrange("(k p) c -> p k c", p=128)), 16)
            tg3 = sm.inc(wsem, gp.dma_start(
                out=rdws[b][:].rearrange("p (k c) -> p k c", k=2),
                in_=ins["RDW"][e, 256 * isl:256 * (isl + 1), :].rearrange("(k p) c -> p k c", p=128)), 16)
            t_te_it = [None, None]
            pGs = [psum(f"pG2{x}", (128, CAP)) for x in range(2)]
            pUs = [psum(f"pU2{x}", (128, CAP)) for x in range(2)]
            for it in range(2):
                pG, pU = pGs[it], pUs[it]
                for k in range(8):
                    sm.wait(pe, tg1, t_xg[k], t_si2)
                    i1 = pe.matmul(pG[:], rgws[b][:, 256 * k + 128 * it:256 * k + 128 * (it + 1)],
                                   xg[k][:, CAP * e:CAP * (e + 1)],
                                   start=(k == 0), stop=(k == 7))
                tgm = sm.inc("pe", i1)
                for k in range(8):
                    sm.wait(pe, tg2, t_te_it[it] or t_te)
                    i2 = pe.matmul(pU[:], ruws[b][:, 256 * k + 128 * it:256 * k + 128 * (it + 1)],
                                   xg[k][:, CAP * e:CAP * (e + 1)],
                                   start=(k == 0), stop=(k == 7))
                tum = sm.inc("pe", i2)
                sm.wait(act, tgm, t_te)
                t_si2 = sm.inc("act", act.activation(sil2[:, CAP * it:CAP * (it + 1)], pG[:], AF.Silu))
                sm.wait(dve, t_si2, tum, t_te, t_dn_pe)
                t_te = sm.inc("dve", dve.tensor_tensor(
                    te_sb[:, CAP * it:CAP * (it + 1)],
                    sil2[:, CAP * it:CAP * (it + 1)], pU[:], ALU.mult))
                t_te_it[it] = t_te
            for it in range(2):
                for nn in range(2):
                    sm.wait(pe, t_te_it[it], tg3, t_rts, t_r2[7])
                    i3 = pe.matmul(PRo[:, 512 * nn:512 * (nn + 1)],
                                   te_sb[0:128, CAP * it:CAP * (it + 1)],
                                   rdws[b][:, 1024 * it + 512 * nn:1024 * it + 512 * (nn + 1)],
                                   start=(isl == 0 and it == 0), stop=(isl == NSL - 1 and it == 1),
                                   skip_group_check=True)
            t_dn = sm.inc("pe", i3)
            free("pG20", "pG21", "pU20", "pU21")
            t_buf_used[b] = t_dn
            t_dn_pe = t_dn
        sm.wait(act, t_dn, t_rt_store)
        t_rts = sm.inc("act", act.copy(rts[:], PRo[:]))
        free("PRo")
        sm.wait(sq, t_rts, t_zr)
        t_rt_store = sm.inc("st", sq.dma_start(
            out=RTD[CAP * e:CAP * (e + 1), :], in_=rts[:]), 16)

    free(*(["rts", "sil2", "te_sb", "rdws0", "rdws1", "ruws0", "ruws1",
            "rgws0", "rgws1"] + [f"xg{k}" for k in range(8)]))

    phase_gate(9)
    t_rtt = None
    t_pX2_rd = None
    t_out = [None] * 8
    for tt in range(NCH):
        tsl = slice(CH * tt, CH * (tt + 1))
        sm.wait(gp, t_rt_store, t_slot, t_rtt)
        t_g = sm.inc("sct", gp.indirect_dma_start(
            out=rtt[:], out_offset=None,
            in_=RTD, in_offset=IndirectOffsetOnAxis(ap=sloti[:, tt:tt + 1], axis=0)), 16)
        for k in range(8):
            pX = psum("pX2", (128, 128), BF16)
            sm.wait(pe, t_g, t_consts, t_pX2_rd, t_rts, t_te)
            t_tp = sm.inc("pe", pe.transpose(pX[:], rtt[:, 128 * k:128 * (k + 1)],
                                             cst["IDENTB"][:]))
            sm.wait(dve, t_tp, t_r2[k], t_out[k])
            t_out[k] = sm.inc("dve", dve.tensor_tensor(r[k][:, tsl], r[k][:, tsl],
                                                       pX[:], ALU.add))
            t_pX2_rd = t_out[k]
            free("pX2")
        t_rtt = t_out[7]

    t_fin = None
    for k in range(8):
        sm.wait(sq, t_out[k])
        t_fin = sm.inc("st", sq.dma_start(out=OUT[128 * k:128 * (k + 1), :], in_=r[k][:]), 16)
    sm.wait(sq, t_fin)

    stack.close()
    return nc, dbg


def _noop():
    pass


def kernel(L0_attn_hidden_states, L0_attn_Wq, L0_attn_Wk, L0_attn_Wv,
           L0_attn_Wo, L0_ffn_gate_w, L0_ffn_shared_gate_w, L0_ffn_shared_up_w,
           L0_ffn_shared_down_w, L0_ffn_routed_gate_w, L0_ffn_routed_up_w,
           L0_ffn_routed_down_w):
    x = np.asarray(L0_attn_hidden_states, np.float32).reshape(T_FULL, D)
    XT = np.ascontiguousarray(x.T)
    shared = {
        "WqT": np.ascontiguousarray(np.asarray(L0_attn_Wq, np.float32).T),
        "WkT": np.ascontiguousarray(np.asarray(L0_attn_Wk, np.float32).T),
        "WvT": np.ascontiguousarray(np.asarray(L0_attn_Wv, np.float32).T),
        "WoT": np.ascontiguousarray(np.asarray(L0_attn_Wo, np.float32).T),
        "GATE": np.ascontiguousarray(np.asarray(L0_ffn_gate_w, np.float32)),
        "SGW": np.ascontiguousarray(np.asarray(L0_ffn_shared_gate_w, np.float32)),
        "SUW": np.ascontiguousarray(np.asarray(L0_ffn_shared_up_w, np.float32)),
        "SDW": np.ascontiguousarray(np.asarray(L0_ffn_shared_down_w, np.float32)),
        "RGW": np.ascontiguousarray(np.asarray(L0_ffn_routed_gate_w, np.float32)),
        "RUW": np.ascontiguousarray(np.asarray(L0_ffn_routed_up_w, np.float32)),
        "RDW": np.ascontiguousarray(np.asarray(L0_ffn_routed_down_w, np.float32)),
    }
    shared.update(CONSTS)
    in_maps = []
    for c in range(NCORES):
        t0 = NT * c
        xpre = np.zeros((D, PRE), np.float32)
        if t0 > 0:
            xpre[:, :t0] = XT[:, :t0]
        pmt = np.zeros((128, PRE // CH), np.float32)
        pmt[:, :t0 // CH] = 1.0
        m = dict(shared)
        m["XS_T"] = np.ascontiguousarray(XT[:, t0:t0 + NT])
        m["XPRE_T"] = xpre
        m["PMT"] = pmt
        in_maps.append(m)
    nc, _ = build(tuple(DEBUG_OUTS))
    res = run_bass_kernel_spmd(nc, in_maps, core_ids=list(range(NCORES)))
    out = np.concatenate(
        [np.asarray(res.results[c]["OUT"], np.float32).T for c in range(NCORES)],
        axis=0)
    return out.reshape(1, T_FULL, D)


# version 09
# speedup vs baseline: 1.6232x; 1.0394x over previous
import contextlib
import numpy as np
import ml_dtypes
import concourse.bass as bass
import concourse.mybir as mybir
from concourse.bass_utils import run_bass_kernel_spmd
from concourse.bass_core_rust import IndirectOffsetOnAxis

F32 = mybir.dt.float32
BF16 = mybir.dt.bfloat16
I32 = mybir.dt.int32
AF = mybir.ActivationFunctionType
ALU = mybir.AluOpType
AX = mybir.AxisListType
BF = ml_dtypes.bfloat16

NCORES = 8
D = 1024
T_FULL = 16384
NT = T_FULL // NCORES
NH, FEAT, HD = 16, 16, 64
CH = 128
NCH = NT // CH
PRE = T_FULL - NT
PT = 512
NPT = PRE // PT
DE = 160
EV = 65
E_EXP = 32
I_FFN = 2048
CAP = 128
TRASH = E_EXP * CAP
NSLOT = E_EXP * CAP + 16
LN_EPS = 1e-5
ATT_EPS = 1e-12

DEBUG_OUTS = []
PHASE_LIMIT = 99


def _tri_consts():
    pairs = [(i, j) for i in range(FEAT) for j in range(i, FEAT)]
    R1 = np.zeros((DE, FEAT + 1), np.float32)
    R2k = np.zeros((DE, FEAT + 1), np.float32)
    R2q = np.zeros((DE, FEAT + 1), np.float32)
    for f, (i, j) in enumerate(pairs):
        R1[f, i] = 1.0
        R2k[f, j] = 1.0
        R2q[f, j] = (1.0 / 32.0) if i == j else (1.0 / 16.0)
    for i in range(FEAT):
        R1[136 + i, i] = 1.0
        R2k[136 + i, FEAT] = 1.0
        R2q[136 + i, FEAT] = 0.25
    R1[152, FEAT] = 1.0
    R2k[152, FEAT] = 1.0
    R2q[152, FEAT] = 1.0
    return R1, R2k, R2q


def _consts():
    c = {}
    R1, R2k, R2q = _tri_consts()
    c["R1KT"] = np.ascontiguousarray(R1.T).astype(BF)
    c["R2KT"] = np.ascontiguousarray(R2k.T).astype(BF)
    c["R12KT"] = np.ascontiguousarray(np.concatenate([R1.T, R2k.T], axis=1)).astype(BF)
    c["R1QD1"] = np.ascontiguousarray(R1[0:128].T).astype(BF)
    c["R2QD1"] = np.ascontiguousarray(R2q[0:128].T).astype(BF)
    c["R1QD2"] = np.ascontiguousarray(R1[128:160].T).astype(BF)
    c["R2QD2"] = np.ascontiguousarray(R2q[128:160].T).astype(BF)
    c["IDENTF"] = np.eye(128, dtype=np.float32)
    c["IDENTB"] = np.eye(128, dtype=BF)
    mh = 0.5 * np.triu(np.ones((CH, CH), np.float32))
    c["MASKH"] = np.tile(mh, (1, NH)).astype(BF)
    c["UEXCL"] = np.triu(np.ones((CH, CH), np.float32), 1).astype(BF)
    c["EIDX"] = np.tile(np.arange(E_EXP, dtype=np.float32)[None, :] * CAP,
                        (128, 1)).astype(np.float32)
    es = np.zeros((FEAT, NH * EV), np.float32)
    for h in range(NH):
        es[h, h * EV:(h + 1) * EV] = 1.0
    c["ESEL"] = es.astype(BF)
    c["ONESB"] = np.ones((128, 256), BF)
    c["ONECB"] = np.ones((128, 1), BF)
    c["ONECF"] = np.ones((128, 1), np.float32)
    c["ONESF"] = np.ones((1, 128), np.float32)
    return c


CONSTS = _consts()


class S:
    def __init__(self, nc, stack):
        self.h = {}
        self.cnt = {}
        self.seen = {}
        for name in ("pe", "dve", "act", "gp", "ld", "gld", "st", "sct", "wt0", "wt1"):
            self.h[name] = stack.enter_context(nc.semaphore(name))
            self.cnt[name] = 0

    def inc(self, name, inst, n=1):
        inst.then_inc(self.h[name], n)
        self.cnt[name] += n
        return (name, self.cnt[name])

    DMA_SEMS = ("ld", "gld", "st", "sct", "wt0", "wt1")

    def wait(self, eng, *toks):
        seen = {}
        for t in toks:
            if t is None:
                continue
            name, v = t
            if name in self.DMA_SEMS:
                v = self.cnt[name]
            if v > 0:
                seen[name] = max(seen.get(name, 0), v)
        done = self.seen.setdefault(id(eng), {})
        for name, v in seen.items():
            if done.get(name, -1) >= v:
                continue
            done[name] = v
            eng.wait_ge(self.h[name], v)


class _EarlyFinishDone(Exception):
    pass


def build(debug_outs=()):
    try:
        return _build(debug_outs)
    except _EarlyFinishDone as ex:
        return ex.args[0]


def _build(debug_outs=()):
    nc = bass.Bass("TRN2", target_bir_lowering=False)
    stack = contextlib.ExitStack()

    def dram_in(name, shape, dtype=F32):
        return nc.dram_tensor(name, list(shape), dtype, kind="ExternalInput").ap()

    ins = {}
    ins["XS_T"] = dram_in("XS_T", (D, NT))
    ins["XPRE_T"] = dram_in("XPRE_T", (D, PRE))
    ins["PMT"] = dram_in("PMT", (128, PRE // CH))
    for w, shp in (("WqT", (D, 256)), ("WkT", (D, 256)), ("WvT", (D, D)),
                   ("WoT", (D, D)), ("GATE", (D, E_EXP)),
                   ("SGW", (D, I_FFN)), ("SUW", (D, I_FFN)), ("SDW", (I_FFN, D)),
                   ("RGW", (E_EXP, D, I_FFN)), ("RUW", (E_EXP, D, I_FFN)),
                   ("RDW", (E_EXP, I_FFN, D))):
        ins[w] = dram_in(w, shp)
    for k, v in CONSTS.items():
        ins[k] = dram_in(k, v.shape, BF16 if v.dtype == BF else F32)

    OUT = nc.dram_tensor("OUT", [D, NT], F32, kind="ExternalOutput").ap()
    dbg = {}

    def dbg_out(name, shape, dtype=F32):
        if name in debug_outs:
            dbg[name] = nc.dram_tensor("dbg_" + name, list(shape), dtype,
                                       kind="ExternalOutput").ap()
        return dbg.get(name)

    KBOWN = nc.dram_tensor("KBOWN", [2, FEAT + 1, NH, NT], BF16, kind="Internal").ap()
    KBPRE = nc.dram_tensor("KBPRE", [2, FEAT + 1, NH, PT], BF16, kind="Internal").ap()
    ROWB = nc.dram_tensor("ROWB", [NT], F32, kind="Internal").ap()
    XSG = nc.dram_tensor("XSG", [NSLOT, D], BF16, kind="Internal").ap()
    RTD = nc.dram_tensor("RTD", [NSLOT, D], BF16, kind="Internal").ap()

    pe, dve, act, gp, sq = nc.tensor, nc.vector, nc.scalar, nc.gpsimd, nc.sync
    sm = S(nc, stack)

    def bar():
        return (("pe", sm.cnt["pe"]), ("dve", sm.cnt["dve"]),
                ("act", sm.cnt["act"]), ("gp", sm.cnt["gp"]), ("st", 1),
                ("ld", 1), ("gld", 1))

    def phase_gate(n):
        if PHASE_LIMIT > n:
            return
        for name in ("pe", "dve", "act", "gp", "ld", "gld", "st", "sct", "wt0", "wt1"):
            if sm.cnt[name] > 0:
                sq.wait_ge(sm.h[name], sm.cnt[name])
        t_fin = None
        for k in range(8):
            t_fin = sm.inc("st", sq.dma_start(
                out=OUT[128 * k:128 * (k + 1), :], in_=r[k][:]), 16)
        sm.wait(sq, t_fin)
        stack.close()
        raise _EarlyFinishDone((nc, dbg))
    frees = {}
    _seq = [0]

    def _uname(name):
        _seq[0] += 1
        return f"{name}_{_seq[0]}"

    def sb(name, shape, dtype):
        assert name not in frees, name
        cm = nc.sbuf_tensor(_uname(name), list(shape), dtype)
        t = cm.__enter__()
        frees[name] = cm
        return t

    def psum(name, shape, dtype=F32):
        assert name not in frees, name
        cm = nc.psum_tensor(_uname(name), list(shape), dtype)
        t = cm.__enter__()
        frees[name] = cm
        return t

    def free(*names):
        order = list(frees.keys())
        for n in sorted(names, key=lambda n: -order.index(n)):
            frees.pop(n).__exit__(None, None, None)

    r = [sb(f"r{k}", (128, NT), F32) for k in range(8)]
    cst = {}
    for k, v in CONSTS.items():
        cst[k] = sb("c_" + k, v.shape, BF16 if v.dtype == BF else F32)
    gate = sb("gate", (128, 8 * E_EXP), F32)
    pmt = sb("pmt", (128, PRE // CH), F32)

    t_r_in = []
    for k in range(8):
        t_r_in.append(sm.inc("ld", sq.dma_start(
            out=r[k][:], in_=ins["XS_T"][128 * k:128 * (k + 1), :]), 16))
    for k in CONSTS:
        sm.inc("ld", sq.dma_start(out=cst[k][:], in_=ins[k]), 16)
    sm.inc("ld", sq.dma_start(out=pmt[:], in_=ins["PMT"]), 16)
    t_gate = sm.inc("ld", sq.dma_start(
        out=gate[:].rearrange("p (k c) -> p k c", k=8),
        in_=ins["GATE"].rearrange("(k p) c -> p k c", p=128)), 16)
    t_consts = ("ld", sm.cnt["ld"])

    def wload(dst, src, ktiles=8):
        return sm.inc("gld", gp.dma_start(
            out=dst[:].rearrange("p (k c) -> p k c", k=ktiles),
            in_=src.rearrange("(k p) c -> p k c", p=128)), 16)


    v_sb = sb("v_sb", (128, NCH * NH * EV), BF16)
    S1 = sb("S1", (128, NH * EV), F32)
    S2 = sb("S2", (128, 4 * EV), F32)
    wk = sb("wk", (128, 8 * 256), BF16)
    wv = sb("wv", (128, 8 * 1024), BF16)
    wq = sb("wq", (128, 8 * 256), BF16)
    t_wq = wload(wq, ins["WqT"])
    t_wk = wload(wk, ins["WkT"])
    t_wv = wload(wv, ins["WvT"])

    ln_bf = [sb(f"ln{k}", (128, NT), BF16) for k in range(8)]

    def ln_exact(tag, t_src, rowM, rowR):
        sqs = sb("sqs" + tag, (128, 512), F32)
        rowS = sb("rowS" + tag, (1, NT), F32)
        rowMb = sb("rowMb" + tag, (1, NT), BF16)
        rowRb = sb("rowRb" + tag, (1, NT), BF16)
        last = None
        t_sqs = None
        for ct in range(NT // 512):
            cs = slice(512 * ct, 512 * (ct + 1))
            pA = psum(f"{tag}A", (1, 512))
            pB = psum(f"{tag}B", (1, 512))
            for k in range(8):
                sm.wait(pe, t_src[k], t_consts, last)
                i1 = pe.matmul(pA[0:1, :], cst["ONECF"][:], r[k][:, cs],
                               start=(k == 0), stop=(k == 7))
                sm.wait(act, t_src[k], t_sqs)
                t2 = sm.inc("act", act.activation(sqs[:], r[k][:, cs], AF.Square))
                sm.wait(pe, t2)
                i3 = pe.matmul(pB[0:1, :], cst["ONECF"][:], sqs[:],
                               start=(k == 0), stop=(k == 7))
                t_sqs = sm.inc("pe", i3)
            t_mm = sm.inc("pe", i1)
            sm.wait(dve, t_mm)
            t1 = sm.inc("dve", dve.tensor_scalar_mul(rowM[0:1, cs], pA[0:1, :], 1.0 / D))
            sm.wait(dve, t_sqs, t1)
            t2 = sm.inc("dve", dve.tensor_scalar_mul(rowS[0:1, cs], pB[0:1, :], 1.0 / D))
            sm.wait(dve, t1)
            t3 = sm.inc("dve", dve.tensor_tensor(rowR[0:1, cs], rowM[0:1, cs],
                                                 rowM[0:1, cs], ALU.mult))
            sm.wait(dve, t2, t3)
            t4 = sm.inc("dve", dve.tensor_tensor(rowS[0:1, cs], rowS[0:1, cs],
                                                 rowR[0:1, cs], ALU.subtract))
            sm.wait(dve, t4)
            t5 = sm.inc("dve", dve.tensor_scalar_add(rowS[0:1, cs], rowS[0:1, cs], LN_EPS))
            sm.wait(dve, t5)
            t6 = sm.inc("dve", dve.reciprocal(rowS[0:1, cs], rowS[0:1, cs]))
            sm.wait(act, t6)
            t7 = sm.inc("act", act.activation(rowR[0:1, cs], rowS[0:1, cs], AF.Sqrt))
            sm.wait(dve, t1, t7)
            t8 = sm.inc("dve", dve.tensor_copy(rowMb[0:1, cs], rowM[0:1, cs]))
            sm.wait(dve, t8)
            t9 = sm.inc("dve", dve.tensor_copy(rowRb[0:1, cs], rowR[0:1, cs]))
            free(f"{tag}A", f"{tag}B")
            pM = psum(f"{tag}M", (128, 512))
            pR = psum(f"{tag}R", (128, 512))
            sm.wait(pe, t8)
            t_mb = sm.inc("pe", pe.matmul(pM[:], cst["ONESB"][0:1, 0:128], rowMb[0:1, cs],
                                          start=True, stop=True))
            sm.wait(pe, t9)
            t_rb = sm.inc("pe", pe.matmul(pR[:], cst["ONESB"][0:1, 0:128], rowRb[0:1, cs],
                                          start=True, stop=True))
            for k in range(8):
                sm.wait(dve, t_mb, t_rb, t_sqs)
                ta = sm.inc("dve", dve.tensor_tensor(sqs[:], r[k][:, cs], pM[:],
                                                     ALU.subtract))
                t_sqs = ta
                sm.wait(dve, ta)
                last = sm.inc("dve", dve.tensor_tensor(ln_bf[k][:, cs], sqs[:], pR[:],
                                                       ALU.mult))
                t_sqs = last
            free(f"{tag}M", f"{tag}R")
        free("rowRb" + tag, "rowMb" + tag, "rowS" + tag, "sqs" + tag)
        return last

    rowM1 = sb("rowM1", (1, NT), F32)
    rowR1 = sb("rowR1", (1, NT), F32)
    t_ln1 = ln_exact("l1", t_r_in, rowM1, rowR1)
    o = dbg_out("ln1", (128, 8 * NT), BF16)
    if o is not None:
        sm.wait(sq, t_ln1)
        for k in range(8):
            sm.inc("st", sq.dma_start(out=o[:, NT * k:NT * (k + 1)], in_=ln_bf[k][:]), 16)
    phase_gate(1)

    qknat = sb("qknat", (128, NT), BF16)

    t_kb = []
    tg = None
    for iqk, wmat, t_w in ((0, wq, t_wq), (1, wk, t_wk)):
        for grp in range(2):
            for ct in range(NT // 512):
                cs = slice(512 * ct, 512 * (ct + 1))
                pg = psum("pj", (128, 512))
                for k in range(8):
                    sm.wait(pe, t_w, t_ln1, tg)
                    ii = pe.matmul(pg[:], wmat[:, 256 * k + 128 * grp:256 * k + 128 * (grp + 1)],
                                   ln_bf[k][:, cs], start=(k == 0), stop=(k == 7))
                t_mm = sm.inc("pe", ii)
                sm.wait(act, t_mm, tg, t_kb[-1] if t_kb else None)
                tg = sm.inc("act", act.copy(qknat[:, cs], pg[:]))
                free("pj")
            sm.wait(sq, tg)
            t_kb.append(sm.inc("st", sq.dma_start(
                out=KBOWN[iqk, 0:FEAT, 8 * grp:8 * (grp + 1), :].rearrange("f h t -> h f t"),
                in_=qknat[:]), 16))
    for iqk in range(2):
        sm.wait(sq, t_consts)
        t_kb.append(sm.inc("st", sq.dma_start(
            out=KBOWN[iqk, FEAT, :, :].rearrange("h t -> (h t)").rearrange("(a b) -> a b", a=128),
            in_=cst["ONESB"][:, 0:256]), 16))
    t_kb_all = t_kb[-1]
    free("qknat")

    t_vown = tg
    for tt in range(NCH):
        ts_ = slice(128 * tt, 128 * (tt + 1))
        for half in range(2):
            pv = psum("pv", (128, 512))
            for k in range(8):
                sm.wait(pe, t_wv, t_ln1, t_vown)
                ii = pe.matmul(pv[:], ln_bf[k][:, ts_],
                               wv[:, 1024 * k + 512 * half:1024 * k + 512 * (half + 1)],
                               start=(k == 0), stop=(k == 7))
            t_mm = sm.inc("pe", ii)
            sm.wait(act, t_mm, t_vown)
            t_vown = sm.inc("act", act.copy(
                v_sb[:].rearrange("p (t h e) -> p t h e", t=NCH, h=NH)[:, tt, 8 * half:8 * (half + 1), 0:64],
                pv[:].rearrange("p (h e) -> p h e", h=8)))
            free("pv")
    sm.wait(dve, t_vown)
    t_vones = sm.inc("dve", dve.memset(
        v_sb[:].rearrange("p (t h e) -> p t h e", t=NCH, h=NH)[:, :, :, 64:65], 1.0))
    o = dbg_out("vsb", (128, NCH * NH * EV), BF16)
    if o is not None:
        sm.wait(sq, t_vones)
        sm.inc("st", sq.dma_start(out=o, in_=v_sb[:]), 16)
    phase_gate(2)
    free(*[f"ln{k}" for k in range(8)] + ["rowR1", "rowM1", "wq"])

    t_sz1 = sm.inc("dve", dve.memset(S1[:], 0.0))
    t_sz2 = sm.inc("dve", dve.memset(S2[:], 0.0))

    xbf = [[sb(f"xbf{p}_{k}", (128, PT), BF16) for k in range(8)] for p in range(2)]
    xh = [[sb(f"xh{p}_{k}", (128, PT), BF16) for k in range(8)] for p in range(2)]
    sqb = sb("sqb", (128, PT), BF16)
    prowM = sb("prowM", (1, PT), F32)
    prowR = sb("prowR", (1, PT), F32)
    prowS = sb("prowS", (1, PT), F32)
    prowMb = sb("prowMb", (1, PT), BF16)
    prowRb = sb("prowRb", (1, PT), BF16)
    knat = sb("knat", (128, 2 * PT), BF16)
    ktp = sb("ktp", (FEAT + 1, NH * PT), BF16)
    kf = sb("kf", (128, 2048), BF16)
    rep_sb = sb("rep_sb", (128, 4 * DE), BF16)
    vpre = sb("vpre", (128, 4 * NH * EV), BF16)
    pmb = sb("pmb", (128, 4), F32)

    t_pm = None
    t_pm_rd = None
    t_row_rd = None
    t_lnrow_prev = None
    t_rep_act = None
    t_kf_last = [None]
    t_xbf = [None] * 8
    t_xh = [None] * 8
    t_xh_war = [[None] * 8, [None] * 8]
    t_knat = None
    t_ktp_w = [None, None]
    t_ktp_r = [None, None]
    t_kf = None
    t_vpre = None
    t_state = None
    t_u1p = t_u2p = None

    t_pre_ones = None
    for ib in range(2):
        for hhalf in range(2):
            sm.wait(sq, t_consts)
            t_pre_ones = sm.inc("st", sq.dma_start(
                out=KBPRE[ib, FEAT, 8 * hhalf:8 * (hhalf + 1), :]
                    .rearrange("h t -> (h t)").rearrange("(a b) -> a b", a=16),
                in_=cst["ONESB"][0:16, 0:256]), 16)

    t_bar3 = bar()
    t_rep_rd = None
    for j in range(NPT):
        js = slice(PT * j, PT * (j + 1))
        buf = j % 2
        xb, xhh = xbf[buf], xh[buf]
        t_xb_prev = t_xbf[buf * 0:8] if False else None
        for k in range(8):
            sm.wait(gp, t_xh_war[buf][k], *(t_bar3 if j == 0 else ()))
            t_xbf[k] = sm.inc("wt%d" % buf, gp.dma_start(
                out=xb[k][:], in_=ins["XPRE_T"][128 * k:128 * (k + 1), js]), 16)
        sm.wait(dve, t_consts, t_pm_rd, *(t_bar3 if j == 0 else ()))
        t_pm = sm.inc("dve", dve.tensor_copy(pmb[:], pmt[:, 4 * j:4 * (j + 1)]))
        pA = psum("ppA", (1, PT))
        pB = psum("ppB", (1, PT))
        t_sqb = None
        for k in range(8):
            sm.wait(pe, t_xbf[k], t_consts, t_lnrow_prev, t_vown, t_row_rd, t_u2p)
            i1 = pe.matmul(pA[0:1, :], cst["ONECB"][:], xb[k][:],
                           start=(k == 0), stop=(k == 7))
            sm.wait(act, t_xbf[k], t_sqb)
            t2 = sm.inc("act", act.activation(sqb[:], xb[k][:], AF.Square))
            sm.wait(pe, t2)
            i3 = pe.matmul(pB[0:1, :], cst["ONECB"][:], sqb[:],
                           start=(k == 0), stop=(k == 7))
            t_sqb = sm.inc("pe", i3)
        t_mm = sm.inc("pe", i1)
        sm.wait(dve, t_mm)
        t1 = sm.inc("dve", dve.tensor_scalar_mul(prowM[0:1, :], pA[0:1, :], 1.0 / D))
        sm.wait(dve, t_sqb, t1)
        t2 = sm.inc("dve", dve.tensor_scalar_mul(prowS[0:1, :], pB[0:1, :], 1.0 / D))
        t_lnrow_prev = t2
        sm.wait(dve, t1)
        t3 = sm.inc("dve", dve.tensor_tensor(prowR[0:1, :], prowM[0:1, :],
                                             prowM[0:1, :], ALU.mult))
        sm.wait(dve, t2, t3)
        t4 = sm.inc("dve", dve.tensor_tensor(prowS[0:1, :], prowS[0:1, :],
                                             prowR[0:1, :], ALU.subtract))
        sm.wait(dve, t4)
        t5 = sm.inc("dve", dve.tensor_scalar_add(prowS[0:1, :], prowS[0:1, :], LN_EPS))
        sm.wait(dve, t5)
        t6 = sm.inc("dve", dve.reciprocal(prowS[0:1, :], prowS[0:1, :]))
        sm.wait(act, t6)
        t7 = sm.inc("act", act.activation(prowR[0:1, :], prowS[0:1, :], AF.Sqrt))
        sm.wait(dve, t1, t7)
        t8 = sm.inc("dve", dve.tensor_copy(prowMb[0:1, :], prowM[0:1, :]))
        sm.wait(dve, t8)
        t9 = sm.inc("dve", dve.tensor_copy(prowRb[0:1, :], prowR[0:1, :]))
        free("ppA", "ppB")
        pM = psum("ppM", (128, PT))
        pR = psum("ppR", (128, PT))
        sm.wait(pe, t8)
        t_mb = sm.inc("pe", pe.matmul(pM[:], cst["ONESB"][0:1, 0:128], prowMb[0:1, :],
                                      start=True, stop=True))
        sm.wait(pe, t9)
        t_rb = sm.inc("pe", pe.matmul(pR[:], cst["ONESB"][0:1, 0:128], prowRb[0:1, :],
                                      start=True, stop=True))
        t_row_rd = t_rb
        t_xh_new = [None] * 8
        for k in range(8):
            sm.wait(dve, t_mb, t_rb, t_xbf[k], t_knat, t_xh_war[buf][k])
            ta = sm.inc("dve", dve.tensor_tensor(xb[k][:], xb[k][:], pM[:],
                                                 ALU.subtract))
            sm.wait(dve, ta)
            t_xh_new[k] = sm.inc("dve", dve.tensor_tensor(xhh[k][:], xb[k][:], pR[:],
                                                          ALU.mult))
        t_xh = t_xh_new
        free("ppM", "ppR")
        pK = psum("ppK", (128, PT))
        tk_ev = t_knat
        for grp in range(2):
            for k in range(8):
                sm.wait(pe, t_wk, t_xh[k], t_xh[7], tk_ev)
                ii = pe.matmul(pK[:], wk[:, 256 * k + 128 * grp:256 * k + 128 * (grp + 1)],
                               xhh[k][:], start=(k == 0), stop=(k == 7))
            t_mm = sm.inc("pe", ii)
            sm.wait(act, t_mm, tk_ev)
            tk_ev = sm.inc("act", act.copy(knat[:, PT * grp:PT * (grp + 1)], pK[:]))
        t_knat = tk_ev
        free("ppK")
        sm.wait(sq, t_knat, t_ktp_r[buf], t_pre_ones)
        tw = sm.inc("st", sq.dma_start(
            out=KBPRE[buf, 0:FEAT, :, :].rearrange("f (g h) t -> h f g t", g=2),
            in_=knat[:].rearrange("p (g t) -> p g t", g=2)), 16)
        t_ktp_w[buf] = tw
        sm.wait(sq, tw)
        tr = sm.inc("ld", sq.dma_start(
            out=ktp[:].rearrange("f (h t) -> f h t", h=NH),
            in_=KBPRE[buf, :, :, :]), 16)
        t_ktp_r[buf] = tr
        t_vev = t_vown
        for cc in range(4):
            tsl = slice(128 * cc, 128 * (cc + 1))
            pV = psum("ppV", (128, 512))
            for half in range(2):
                for k in range(8):
                    sm.wait(pe, t_wv, t_xh[k], t_vev, t_knat, t_state)
                    ii = pe.matmul(pV[:], xhh[k][:, tsl],
                                   wv[:, 1024 * k + 512 * half:1024 * k + 512 * (half + 1)],
                                   start=(k == 0), stop=(k == 7))
                t_mm = sm.inc("pe", ii)
                sm.wait(act, t_mm, t_vev)
                t_vev = sm.inc("act", act.copy(
                    vpre[:].rearrange("p (c h e) -> p c h e", c=4, h=NH)
                        [:, cc, 8 * half:8 * (half + 1), 0:64],
                    pV[:].rearrange("p (h e) -> p h e", h=8)))
            free("ppV")
            sm.wait(dve, t_vev, t_pm)
            t_vm = sm.inc("dve", dve.tensor_scalar_mul(
                vpre[:].rearrange("p (c h e) -> p c h e", c=4, h=NH)[:, cc, :, 64:65]
                    .rearrange("p h o -> p (h o)"),
                cst["ONESB"][:, 0:16], pmb[:, cc:cc + 1]))
        t_vpre = t_vm
        t_pm_rd = t_vm
        PSS1p = psum("PSS1p", (128, 1024))
        PSS2p = psum("PSS2p", (128, 512))
        pR1 = [psum("ppR1a", (128, 1024)), psum("ppR1b", (128, 1024))]
        t_kf_h = [None] * NH
        kf2 = kf

        def emit_sub(h, sub):
            p = h % 2
            pR = pR1[p]
            for tt2 in range(2):
                cc = 2 * sub + tt2
                lhs = ktp[:, PT * h + 128 * cc:PT * h + 128 * (cc + 1)]
                sm.wait(pe, tr, t_consts, t_subrd[p][sub],
                        t_kfd[0] if sub == 1 else None)
                i1 = pe.matmul(pR[:, 512 * tt2:512 * tt2 + 2 * DE], lhs,
                               cst["R12KT"][:], start=True, stop=True)
            t_rep = sm.inc("pe", i1)
            sm.wait(act, t_rep, t_repact[0])
            te = sm.inc("act", act.copy(
                rep_sb[:].rearrange("p (q a f) -> p q a f", q=2, a=2)[:, p],
                pR[:].rearrange("p (a f) -> p a f", a=2)[:, :, 0:DE]))
            t_repact[0] = te
            sm.wait(dve, te, t_kfd[0])
            t_kfd[0] = sm.inc("dve", dve.tensor_tensor(
                kf2[:].rearrange("p (q c f) -> p q c f", q=2, c=4)
                    [:, p, 2 * sub:2 * sub + 2, 0:DE],
                pR[:].rearrange("p (a f) -> p a f", a=2)[:, :, DE:2 * DE],
                rep_sb[:].rearrange("p (q a f) -> p q a f", q=2, a=2)[:, p],
                ALU.mult))
            t_subrd[p][sub] = t_kfd[0]
            if sub == 1:
                t_kf_h[h] = t_kfd[0]

        def emit_states(h, part):
            p = h % 2
            hg = h % 8
            last = None
            for cc in range(4):
                sm.wait(pe, t_kf_h[h], t_vpre, t_u1p, t_u2p)
                if part == 0:
                    last = pe.matmul(PSS1p[:, 128 * hg:128 * hg + EV],
                                     kf2[:, 1024 * p + 256 * cc:1024 * p + 256 * cc + 128],
                                     vpre[:, 1040 * cc + EV * h:1040 * cc + EV * (h + 1)],
                                     start=(cc == 0), stop=(cc == 3),
                                     skip_group_check=True)
                else:
                    last = pe.matmul(PSS2p[32 * (h % 4):32 * (h % 4) + 32,
                                          EV * (h // 4):EV * (h // 4) + EV],
                                     kf2[:, 1024 * p + 256 * cc + 128:1024 * p + 256 * cc + DE],
                                     vpre[:, 1040 * cc + EV * h:1040 * cc + EV * (h + 1)],
                                     start=(cc == 0), stop=(cc == 3),
                                     tile_position=(0, 32 * (h % 4)),
                                     skip_group_check=True)
            return sm.inc("pe", last)

        t_subrd = [[t_rep_rd, t_rep_rd], [t_rep_rd, t_rep_rd]]
        t_repact = [t_rep_act]
        t_kfd = [t_kf_last[0]]
        for h in range(NH + 1):
            if h < NH:
                emit_sub(h, 0)
            if h >= 1:
                emit_states(h - 1, 0)
            if h < NH:
                emit_sub(h, 1)
            if h >= 1:
                t_state = emit_states(h - 1, 1)
            if h == 8 or h == NH:
                sm.wait(dve, t_state, t_sz1, t_sz2, t_u1p, t_u2p)
                half = 0 if h == 8 else 1
                t_u1p = sm.inc("dve", dve.tensor_tensor(
                    S1[:].rearrange("p (g e) -> p g e", g=NH)[:, 8 * half:8 * half + 8],
                    S1[:].rearrange("p (g e) -> p g e", g=NH)[:, 8 * half:8 * half + 8],
                    PSS1p[:].rearrange("p (g e) -> p g e", g=8)[:, :, 0:EV], ALU.add))
                if h == NH:
                    sm.wait(dve, t_u1p)
                    t_u2p = sm.inc("dve", dve.tensor_tensor(S2[:], S2[:],
                                                            PSS2p[:, 0:4 * EV], ALU.add))
        t_rep_rd = t_subrd[1][1]
        t_rep_act = t_repact[0]
        t_kf_last[0] = t_kfd[0]
        free("ppR1a", "ppR1b", "PSS1p", "PSS2p")
        for k in range(8):
            t_xh_war[buf][k] = t_state

    free(*([f"xbf{p}_{k}" for p in range(2) for k in range(8)] +
           [f"xh{p}_{k}" for p in range(2) for k in range(8)] +
           ["sqb", "prowM", "prowR", "prowS", "prowMb", "prowRb", "knat", "ktp",
            "kf", "rep_sb", "vpre", "pmb", "wv", "wk"]))
    phase_gate(3)

    S1b = sb("S1b", (128, NH * EV), BF16)
    S2b = sb("S2b", (128, 4 * EV), BF16)
    t_s1 = t_u1p
    t_s2 = t_u2p
    sm.wait(act, t_u1p, t_u2p, t_sz1, t_sz2)
    t_s1b = sm.inc("act", act.copy(S1b[:], S1[:]))
    t_s2b = sm.inc("act", act.copy(S2b[:], S2[:]))
    o = dbg_out("s1", (128, NH * EV), F32)
    if o is not None:
        sm.wait(sq, t_s1)
        sm.inc("st", sq.dma_start(out=o, in_=S1[:]), 16)

    y_sb = [sb(f"y{k}", (128, NT), BF16) for k in range(8)]
    qt = sb("qt", (FEAT + 1, NH * CH), BF16)
    kt = sb("kt", (FEAT + 1, NH * CH), BF16)
    qf1 = sb("qf1", (128, NH * CH), BF16)
    qf2 = sb("qf2", (128, 4 * CH), BF16)
    kfo = sb("kfo", (128, NH * DE), BF16)
    repo = sb("repo", (128, 4 * DE), BF16)
    qrep = sb("qrep", (128, 512), BF16)
    sqf = sb("sqf", (128, NH * CH), BF16)
    at = sb("at", (128, NH * CH), BF16)
    nums = sb("nums", (EV, NH * CH), BF16)
    dens = sb("dens", (128, NH), F32)
    recs = sb("recs", (128, NH), F32)
    recT = sb("recT", (16, CH), BF16)

    t_qt = t_kt = None
    t_kfo = t_qf = t_at = t_num_ev = t_y = t_recT = None
    t_u1 = t_u2 = None
    t_pss = t_dmm = t_bc = t_tr = None
    t_bar4 = bar()

    for c in range(NCH):
        tsl = slice(CH * c, CH * (c + 1))
        sm.wait(sq, t_kb_all, t_qt, t_dmm, *(t_bar4 if c == 0 else ()))
        t_qt = sm.inc("ld", sq.dma_start(
            out=qt[:].rearrange("f (h t) -> f h t", h=NH),
            in_=KBOWN[0, :, :, tsl]), 16)
        sm.wait(sq, t_kt, t_dmm)
        t_kt = sm.inc("ld", sq.dma_start(
            out=kt[:].rearrange("f (h t) -> f h t", h=NH),
            in_=KBOWN[1, :, :, tsl]), 16)

        pRc = psum("scRc", (128, 2048))
        t_kfo_new = t_kfo
        for rnd in range(4):
            for hh in range(4):
                h = 4 * rnd + hh
                lhs = kt[:, CH * h:CH * (h + 1)]
                sm.wait(pe, t_kt, t_consts, t_kfo_new, t_num_ev, t_y)
                i1 = pe.matmul(pRc[:, 512 * hh:512 * hh + 2 * DE], lhs,
                               cst["R12KT"][:], start=True, stop=True)
            t_rep = sm.inc("pe", i1)
            sm.wait(act, t_rep)
            te = sm.inc("act", act.copy(
                repo[:].rearrange("p (a f) -> p a f", a=4),
                pRc[:].rearrange("p (a f) -> p a f", a=4)[:, :, 0:DE]))
            sm.wait(dve, te, t_pss)
            t_kfo_new = sm.inc("dve", dve.tensor_tensor(
                kfo[:].rearrange("p (h f) -> p h f", h=NH)[:, 4 * rnd:4 * rnd + 4, :],
                pRc[:].rearrange("p (a f) -> p a f", a=4)[:, :, DE:2 * DE],
                repo[:].rearrange("p (a f) -> p a f", a=4), ALU.mult))
        t_kfo = t_kfo_new
        free("scRc")

        PSS1 = psum("PSS1", (128, 2048))
        PSS2 = psum("PSS2", (128, 512))
        for h in range(NH):
            sm.wait(pe, t_kfo, t_vones, t_u1, t_u2, t_num_ev, t_y)
            i1 = pe.matmul(PSS1[:, 128 * h:128 * h + EV],
                           kfo[:, DE * h:DE * h + 128],
                           v_sb[:, 1040 * c + EV * h:1040 * c + EV * (h + 1)],
                           start=True, stop=True)
            i2 = pe.matmul(PSS2[32 * (h % 4):32 * (h % 4) + 32,
                                EV * (h // 4):EV * (h // 4) + EV],
                           kfo[:, DE * h + 128:DE * h + DE],
                           v_sb[:, 1040 * c + EV * h:1040 * c + EV * (h + 1)],
                           start=True, stop=True, tile_position=(0, 32 * (h % 4)))
        t_pss = sm.inc("pe", i2)

        pQ1 = psum("scQ1", (128, 512))
        pQ2 = psum("scQ2", (128, 512))
        t_qf_new = t_qf
        for g in range(4):
            rhs = qt[:, 512 * g:512 * (g + 1)]
            sm.wait(pe, t_qt, t_consts, t_qf_new, t_num_ev, t_y)
            i1 = pe.matmul(pQ1[:], cst["R1QD1"][:], rhs, start=True, stop=True)
            i2 = pe.matmul(pQ2[:], cst["R2QD1"][:], rhs, start=True, stop=True)
            t_rep = sm.inc("pe", i2)
            sm.wait(act, t_rep)
            te = sm.inc("act", act.copy(qrep[:], pQ1[:]))
            sm.wait(dve, te)
            t_qf_new = sm.inc("dve", dve.tensor_tensor(
                qf1[:, 512 * g:512 * (g + 1)], pQ2[:], qrep[:], ALU.mult))
        for h in range(NH):
            rhs = qt[:, CH * h:CH * (h + 1)]
            jj = h % 4
            sm.wait(pe, t_qt, t_consts, t_qf_new)
            i1 = pe.matmul(pQ1[32 * jj:32 * jj + 32, 128 * (h // 4):128 * (h // 4) + 128],
                           cst["R1QD2"][:], rhs, start=True, stop=True,
                           tile_position=(0, 32 * jj))
            i2 = pe.matmul(pQ2[32 * jj:32 * jj + 32, 128 * (h // 4):128 * (h // 4) + 128],
                           cst["R2QD2"][:], rhs, start=True, stop=True,
                           tile_position=(0, 32 * jj))
        t_rep = sm.inc("pe", i2)
        sm.wait(act, t_rep)
        te = sm.inc("act", act.copy(qrep[:], pQ1[:]))
        sm.wait(dve, te)
        t_qf = sm.inc("dve", dve.tensor_tensor(qf2[:], pQ2[:], qrep[:], ALU.mult))
        free("scQ1", "scQ2")

        sm.wait(dve, t_pss, t_s1, t_s2, t_s1b, t_s2b)
        t_u1 = sm.inc("dve", dve.tensor_tensor(
            S1[:], S1[:],
            PSS1[:].rearrange("p (h e) -> p h e", h=NH)[:, :, 0:EV], ALU.add))
        sm.wait(dve, t_u1)
        t_u2 = sm.inc("dve", dve.tensor_tensor(S2[:], S2[:], PSS2[:, 0:4 * EV], ALU.add))
        free("PSS1", "PSS2")

        PS_S = psum("PS_S", (128, 2048))
        for h in range(NH):
            sm.wait(pe, t_qt, t_kt, t_at, t_u2)
            ii = pe.matmul(PS_S[:, CH * h:CH * (h + 1)],
                           kt[0:FEAT, CH * h:CH * (h + 1)],
                           qt[0:FEAT, CH * h:CH * (h + 1)], start=True, stop=True)
        t_smm = sm.inc("pe", ii)
        sm.wait(act, t_smm, t_at)
        t_sq = sm.inc("act", act.activation(sqf[:], PS_S[:], AF.Square,
                                            bias=1.0, scale=0.25))
        sm.wait(dve, t_sq, t_consts, t_dmm)
        t_at = sm.inc("dve", dve.scalar_tensor_tensor(
            at[:], sqf[:], 1.0, cst["MASKH"][:], ALU.add, ALU.mult))
        free("PS_S")

        PS_N = psum("PS_N", (EV, 2048))
        for h in range(NH):
            hs = slice(CH * h, CH * (h + 1))
            jj = h % 4
            sm.wait(pe, t_at, t_vones, t_s1b, t_s2b, t_qf, t_num_ev)
            i1 = pe.matmul(PS_N[:, hs],
                           v_sb[:, 1040 * c + EV * h:1040 * c + EV * (h + 1)],
                           at[:, hs], start=True, stop=False)
            i2 = pe.matmul(PS_N[:, hs], S1b[:, EV * h:EV * (h + 1)],
                           qf1[:, hs], start=False, stop=False)
            i3 = pe.matmul(PS_N[:, hs],
                           S2b[32 * jj:32 * jj + 32, EV * (h // 4):EV * (h // 4) + EV],
                           qf2[32 * jj:32 * jj + 32, 128 * (h // 4):128 * (h // 4) + 128],
                           start=False, stop=True, tile_position=(32 * jj, 0))
        t_num = sm.inc("pe", i3)

        PS_D = psum("PS_D", (128, 16))
        for h in range(NH):
            hs = slice(CH * h, CH * (h + 1))
            jj = h % 4
            sm.wait(pe, t_at, t_s1b, t_s2b, t_qf, t_recT)
            i1 = pe.matmul(PS_D[:, h:h + 1], at[:, hs], cst["ONECB"][:],
                           start=True, stop=False)
            i2 = pe.matmul(PS_D[:, h:h + 1], qf1[:, hs],
                           S1b[:, EV * h + 64:EV * h + 65], start=False, stop=False)
            i3 = pe.matmul(PS_D[:, h:h + 1],
                           qf2[32 * jj:32 * jj + 32, 128 * (h // 4):128 * (h // 4) + 128],
                           S2b[32 * jj:32 * jj + 32,
                               EV * (h // 4) + 64:EV * (h // 4) + 65],
                           start=False, stop=True, tile_position=(32 * jj, 0))
        t_dmm = sm.inc("pe", i3)
        sm.wait(dve, t_dmm, t_tr)
        td1 = sm.inc("dve", dve.tensor_scalar_add(dens[:], PS_D[:], ATT_EPS))
        sm.wait(dve, td1)
        td2 = sm.inc("dve", dve.reciprocal(recs[:], dens[:]))
        free("PS_D")
        PS_R = psum("PS_R", (16, 128))
        sm.wait(pe, td2, t_recT)
        t_tr = sm.inc("pe", pe.transpose(PS_R[:], recs[:], cst["IDENTF"][:]))
        sm.wait(act, t_tr, t_recT, t_bc)
        t_recT = sm.inc("act", act.copy(recT[:], PS_R[:]))
        free("PS_R")
        PS_B = psum("PS_B", (EV, 2048))
        for h in range(NH):
            sm.wait(pe, t_recT, t_consts, t_y)
            ii = pe.matmul(PS_B[:, CH * h:CH * (h + 1)],
                           cst["ESEL"][:, EV * h:EV * (h + 1)], recT[:],
                           start=True, stop=True)
        t_bc = sm.inc("pe", ii)
        sm.wait(act, t_num, t_num_ev, t_y)
        t_num_ev = sm.inc("act", act.copy(nums[:], PS_N[:]))
        t_y_new = t_y
        for h in range(NH):
            hs = slice(CH * h, CH * (h + 1))
            sm.wait(dve, t_num_ev, t_bc)
            t_y_new = sm.inc("dve", dve.tensor_tensor(
                y_sb[h // 2][64 * (h % 2):64 * (h % 2) + 64, tsl],
                nums[0:64, hs], PS_B[0:64, hs], ALU.mult))
        t_y = t_y_new
        free("PS_B", "PS_N")

        sm.wait(act, t_u1, t_u2, t_num, t_dmm)
        t_s1b = sm.inc("act", act.copy(S1b[:], S1[:]))
        t_s2b = sm.inc("act", act.copy(S2b[:], S2[:]))

    o = dbg_out("ysb", (128, 8 * NT), BF16)
    if o is not None:
        sm.wait(sq, t_y)
        for k in range(8):
            sm.inc("st", sq.dma_start(out=o[:, NT * k:NT * (k + 1)], in_=y_sb[k][:]), 16)
    phase_gate(4)

    free("recT", "recs", "dens", "nums", "at", "sqf", "qrep", "repo",
         "kfo", "qf2", "qf1", "kt", "qt")

    wo = sb("wo", (128, 8 * 1024), BF16)
    sm.wait(gp, *bar())
    t_wo = wload(wo, ins["WoT"])
    t_r1 = [None] * 8
    t_pwo_rd = None
    for m in range(8):
        for ct in range(NT // 512):
            cs = slice(512 * ct, 512 * (ct + 1))
            pw = psum("pwo", (128, 512))
            for k in range(8):
                sm.wait(pe, t_wo, t_y, t_pwo_rd)
                ii = pe.matmul(pw[:], wo[:, 1024 * k + 128 * m:1024 * k + 128 * (m + 1)],
                               y_sb[k][:, cs], start=(k == 0), stop=(k == 7))
            t_mm = sm.inc("pe", ii)
            sm.wait(dve, t_mm, t_r1[m])
            t_r1[m] = sm.inc("dve", dve.tensor_tensor(r[m][:, cs], r[m][:, cs],
                                                      pw[:], ALU.add))
            t_pwo_rd = t_r1[m]
            free("pwo")
    free(*(["wo"] + [f"y{k}" for k in range(8)] +
           ["S2b", "S1b", "S2", "S1", "v_sb"]))
    o = dbg_out("r1", (128, 8 * NT), F32)
    if o is not None:
        sm.wait(sq, *t_r1)
        for k in range(8):
            sm.inc("st", sq.dma_start(out=o[:, NT * k:NT * (k + 1)], in_=r[k][:]), 16)

    phase_gate(5)
    Bcol = sb("Bcol", (32, 1), F32)
    masksb = sb("masksb", (128, NCH * E_EXP), BF16)
    scrowb = sb("scrowb", (1, NT), BF16)
    rks = sb("rks", (128, E_EXP), F32)
    brow = sb("brow", (1, E_EXP), F32)
    browb = sb("browb", (1, E_EXP), BF16)
    slotf = sb("slotf", (128, NCH), F32)
    sloti = sb("sloti", (128, NCH), I32)
    tmp32 = sb("tmp32", (128, E_EXP), F32)
    tmpc = sb("tmpc", (128, 1), F32)
    zero_sb = sb("zero_sb", (128, D), BF16)
    xst = sb("xst", (128, D), BF16)
    rtt = sb("rtt", (128, D), BF16)
    rowM = sb("rowM", (1, NT), F32)
    rowR = sb("rowR", (1, NT), F32)
    ln_bf = [sb(f"ln{k}", (128, NT), BF16) for k in range(8)]

    t_ln2 = ln_exact("l2", [t_r1[7]] * 8, rowM, rowR)
    o = dbg_out("ln2", (128, 8 * NT), BF16)
    if o is not None:
        sm.wait(sq, t_ln2)
        for k in range(8):
            sm.inc("st", sq.dma_start(out=o[:, NT * k:NT * (k + 1)], in_=ln_bf[k][:]), 16)

    phase_gate(6)
    negC = sb("negC", (32, NT), F32)
    mnrow = sb("mnrow", (16, CH), F32)
    ctok = sb("ctok", (128, NCH * E_EXP), F32)
    mneg = sb("mneg", (128, NCH), F32)
    scr0 = sb("scr0", (1, NT), F32)
    scrV = sb("scrV", (1, NT), F32)

    pBc = psum("pBc", (32, 1))
    for k in range(8):
        sm.wait(pe, t_gate, t_consts, t_r1[7], t_ln2)
        ii = pe.matmul(pBc[:], gate[:, 32 * k:32 * (k + 1)], cst["ONECF"][:, 0:1],
                       start=(k == 0), stop=(k == 7))
    t_bm = sm.inc("pe", ii)
    sm.wait(dve, t_bm)
    t_bcol = sm.inc("dve", dve.tensor_copy(Bcol[:], pBc[:]))
    free("pBc")

    t_negC = None
    for ct in range(NT // 512):
        cs = slice(512 * ct, 512 * (ct + 1))
        pAg = psum("pAg", (32, 512))
        for k in range(8):
            sm.wait(pe, t_r1[k], t_gate, t_ln2, t_negC, t_bcol)
            ii = pe.matmul(pAg[:], gate[:, 32 * k:32 * (k + 1)], r[k][:, cs],
                           start=(k == 0), stop=(k == 7))
        t_ag = sm.inc("pe", ii)
        pMb = psum("pMb", (32, 512))
        sm.wait(pe, t_negC)
        t_mb = sm.inc("pe", pe.matmul(pMb[:], cst["ONESF"][0:1, 0:32], rowM[0:1, cs],
                                      start=True, stop=True))
        sm.wait(act, t_mb, t_negC)
        t_me = sm.inc("act", act.copy(negC[:, cs], pMb[:]))
        sm.wait(dve, t_me, t_ag, t_bcol)
        t_negC = sm.inc("dve", dve.scalar_tensor_tensor(
            negC[:, cs], negC[:, cs], Bcol[:, 0:1], pAg[:], ALU.mult, ALU.subtract))
        free("pAg", "pMb")

    pT = psum("pT", (128, 512))
    t_ct = None
    for tt in range(NCH):
        sm.wait(pe, t_negC, t_ct)
        ii = pe.transpose(pT[:, 32 * tt:32 * (tt + 1)],
                          negC[:, CH * tt:CH * (tt + 1)], cst["IDENTF"][0:32, 0:32])
        t_ct = sm.inc("pe", ii)
    sm.wait(dve, t_ct)
    t_ctok = sm.inc("dve", dve.tensor_copy(ctok[:], pT[:]))
    free("pT")
    sm.wait(dve, t_ctok)
    t_mn = sm.inc("dve", dve.tensor_reduce(
        mneg[:].rearrange("p (t o) -> p t o", o=1),
        ctok[:].rearrange("p (t e) -> p t e", e=E_EXP), axis=AX.X, op=ALU.min))
    t_mk = None
    for tt in range(NCH):
        sm.wait(dve, t_mn)
        t_mk = sm.inc("dve", dve.tensor_single_scalar(
            masksb[:, 32 * tt:32 * (tt + 1)], ctok[:, 32 * tt:32 * (tt + 1)],
            mneg[:, tt:tt + 1], ALU.is_equal))
    t_mkb = t_mk

    PS_MT = psum("PS_MT", (16, 128))
    sm.wait(pe, t_mn)
    t_mt = sm.inc("pe", pe.transpose(PS_MT[:], mneg[:], cst["IDENTF"][:]))
    sm.wait(act, t_mt)
    t_mr = sm.inc("act", act.copy(mnrow[:], PS_MT[:]))
    free("PS_MT")
    sm.wait(sq, t_mr)
    t_rw = sm.inc("st", sq.dma_start(
        out=ROWB.rearrange("(a b) -> a b", a=16), in_=mnrow[:]), 16)
    sm.wait(sq, t_rw)
    t_rr = sm.inc("ld", sq.dma_start(
        out=scr0[0:1, :], in_=ROWB.rearrange("(o t) -> o t", o=1)), 16)
    sm.wait(dve, t_rr, t_ln2)
    t_val = sm.inc("dve", dve.scalar_tensor_tensor(
        scrV[0:1, :], scr0[0:1, :], -1.0, rowR[0:1, :], ALU.mult, ALU.mult))
    sm.wait(act, t_val)
    t_sc = sm.inc("act", act.activation(scr0[0:1, :], scrV[0:1, :], AF.Sigmoid))
    sm.wait(dve, t_sc)
    t_scb = sm.inc("dve", dve.tensor_copy(scrowb[:], scr0[0:1, :]))
    o = dbg_out("scores", (1, NT), F32)
    if o is not None:
        sm.wait(sq, t_sc)
        sm.inc("st", sq.dma_start(out=o, in_=scr0[0:1, :]), 16)
    o = dbg_out("masks", (128, NCH * E_EXP), BF16)
    if o is not None:
        sm.wait(sq, t_mk)
        sm.inc("st", sq.dma_start(out=o, in_=masksb[:]), 16)
    sm.wait(sq, t_ctok, t_rw, t_scb, t_mk)
    free("scrV", "scr0", "mneg", "ctok", "mnrow", "negC")

    phase_gate(7)
    xs = [sb(f"xs{k}", (128, NT), BF16) for k in range(8)]
    t_xs = [None] * 8
    for ct in range(NT // 512):
        cs = slice(512 * ct, 512 * (ct + 1))
        pSc = psum("pSc", (128, 512))
        sm.wait(pe, t_scb, t_xs[7])
        t_sb_ = sm.inc("pe", pe.matmul(pSc[:], cst["ONESB"][0:1, 0:128], scrowb[0:1, cs],
                                       start=True, stop=True))
        for k in range(8):
            sm.wait(dve, t_sb_, t_ln2, ("st", 1))
            t_xs[k] = sm.inc("dve", dve.tensor_tensor(xs[k][:, cs], ln_bf[k][:, cs],
                                                      pSc[:], ALU.mult))
        free("pSc")

    sgws = [sb("sgws0", (128, 8 * 512), BF16)]
    suws = [sb("suws0", (128, 8 * 512), BF16)]
    sdws = [sb("sdws0", (128, 4 * 1024), BF16)]
    silu_sb = sb("silu_sb", (128, 256), BF16)
    tsh = sb("tsh", (128, 256), BF16)
    t_sw_used = [None]
    t_r2 = list(t_r1)

    sm.wait(gp, *bar())
    for isl in range(4):
        b = 0
        sm.wait(gp, t_sw_used[b])
        tg1 = sm.inc("gld", gp.dma_start(
            out=sgws[b][:].rearrange("p (k c) -> p k c", k=8),
            in_=ins["SGW"][:, 512 * isl:512 * (isl + 1)].rearrange("(k p) c -> p k c", p=128)), 16)
        tg2 = sm.inc("gld", gp.dma_start(
            out=suws[b][:].rearrange("p (k c) -> p k c", k=8),
            in_=ins["SUW"][:, 512 * isl:512 * (isl + 1)].rearrange("(k p) c -> p k c", p=128)), 16)
        tg3 = sm.inc("gld", gp.dma_start(
            out=sdws[b][:].rearrange("p (k c) -> p k c", k=4),
            in_=ins["SDW"][512 * isl:512 * (isl + 1), :].rearrange("(k p) c -> p k c", p=128)), 16)
        t_si = t_th = None
        for ct in range(NT // 256):
            cs = slice(256 * ct, 256 * (ct + 1))
            pDn = psum("pDn", (128, 2048))
            for it in range(4):
                pG = psum("pG", (128, 256))
                pU = psum("pU", (128, 256))
                for k in range(8):
                    sm.wait(pe, tg1, tg2, t_ln2, t_si, t_xs[7])
                    i1 = pe.matmul(pG[:], sgws[b][:, 512 * k + 128 * it:512 * k + 128 * (it + 1)],
                                   ln_bf[k][:, cs], start=(k == 0), stop=(k == 7))
                tgm = sm.inc("pe", i1)
                for k in range(8):
                    sm.wait(pe, tg2, t_th)
                    i2 = pe.matmul(pU[:], suws[b][:, 512 * k + 128 * it:512 * k + 128 * (it + 1)],
                                   ln_bf[k][:, cs], start=(k == 0), stop=(k == 7))
                tum = sm.inc("pe", i2)
                sm.wait(act, tgm, t_th)
                t_si = sm.inc("act", act.activation(silu_sb[:], pG[:], AF.Silu))
                sm.wait(dve, t_si, tum, t_th)
                t_th = sm.inc("dve", dve.tensor_tensor(tsh[:], silu_sb[:], pU[:],
                                                       ALU.mult))
                free("pG", "pU")
                for m in range(8):
                    sm.wait(pe, t_th, tg3, t_r2[7])
                    i3 = pe.matmul(pDn[:, 256 * m:256 * (m + 1)],
                                   sdws[b][:, 1024 * it + 128 * m:1024 * it + 128 * (m + 1)],
                                   tsh[:], start=(it == 0 and m % 2 == 0), stop=(it == 3),
                                   skip_group_check=True)
            t_dn = sm.inc("pe", i3)
            for m in range(8):
                sm.wait(dve, t_dn, t_r2[m])
                t_r2[m] = sm.inc("dve", dve.tensor_tensor(
                    r[m][:, cs], r[m][:, cs], pDn[:, 256 * m:256 * (m + 1)], ALU.add))
            free("pDn")
        t_sw_used[b] = t_r2[7]
    t_shared_done = t_dn

    free("tsh", "silu_sb", "sdws0", "suws0", "sgws0")

    phase_gate(8)
    sm.wait(dve, t_consts)
    t_br = sm.inc("dve", dve.memset(brow[:], 0.0))
    sm.wait(dve, t_br)
    t_brb = sm.inc("dve", dve.tensor_copy(browb[:], brow[:]))
    t_slot = None
    for tt in range(NCH):
        ms = slice(32 * tt, 32 * (tt + 1))
        pRk = psum("pRk", (128, 32))
        pBb = psum("pBb", (128, 32))
        pTt = psum("pTt", (1, 32))
        sm.wait(pe, t_mkb, t_consts, t_brb, t_r2[7], t_shared_done)
        i1 = pe.matmul(pRk[:], cst["UEXCL"][:], masksb[:, ms], start=True, stop=True)
        t_rk = sm.inc("pe", i1)
        i2 = pe.matmul(pBb[:], cst["ONESB"][0:1, 0:128],
                       browb[:], start=True, stop=True)
        t_bb = sm.inc("pe", i2)
        i3 = pe.matmul(pTt[0:1, :], cst["ONECB"][:], masksb[:, ms], start=True, stop=True)
        t_tt = sm.inc("pe", i3)
        sm.wait(dve, t_rk, t_bb, t_slot)
        t1 = sm.inc("dve", dve.tensor_copy(rks[:], pRk[:]))
        sm.wait(dve, t1)
        t2 = sm.inc("dve", dve.tensor_tensor(rks[:], rks[:], pBb[:], ALU.add))
        sm.wait(dve, t2, t_tt)
        t3 = sm.inc("dve", dve.tensor_tensor(brow[:], brow[:], pTt[0:1, :], ALU.add))
        sm.wait(dve, t3)
        t_brb = sm.inc("dve", dve.tensor_copy(browb[:], brow[:]))
        free("pRk", "pBb", "pTt")
        sm.wait(dve, t2, t_mk)
        t4 = sm.inc("dve", dve.tensor_tensor(tmp32[:], rks[:], cst["EIDX"][:], ALU.add))
        sm.wait(dve, t4)
        t5 = sm.inc("dve", dve.tensor_tensor(tmp32[:], tmp32[:], masksb[:, ms], ALU.mult))
        sm.wait(dve, t5)
        t6 = sm.inc("dve", dve.tensor_reduce(
            slotf[:, tt:tt + 1], tmp32[:], axis=AX.X, op=ALU.add))
        sm.wait(dve, t6)
        t7 = sm.inc("dve", dve.tensor_tensor(tmp32[:], rks[:], masksb[:, ms], ALU.mult))
        sm.wait(dve, t7)
        t8 = sm.inc("dve", dve.tensor_reduce(tmpc[:], tmp32[:], axis=AX.X, op=ALU.add))
        sm.wait(dve, t8)
        t9 = sm.inc("dve", dve.tensor_single_scalar(tmpc[:], tmpc[:], float(CAP - 0.5),
                                                    ALU.is_gt))
        sm.wait(dve, t6, t9)
        ta = sm.inc("dve", dve.tensor_scalar(
            tmp32[:, 0:1], slotf[:, tt:tt + 1], -1.0, float(TRASH), ALU.mult, ALU.add))
        sm.wait(dve, ta)
        tb = sm.inc("dve", dve.tensor_tensor(tmp32[:, 0:1], tmp32[:, 0:1], tmpc[:],
                                             ALU.mult))
        sm.wait(dve, tb)
        tcp = sm.inc("dve", dve.tensor_tensor(slotf[:, tt:tt + 1], slotf[:, tt:tt + 1],
                                              tmp32[:, 0:1], ALU.add))
        sm.wait(dve, tcp)
        t_slot = sm.inc("dve", dve.tensor_copy(sloti[:, tt:tt + 1], slotf[:, tt:tt + 1]))
    o = dbg_out("sloti", (128, NCH), I32)
    if o is not None:
        sm.wait(sq, t_slot)
        sm.inc("st", sq.dma_start(out=o, in_=sloti[:]), 16)

    sm.wait(dve, t_slot)
    t_z = sm.inc("dve", dve.memset(zero_sb[:], 0.0))
    t_zs = None
    sm.wait(sq, t_z)
    for blk in range(NSLOT // 128):
        t_zs = sm.inc("st", sq.dma_start(out=XSG[128 * blk:128 * (blk + 1), :],
                                         in_=zero_sb[:]), 16)
    t_zs = sm.inc("st", sq.dma_start(out=XSG[NSLOT - 16:NSLOT, :], in_=zero_sb[0:16, :]), 16)
    t_zr = sm.inc("st", sq.dma_start(out=RTD[TRASH:TRASH + 16, :], in_=zero_sb[0:16, :]), 16)

    t_xst = None
    t_scat = None
    for tt in range(NCH):
        tsl = slice(CH * tt, CH * (tt + 1))
        pX = psum("pX", (128, 128), BF16)
        t_tp = None
        for k in range(8):
            sm.wait(pe, t_xs[k], t_consts, t_xst, t_slot)
            t_tp = sm.inc("pe", pe.transpose(pX[:], xs[k][:, tsl], cst["IDENTB"][:]))
            sm.wait(act, t_tp, t_scat)
            t_xst = sm.inc("act", act.copy(xst[:, 128 * k:128 * (k + 1)], pX[:]))
        free("pX")
        sm.wait(gp, t_xst, t_zs, t_slot)
        t_scat = sm.inc("sct", gp.indirect_dma_start(
            out=XSG, out_offset=IndirectOffsetOnAxis(ap=sloti[:, tt:tt + 1], axis=0),
            in_=xst[:], in_offset=None), 16)
    t_scat_all = t_scat

    free(*[f"xs{k}" for k in range(8)])
    free(*[f"ln{k}" for k in range(8)] + ["rowR", "rowM"])

    xg = [sb(f"xg{k}", (128, E_EXP * CAP), BF16) for k in range(8)]
    t_xg = [None] * 8
    for k in range(8):
        sm.wait(sq, t_scat_all, t_r2[7], t_shared_done)
        t_xg[k] = sm.inc("ld", sq.dma_start(
            out=xg[k][:], in_=XSG[0:E_EXP * CAP, 128 * k:128 * (k + 1)],
            transpose=True), 16)

    NSL = 8
    rgws = [sb(f"rgws{b}", (128, 8 * 256), BF16) for b in range(2)]
    ruws = [sb(f"ruws{b}", (128, 8 * 256), BF16) for b in range(2)]
    rdws = [sb(f"rdws{b}", (128, 2 * 1024), BF16) for b in range(2)]
    te_sb = sb("te_sb", (128, 2 * CAP), BF16)
    sil2 = sb("sil2", (128, 2 * CAP), BF16)
    rts = sb("rts", (128, 1024), BF16)

    t_buf_used = [None, None]
    t_rts = None
    t_rt_store = None
    t_si2 = t_te = t_dn_pe = None
    nsl_ctr = 0

    for e in range(E_EXP):
        PRo = psum("PRo", (128, 1024))
        for isl in range(NSL):
            b = nsl_ctr % 2
            nsl_ctr += 1
            wsem = "wt%d" % b
            sm.wait(gp, t_buf_used[b], t_r2[7], t_shared_done, ("sct", 1))
            tg1 = sm.inc(wsem, gp.dma_start(
                out=rgws[b][:].rearrange("p (k c) -> p k c", k=8),
                in_=ins["RGW"][e, :, 256 * isl:256 * (isl + 1)].rearrange("(k p) c -> p k c", p=128)), 16)
            tg2 = sm.inc(wsem, gp.dma_start(
                out=ruws[b][:].rearrange("p (k c) -> p k c", k=8),
                in_=ins["RUW"][e, :, 256 * isl:256 * (isl + 1)].rearrange("(k p) c -> p k c", p=128)), 16)
            tg3 = sm.inc(wsem, gp.dma_start(
                out=rdws[b][:].rearrange("p (k c) -> p k c", k=2),
                in_=ins["RDW"][e, 256 * isl:256 * (isl + 1), :].rearrange("(k p) c -> p k c", p=128)), 16)
            t_te_it = [None, None]
            pGs = [psum(f"pG2{x}", (128, CAP)) for x in range(2)]
            pUs = [psum(f"pU2{x}", (128, CAP)) for x in range(2)]
            for it in range(2):
                pG, pU = pGs[it], pUs[it]
                for k in range(8):
                    sm.wait(pe, tg1, t_xg[k], t_si2)
                    i1 = pe.matmul(pG[:], rgws[b][:, 256 * k + 128 * it:256 * k + 128 * (it + 1)],
                                   xg[k][:, CAP * e:CAP * (e + 1)],
                                   start=(k == 0), stop=(k == 7))
                tgm = sm.inc("pe", i1)
                for k in range(8):
                    sm.wait(pe, tg2, t_te_it[it] or t_te)
                    i2 = pe.matmul(pU[:], ruws[b][:, 256 * k + 128 * it:256 * k + 128 * (it + 1)],
                                   xg[k][:, CAP * e:CAP * (e + 1)],
                                   start=(k == 0), stop=(k == 7))
                tum = sm.inc("pe", i2)
                sm.wait(act, tgm, t_te)
                t_si2 = sm.inc("act", act.activation(sil2[:, CAP * it:CAP * (it + 1)], pG[:], AF.Silu))
                sm.wait(dve, t_si2, tum, t_te, t_dn_pe)
                t_te = sm.inc("dve", dve.tensor_tensor(
                    te_sb[:, CAP * it:CAP * (it + 1)],
                    sil2[:, CAP * it:CAP * (it + 1)], pU[:], ALU.mult))
                t_te_it[it] = t_te
            for it in range(2):
                for nn in range(2):
                    sm.wait(pe, t_te_it[it], tg3, t_rts, t_r2[7])
                    i3 = pe.matmul(PRo[:, 512 * nn:512 * (nn + 1)],
                                   te_sb[0:128, CAP * it:CAP * (it + 1)],
                                   rdws[b][:, 1024 * it + 512 * nn:1024 * it + 512 * (nn + 1)],
                                   start=(isl == 0 and it == 0), stop=(isl == NSL - 1 and it == 1),
                                   skip_group_check=True)
            t_dn = sm.inc("pe", i3)
            free("pG20", "pG21", "pU20", "pU21")
            t_buf_used[b] = t_dn
            t_dn_pe = t_dn
        sm.wait(act, t_dn, t_rt_store)
        t_rts = sm.inc("act", act.copy(rts[:], PRo[:]))
        free("PRo")
        sm.wait(sq, t_rts, t_zr)
        t_rt_store = sm.inc("st", sq.dma_start(
            out=RTD[CAP * e:CAP * (e + 1), :], in_=rts[:]), 16)

    free(*(["rts", "sil2", "te_sb", "rdws0", "rdws1", "ruws0", "ruws1",
            "rgws0", "rgws1"] + [f"xg{k}" for k in range(8)]))

    phase_gate(9)
    t_rtt = None
    t_pX2_rd = None
    t_out = [None] * 8
    for tt in range(NCH):
        tsl = slice(CH * tt, CH * (tt + 1))
        sm.wait(gp, t_rt_store, t_slot, t_rtt)
        t_g = sm.inc("sct", gp.indirect_dma_start(
            out=rtt[:], out_offset=None,
            in_=RTD, in_offset=IndirectOffsetOnAxis(ap=sloti[:, tt:tt + 1], axis=0)), 16)
        for k in range(8):
            pX = psum("pX2", (128, 128), BF16)
            sm.wait(pe, t_g, t_consts, t_pX2_rd, t_rts, t_te)
            t_tp = sm.inc("pe", pe.transpose(pX[:], rtt[:, 128 * k:128 * (k + 1)],
                                             cst["IDENTB"][:]))
            sm.wait(dve, t_tp, t_r2[k], t_out[k])
            t_out[k] = sm.inc("dve", dve.tensor_tensor(r[k][:, tsl], r[k][:, tsl],
                                                       pX[:], ALU.add))
            t_pX2_rd = t_out[k]
            free("pX2")
        t_rtt = t_out[7]

    t_fin = None
    for k in range(8):
        sm.wait(sq, t_out[k])
        t_fin = sm.inc("st", sq.dma_start(out=OUT[128 * k:128 * (k + 1), :], in_=r[k][:]), 16)
    sm.wait(sq, t_fin)

    stack.close()
    return nc, dbg


def _noop():
    pass


def kernel(L0_attn_hidden_states, L0_attn_Wq, L0_attn_Wk, L0_attn_Wv,
           L0_attn_Wo, L0_ffn_gate_w, L0_ffn_shared_gate_w, L0_ffn_shared_up_w,
           L0_ffn_shared_down_w, L0_ffn_routed_gate_w, L0_ffn_routed_up_w,
           L0_ffn_routed_down_w):
    x = np.asarray(L0_attn_hidden_states, np.float32).reshape(T_FULL, D)
    XT = np.ascontiguousarray(x.T)
    shared = {
        "WqT": np.ascontiguousarray(np.asarray(L0_attn_Wq, np.float32).T),
        "WkT": np.ascontiguousarray(np.asarray(L0_attn_Wk, np.float32).T),
        "WvT": np.ascontiguousarray(np.asarray(L0_attn_Wv, np.float32).T),
        "WoT": np.ascontiguousarray(np.asarray(L0_attn_Wo, np.float32).T),
        "GATE": np.ascontiguousarray(np.asarray(L0_ffn_gate_w, np.float32)),
        "SGW": np.ascontiguousarray(np.asarray(L0_ffn_shared_gate_w, np.float32)),
        "SUW": np.ascontiguousarray(np.asarray(L0_ffn_shared_up_w, np.float32)),
        "SDW": np.ascontiguousarray(np.asarray(L0_ffn_shared_down_w, np.float32)),
        "RGW": np.ascontiguousarray(np.asarray(L0_ffn_routed_gate_w, np.float32)),
        "RUW": np.ascontiguousarray(np.asarray(L0_ffn_routed_up_w, np.float32)),
        "RDW": np.ascontiguousarray(np.asarray(L0_ffn_routed_down_w, np.float32)),
    }
    shared.update(CONSTS)
    in_maps = []
    for c in range(NCORES):
        t0 = NT * c
        xpre = np.zeros((D, PRE), np.float32)
        if t0 > 0:
            xpre[:, :t0] = XT[:, :t0]
        pmt = np.zeros((128, PRE // CH), np.float32)
        pmt[:, :t0 // CH] = 1.0
        m = dict(shared)
        m["XS_T"] = np.ascontiguousarray(XT[:, t0:t0 + NT])
        m["XPRE_T"] = xpre
        m["PMT"] = pmt
        in_maps.append(m)
    nc, _ = build(tuple(DEBUG_OUTS))
    res = run_bass_kernel_spmd(nc, in_maps, core_ids=list(range(NCORES)))
    out = np.concatenate(
        [np.asarray(res.results[c]["OUT"], np.float32).T for c in range(NCORES)],
        axis=0)
    return out.reshape(1, T_FULL, D)


# version 10
# speedup vs baseline: 1.7020x; 1.0486x over previous
import contextlib
import numpy as np
import ml_dtypes
import concourse.bass as bass
import concourse.mybir as mybir
from concourse.bass_utils import run_bass_kernel_spmd
from concourse.bass_core_rust import IndirectOffsetOnAxis

F32 = mybir.dt.float32
BF16 = mybir.dt.bfloat16
I32 = mybir.dt.int32
AF = mybir.ActivationFunctionType
ALU = mybir.AluOpType
AX = mybir.AxisListType
BF = ml_dtypes.bfloat16

NCORES = 8
D = 1024
T_FULL = 16384
NT = T_FULL // NCORES
NH, FEAT, HD = 16, 16, 64
CH = 128
NCH = NT // CH
PRE = T_FULL - NT
PT = 512
NPT = PRE // PT
DE = 160
EV = 65
E_EXP = 32
I_FFN = 2048
CAP = 128
TRASH = E_EXP * CAP
NSLOT = E_EXP * CAP + 16
LN_EPS = 1e-5
ATT_EPS = 1e-12

DEBUG_OUTS = []
PHASE_LIMIT = 99


def _tri_consts():
    pairs = [(i, j) for i in range(FEAT) for j in range(i, FEAT)]
    R1 = np.zeros((DE, FEAT + 1), np.float32)
    R2k = np.zeros((DE, FEAT + 1), np.float32)
    R2q = np.zeros((DE, FEAT + 1), np.float32)
    for f, (i, j) in enumerate(pairs):
        R1[f, i] = 1.0
        R2k[f, j] = 1.0
        R2q[f, j] = (1.0 / 32.0) if i == j else (1.0 / 16.0)
    for i in range(FEAT):
        R1[136 + i, i] = 1.0
        R2k[136 + i, FEAT] = 1.0
        R2q[136 + i, FEAT] = 0.25
    R1[152, FEAT] = 1.0
    R2k[152, FEAT] = 1.0
    R2q[152, FEAT] = 1.0
    return R1, R2k, R2q


def _consts():
    c = {}
    R1, R2k, R2q = _tri_consts()
    c["R1KT"] = np.ascontiguousarray(R1.T).astype(BF)
    c["R2KT"] = np.ascontiguousarray(R2k.T).astype(BF)
    c["R12KT"] = np.ascontiguousarray(np.concatenate([R1.T, R2k.T], axis=1)).astype(BF)
    c["R12D1"] = np.ascontiguousarray(np.concatenate([R1[0:128].T, R2k[0:128].T], axis=1)).astype(BF)
    c["R12D2"] = np.ascontiguousarray(np.concatenate([R1[128:160].T, R2k[128:160].T], axis=1)).astype(BF)
    c["R1QD1"] = np.ascontiguousarray(R1[0:128].T).astype(BF)
    c["R2QD1"] = np.ascontiguousarray(R2q[0:128].T).astype(BF)
    c["R1QD2"] = np.ascontiguousarray(R1[128:160].T).astype(BF)
    c["R2QD2"] = np.ascontiguousarray(R2q[128:160].T).astype(BF)
    c["IDENTF"] = np.eye(128, dtype=np.float32)
    c["IDENTB"] = np.eye(128, dtype=BF)
    mh = 0.5 * np.triu(np.ones((CH, CH), np.float32))
    c["MASKH"] = np.tile(mh, (1, NH)).astype(BF)
    c["UEXCL"] = np.triu(np.ones((CH, CH), np.float32), 1).astype(BF)
    c["EIDX"] = np.tile(np.arange(E_EXP, dtype=np.float32)[None, :] * CAP,
                        (128, 1)).astype(np.float32)
    es = np.zeros((FEAT, NH * EV), np.float32)
    for h in range(NH):
        es[h, h * EV:(h + 1) * EV] = 1.0
    c["ESEL"] = es.astype(BF)
    c["ONESB"] = np.ones((128, 256), BF)
    c["ONECB"] = np.ones((128, 1), BF)
    c["ONECF"] = np.ones((128, 1), np.float32)
    c["ONESF"] = np.ones((1, 128), np.float32)
    return c


CONSTS = _consts()


class S:
    def __init__(self, nc, stack):
        self.h = {}
        self.cnt = {}
        self.seen = {}
        for name in ("pe", "dve", "act", "gp", "ld", "gld", "st", "sct", "wt0", "wt1"):
            self.h[name] = stack.enter_context(nc.semaphore(name))
            self.cnt[name] = 0

    def inc(self, name, inst, n=1):
        inst.then_inc(self.h[name], n)
        self.cnt[name] += n
        return (name, self.cnt[name])

    DMA_SEMS = ("ld", "gld", "st", "sct", "wt0", "wt1")

    def wait(self, eng, *toks):
        seen = {}
        for t in toks:
            if t is None:
                continue
            name, v = t
            if name in self.DMA_SEMS:
                v = self.cnt[name]
            if v > 0:
                seen[name] = max(seen.get(name, 0), v)
        done = self.seen.setdefault(id(eng), {})
        for name, v in seen.items():
            if done.get(name, -1) >= v:
                continue
            done[name] = v
            eng.wait_ge(self.h[name], v)


class _EarlyFinishDone(Exception):
    pass


def build(debug_outs=()):
    try:
        return _build(debug_outs)
    except _EarlyFinishDone as ex:
        return ex.args[0]


def _build(debug_outs=()):
    nc = bass.Bass("TRN2", target_bir_lowering=False)
    stack = contextlib.ExitStack()

    def dram_in(name, shape, dtype=F32):
        return nc.dram_tensor(name, list(shape), dtype, kind="ExternalInput").ap()

    ins = {}
    ins["XS_T"] = dram_in("XS_T", (D, NT))
    ins["XPRE_T"] = dram_in("XPRE_T", (D, PRE))
    ins["PMT"] = dram_in("PMT", (128, PRE // CH))
    for w, shp in (("WqT", (D, 256)), ("WkT", (D, 256)), ("WvT", (D, D)),
                   ("WoT", (D, D)), ("GATE", (D, E_EXP)),
                   ("SGW", (D, I_FFN)), ("SUW", (D, I_FFN)), ("SDW", (I_FFN, D)),
                   ("RGW", (E_EXP, D, I_FFN)), ("RUW", (E_EXP, D, I_FFN)),
                   ("RDW", (E_EXP, I_FFN, D))):
        ins[w] = dram_in(w, shp)
    for k, v in CONSTS.items():
        ins[k] = dram_in(k, v.shape, BF16 if v.dtype == BF else F32)

    OUT = nc.dram_tensor("OUT", [D, NT], F32, kind="ExternalOutput").ap()
    dbg = {}

    def dbg_out(name, shape, dtype=F32):
        if name in debug_outs:
            dbg[name] = nc.dram_tensor("dbg_" + name, list(shape), dtype,
                                       kind="ExternalOutput").ap()
        return dbg.get(name)

    KBOWN = nc.dram_tensor("KBOWN", [2, FEAT + 1, NH, NT], BF16, kind="Internal").ap()
    KBPRE = nc.dram_tensor("KBPRE", [2, FEAT + 1, NH, PT], BF16, kind="Internal").ap()
    ROWB = nc.dram_tensor("ROWB", [NT], F32, kind="Internal").ap()
    XSG = nc.dram_tensor("XSG", [NSLOT, D], BF16, kind="Internal").ap()
    RTD = nc.dram_tensor("RTD", [NSLOT, D], BF16, kind="Internal").ap()

    pe, dve, act, gp, sq = nc.tensor, nc.vector, nc.scalar, nc.gpsimd, nc.sync
    sm = S(nc, stack)

    def bar():
        return (("pe", sm.cnt["pe"]), ("dve", sm.cnt["dve"]),
                ("act", sm.cnt["act"]), ("gp", sm.cnt["gp"]), ("st", 1),
                ("ld", 1), ("gld", 1))

    def phase_gate(n):
        if PHASE_LIMIT > n:
            return
        for name in ("pe", "dve", "act", "gp", "ld", "gld", "st", "sct", "wt0", "wt1"):
            if sm.cnt[name] > 0:
                sq.wait_ge(sm.h[name], sm.cnt[name])
        t_fin = None
        for k in range(8):
            t_fin = sm.inc("st", sq.dma_start(
                out=OUT[128 * k:128 * (k + 1), :], in_=r[k][:]), 16)
        sm.wait(sq, t_fin)
        stack.close()
        raise _EarlyFinishDone((nc, dbg))
    frees = {}
    _seq = [0]

    def _uname(name):
        _seq[0] += 1
        return f"{name}_{_seq[0]}"

    def sb(name, shape, dtype):
        assert name not in frees, name
        cm = nc.sbuf_tensor(_uname(name), list(shape), dtype)
        t = cm.__enter__()
        frees[name] = cm
        return t

    def psum(name, shape, dtype=F32):
        assert name not in frees, name
        cm = nc.psum_tensor(_uname(name), list(shape), dtype)
        t = cm.__enter__()
        frees[name] = cm
        return t

    def free(*names):
        order = list(frees.keys())
        for n in sorted(names, key=lambda n: -order.index(n)):
            frees.pop(n).__exit__(None, None, None)

    r = [sb(f"r{k}", (128, NT), F32) for k in range(8)]
    cst = {}
    for k, v in CONSTS.items():
        cst[k] = sb("c_" + k, v.shape, BF16 if v.dtype == BF else F32)
    gate = sb("gate", (128, 8 * E_EXP), F32)
    pmt = sb("pmt", (128, PRE // CH), F32)

    t_r_in = []
    for k in range(8):
        t_r_in.append(sm.inc("ld", sq.dma_start(
            out=r[k][:], in_=ins["XS_T"][128 * k:128 * (k + 1), :]), 16))
    for k in CONSTS:
        sm.inc("ld", sq.dma_start(out=cst[k][:], in_=ins[k]), 16)
    sm.inc("ld", sq.dma_start(out=pmt[:], in_=ins["PMT"]), 16)
    t_gate = sm.inc("ld", sq.dma_start(
        out=gate[:].rearrange("p (k c) -> p k c", k=8),
        in_=ins["GATE"].rearrange("(k p) c -> p k c", p=128)), 16)
    t_consts = ("ld", sm.cnt["ld"])

    def wload(dst, src, ktiles=8):
        return sm.inc("gld", gp.dma_start(
            out=dst[:].rearrange("p (k c) -> p k c", k=ktiles),
            in_=src.rearrange("(k p) c -> p k c", p=128)), 16)


    v_sb = sb("v_sb", (128, NCH * NH * EV), BF16)
    S1 = sb("S1", (128, NH * EV), F32)
    S2 = sb("S2", (128, 4 * EV), F32)
    wk = sb("wk", (128, 8 * 256), BF16)
    wv = sb("wv", (128, 8 * 1024), BF16)
    wq = sb("wq", (128, 8 * 256), BF16)
    t_wq = wload(wq, ins["WqT"])
    t_wk = wload(wk, ins["WkT"])
    t_wv = wload(wv, ins["WvT"])

    ln_bf = [sb(f"ln{k}", (128, NT), BF16) for k in range(8)]

    def ln_exact(tag, t_src, rowM, rowR):
        sqs = sb("sqs" + tag, (128, 512), F32)
        rowS = sb("rowS" + tag, (1, NT), F32)
        rowMb = sb("rowMb" + tag, (1, NT), BF16)
        rowRb = sb("rowRb" + tag, (1, NT), BF16)
        last = None
        t_sqs = None
        for ct in range(NT // 512):
            cs = slice(512 * ct, 512 * (ct + 1))
            pA = psum(f"{tag}A", (1, 512))
            pB = psum(f"{tag}B", (1, 512))
            for k in range(8):
                sm.wait(pe, t_src[k], t_consts, last)
                i1 = pe.matmul(pA[0:1, :], cst["ONECF"][:], r[k][:, cs],
                               start=(k == 0), stop=(k == 7))
                sm.wait(act, t_src[k], t_sqs)
                t2 = sm.inc("act", act.activation(sqs[:], r[k][:, cs], AF.Square))
                sm.wait(pe, t2)
                i3 = pe.matmul(pB[0:1, :], cst["ONECF"][:], sqs[:],
                               start=(k == 0), stop=(k == 7))
                t_sqs = sm.inc("pe", i3)
            t_mm = sm.inc("pe", i1)
            sm.wait(dve, t_mm)
            t1 = sm.inc("dve", dve.tensor_scalar_mul(rowM[0:1, cs], pA[0:1, :], 1.0 / D))
            sm.wait(dve, t_sqs, t1)
            t2 = sm.inc("dve", dve.tensor_scalar_mul(rowS[0:1, cs], pB[0:1, :], 1.0 / D))
            sm.wait(dve, t1)
            t3 = sm.inc("dve", dve.tensor_tensor(rowR[0:1, cs], rowM[0:1, cs],
                                                 rowM[0:1, cs], ALU.mult))
            sm.wait(dve, t2, t3)
            t4 = sm.inc("dve", dve.tensor_tensor(rowS[0:1, cs], rowS[0:1, cs],
                                                 rowR[0:1, cs], ALU.subtract))
            sm.wait(dve, t4)
            t5 = sm.inc("dve", dve.tensor_scalar_add(rowS[0:1, cs], rowS[0:1, cs], LN_EPS))
            sm.wait(dve, t5)
            t6 = sm.inc("dve", dve.reciprocal(rowS[0:1, cs], rowS[0:1, cs]))
            sm.wait(act, t6)
            t7 = sm.inc("act", act.activation(rowR[0:1, cs], rowS[0:1, cs], AF.Sqrt))
            sm.wait(dve, t1, t7)
            t8 = sm.inc("dve", dve.tensor_copy(rowMb[0:1, cs], rowM[0:1, cs]))
            sm.wait(dve, t8)
            t9 = sm.inc("dve", dve.tensor_copy(rowRb[0:1, cs], rowR[0:1, cs]))
            free(f"{tag}A", f"{tag}B")
            pM = psum(f"{tag}M", (128, 512))
            pR = psum(f"{tag}R", (128, 512))
            sm.wait(pe, t8)
            t_mb = sm.inc("pe", pe.matmul(pM[:], cst["ONESB"][0:1, 0:128], rowMb[0:1, cs],
                                          start=True, stop=True))
            sm.wait(pe, t9)
            t_rb = sm.inc("pe", pe.matmul(pR[:], cst["ONESB"][0:1, 0:128], rowRb[0:1, cs],
                                          start=True, stop=True))
            for k in range(8):
                sm.wait(dve, t_mb, t_rb, t_sqs)
                ta = sm.inc("dve", dve.tensor_tensor(sqs[:], r[k][:, cs], pM[:],
                                                     ALU.subtract))
                t_sqs = ta
                sm.wait(dve, ta)
                last = sm.inc("dve", dve.tensor_tensor(ln_bf[k][:, cs], sqs[:], pR[:],
                                                       ALU.mult))
                t_sqs = last
            free(f"{tag}M", f"{tag}R")
        free("rowRb" + tag, "rowMb" + tag, "rowS" + tag, "sqs" + tag)
        return last

    rowM1 = sb("rowM1", (1, NT), F32)
    rowR1 = sb("rowR1", (1, NT), F32)
    t_ln1 = ln_exact("l1", t_r_in, rowM1, rowR1)
    o = dbg_out("ln1", (128, 8 * NT), BF16)
    if o is not None:
        sm.wait(sq, t_ln1)
        for k in range(8):
            sm.inc("st", sq.dma_start(out=o[:, NT * k:NT * (k + 1)], in_=ln_bf[k][:]), 16)
    phase_gate(1)

    qknat = sb("qknat", (128, NT), BF16)

    t_kb = []
    tg = None
    for iqk, wmat, t_w in ((0, wq, t_wq), (1, wk, t_wk)):
        for grp in range(2):
            for ct in range(NT // 512):
                cs = slice(512 * ct, 512 * (ct + 1))
                pg = psum("pj", (128, 512))
                for k in range(8):
                    sm.wait(pe, t_w, t_ln1, tg)
                    ii = pe.matmul(pg[:], wmat[:, 256 * k + 128 * grp:256 * k + 128 * (grp + 1)],
                                   ln_bf[k][:, cs], start=(k == 0), stop=(k == 7))
                t_mm = sm.inc("pe", ii)
                sm.wait(act, t_mm, tg, t_kb[-1] if t_kb else None)
                tg = sm.inc("act", act.copy(qknat[:, cs], pg[:]))
                free("pj")
            sm.wait(sq, tg)
            t_kb.append(sm.inc("st", sq.dma_start(
                out=KBOWN[iqk, 0:FEAT, 8 * grp:8 * (grp + 1), :].rearrange("f h t -> h f t"),
                in_=qknat[:]), 16))
    for iqk in range(2):
        sm.wait(sq, t_consts)
        t_kb.append(sm.inc("st", sq.dma_start(
            out=KBOWN[iqk, FEAT, :, :].rearrange("h t -> (h t)").rearrange("(a b) -> a b", a=128),
            in_=cst["ONESB"][:, 0:256]), 16))
    t_kb_all = t_kb[-1]
    free("qknat")

    t_vown = tg
    for tt in range(NCH):
        ts_ = slice(128 * tt, 128 * (tt + 1))
        for half in range(2):
            pv = psum("pv", (128, 512))
            for k in range(8):
                sm.wait(pe, t_wv, t_ln1, t_vown)
                ii = pe.matmul(pv[:], ln_bf[k][:, ts_],
                               wv[:, 1024 * k + 512 * half:1024 * k + 512 * (half + 1)],
                               start=(k == 0), stop=(k == 7))
            t_mm = sm.inc("pe", ii)
            sm.wait(act, t_mm, t_vown)
            t_vown = sm.inc("act", act.copy(
                v_sb[:].rearrange("p (t h e) -> p t h e", t=NCH, h=NH)[:, tt, 8 * half:8 * (half + 1), 0:64],
                pv[:].rearrange("p (h e) -> p h e", h=8)))
            free("pv")
    sm.wait(dve, t_vown)
    t_vones = sm.inc("dve", dve.memset(
        v_sb[:].rearrange("p (t h e) -> p t h e", t=NCH, h=NH)[:, :, :, 64:65], 1.0))
    o = dbg_out("vsb", (128, NCH * NH * EV), BF16)
    if o is not None:
        sm.wait(sq, t_vones)
        sm.inc("st", sq.dma_start(out=o, in_=v_sb[:]), 16)
    phase_gate(2)
    free(*[f"ln{k}" for k in range(8)] + ["rowR1", "rowM1", "wq"])

    t_sz1 = sm.inc("dve", dve.memset(S1[:], 0.0))
    t_sz2 = sm.inc("dve", dve.memset(S2[:], 0.0))

    xbf = [[sb(f"xbf{p}_{k}", (128, PT), BF16) for k in range(8)] for p in range(2)]
    xh = [[sb(f"xh{p}_{k}", (128, PT), BF16) for k in range(8)] for p in range(2)]
    sqb = sb("sqb", (128, PT), BF16)
    prowM = sb("prowM", (1, PT), F32)
    prowR = sb("prowR", (1, PT), F32)
    prowS = sb("prowS", (1, PT), F32)
    prowMb = sb("prowMb", (1, PT), BF16)
    prowRb = sb("prowRb", (1, PT), BF16)
    knat = sb("knat", (128, 2 * PT), BF16)
    ktp = sb("ktp", (FEAT + 1, NH * PT), BF16)
    kf = sb("kf", (128, 2048), BF16)
    rep_sb = sb("rep_sb", (128, 8 * DE), BF16)
    vpre = sb("vpre", (128, 4 * NH * EV), BF16)
    pmb = sb("pmb", (128, 4), F32)

    t_pm = None
    t_pm_rd = None
    t_row_rd = None
    t_lnrow_prev = None
    t_rep_act = None
    t_kf_last = [None]
    t_xbf = [None] * 8
    t_xh = [None] * 8
    t_xh_war = [[None] * 8, [None] * 8]
    t_knat = None
    t_ktp_w = [None, None]
    t_ktp_r = [None, None]
    t_kf = None
    t_vpre = None
    t_state = None
    t_u1p = t_u2p = None

    t_pre_ones = None
    for ib in range(2):
        for hhalf in range(2):
            sm.wait(sq, t_consts)
            t_pre_ones = sm.inc("st", sq.dma_start(
                out=KBPRE[ib, FEAT, 8 * hhalf:8 * (hhalf + 1), :]
                    .rearrange("h t -> (h t)").rearrange("(a b) -> a b", a=16),
                in_=cst["ONESB"][0:16, 0:256]), 16)

    t_bar3 = bar()
    t_rep_rd = None
    for j in range(NPT):
        js = slice(PT * j, PT * (j + 1))
        buf = j % 2
        xb, xhh = xbf[buf], xh[buf]
        t_xb_prev = t_xbf[buf * 0:8] if False else None
        for k in range(8):
            sm.wait(gp, t_xh_war[buf][k], *(t_bar3 if j == 0 else ()))
            t_xbf[k] = sm.inc("wt%d" % buf, gp.dma_start(
                out=xb[k][:], in_=ins["XPRE_T"][128 * k:128 * (k + 1), js]), 16)
        sm.wait(dve, t_consts, t_pm_rd, *(t_bar3 if j == 0 else ()))
        t_pm = sm.inc("dve", dve.tensor_copy(pmb[:], pmt[:, 4 * j:4 * (j + 1)]))
        pA = psum("ppA", (1, PT))
        pB = psum("ppB", (1, PT))
        t_sqb = None
        for k in range(8):
            sm.wait(pe, t_xbf[k], t_consts, t_lnrow_prev, t_vown, t_row_rd, t_u2p)
            i1 = pe.matmul(pA[0:1, :], cst["ONECB"][:], xb[k][:],
                           start=(k == 0), stop=(k == 7))
            sm.wait(act, t_xbf[k], t_sqb)
            t2 = sm.inc("act", act.activation(sqb[:], xb[k][:], AF.Square))
            sm.wait(pe, t2)
            i3 = pe.matmul(pB[0:1, :], cst["ONECB"][:], sqb[:],
                           start=(k == 0), stop=(k == 7))
            t_sqb = sm.inc("pe", i3)
        t_mm = sm.inc("pe", i1)
        sm.wait(dve, t_mm)
        t1 = sm.inc("dve", dve.tensor_scalar_mul(prowM[0:1, :], pA[0:1, :], 1.0 / D))
        sm.wait(dve, t_sqb, t1)
        t2 = sm.inc("dve", dve.tensor_scalar_mul(prowS[0:1, :], pB[0:1, :], 1.0 / D))
        t_lnrow_prev = t2
        sm.wait(dve, t1)
        t3 = sm.inc("dve", dve.tensor_tensor(prowR[0:1, :], prowM[0:1, :],
                                             prowM[0:1, :], ALU.mult))
        sm.wait(dve, t2, t3)
        t4 = sm.inc("dve", dve.tensor_tensor(prowS[0:1, :], prowS[0:1, :],
                                             prowR[0:1, :], ALU.subtract))
        sm.wait(dve, t4)
        t5 = sm.inc("dve", dve.tensor_scalar_add(prowS[0:1, :], prowS[0:1, :], LN_EPS))
        sm.wait(dve, t5)
        t6 = sm.inc("dve", dve.reciprocal(prowS[0:1, :], prowS[0:1, :]))
        sm.wait(act, t6)
        t7 = sm.inc("act", act.activation(prowR[0:1, :], prowS[0:1, :], AF.Sqrt))
        sm.wait(dve, t1, t7)
        t8 = sm.inc("dve", dve.tensor_copy(prowMb[0:1, :], prowM[0:1, :]))
        sm.wait(dve, t8)
        t9 = sm.inc("dve", dve.tensor_copy(prowRb[0:1, :], prowR[0:1, :]))
        free("ppA", "ppB")
        pM = psum("ppM", (128, PT))
        pR = psum("ppR", (128, PT))
        sm.wait(pe, t8)
        t_mb = sm.inc("pe", pe.matmul(pM[:], cst["ONESB"][0:1, 0:128], prowMb[0:1, :],
                                      start=True, stop=True))
        sm.wait(pe, t9)
        t_rb = sm.inc("pe", pe.matmul(pR[:], cst["ONESB"][0:1, 0:128], prowRb[0:1, :],
                                      start=True, stop=True))
        t_row_rd = t_rb
        t_xh_new = [None] * 8
        for k in range(8):
            sm.wait(dve, t_mb, t_rb, t_xbf[k], t_knat, t_xh_war[buf][k])
            ta = sm.inc("dve", dve.tensor_tensor(xb[k][:], xb[k][:], pM[:],
                                                 ALU.subtract))
            sm.wait(dve, ta)
            t_xh_new[k] = sm.inc("dve", dve.tensor_tensor(xhh[k][:], xb[k][:], pR[:],
                                                          ALU.mult))
        t_xh = t_xh_new
        free("ppM", "ppR")
        pK = psum("ppK", (128, PT))
        tk_ev = t_knat
        for grp in range(2):
            for k in range(8):
                sm.wait(pe, t_wk, t_xh[k], t_xh[7], tk_ev)
                ii = pe.matmul(pK[:], wk[:, 256 * k + 128 * grp:256 * k + 128 * (grp + 1)],
                               xhh[k][:], start=(k == 0), stop=(k == 7))
            t_mm = sm.inc("pe", ii)
            sm.wait(act, t_mm, tk_ev)
            tk_ev = sm.inc("act", act.copy(knat[:, PT * grp:PT * (grp + 1)], pK[:]))
        t_knat = tk_ev
        free("ppK")
        sm.wait(sq, t_knat, t_ktp_r[buf], t_pre_ones)
        tw = sm.inc("st", sq.dma_start(
            out=KBPRE[buf, 0:FEAT, :, :].rearrange("f (g h) t -> h f g t", g=2),
            in_=knat[:].rearrange("p (g t) -> p g t", g=2)), 16)
        t_ktp_w[buf] = tw
        sm.wait(sq, tw)
        tr = sm.inc("ld", sq.dma_start(
            out=ktp[:].rearrange("f (h t) -> f h t", h=NH),
            in_=KBPRE[buf, :, :, :]), 16)
        t_ktp_r[buf] = tr
        t_vev = t_vown
        for cc in range(4):
            tsl = slice(128 * cc, 128 * (cc + 1))
            pV = psum("ppV", (128, 512))
            for half in range(2):
                for k in range(8):
                    sm.wait(pe, t_wv, t_xh[k], t_vev, t_knat, t_state)
                    ii = pe.matmul(pV[:], xhh[k][:, tsl],
                                   wv[:, 1024 * k + 512 * half:1024 * k + 512 * (half + 1)],
                                   start=(k == 0), stop=(k == 7))
                t_mm = sm.inc("pe", ii)
                sm.wait(act, t_mm, t_vev)
                t_vev = sm.inc("act", act.copy(
                    vpre[:].rearrange("p (c h e) -> p c h e", c=4, h=NH)
                        [:, cc, 8 * half:8 * (half + 1), 0:64],
                    pV[:].rearrange("p (h e) -> p h e", h=8)))
            free("ppV")
            sm.wait(dve, t_vev, t_pm)
            t_vm = sm.inc("dve", dve.tensor_scalar_mul(
                vpre[:].rearrange("p (c h e) -> p c h e", c=4, h=NH)[:, cc, :, 64:65]
                    .rearrange("p h o -> p (h o)"),
                cst["ONESB"][:, 0:16], pmb[:, cc:cc + 1]))
        t_vpre = t_vm
        t_pm_rd = t_vm
        PSS1p = psum("PSS1p", (128, 512))
        PSS2p = psum("PSS2p", (128, 512))
        pD1 = [psum("pD1a", (128, 1024)), psum("pD1b", (128, 1024))]
        pD2 = [psum("pD2a", (128, 256)), psum("pD2b", (128, 256))]
        t_kf_h = [None] * NH
        kf2 = kf

        def emit_reps(h):
            p = h % 2
            for cc in range(4):
                lhs = ktp[:, PT * h + 128 * cc:PT * h + 128 * (cc + 1)]
                sm.wait(pe, tr, t_consts, t_reprd[p])
                i1 = pe.matmul(pD1[p][:, 256 * cc:256 * (cc + 1)], lhs,
                               cst["R12D1"][:], start=True, stop=True)
                i2 = pe.matmul(pD2[p][:, 64 * cc:64 * (cc + 1)], lhs,
                               cst["R12D2"][:], start=True, stop=True)
            t_rep = sm.inc("pe", i2)
            sm.wait(act, t_rep, t_repact[0])
            te = sm.inc("act", act.copy(
                rep_sb[:].rearrange("p (q c f) -> p q c f", q=2, c=4)[:, p, :, 0:128],
                pD1[p][:].rearrange("p (c f) -> p c f", c=4)[:, :, 0:128]))
            sm.wait(act, te)
            te2 = sm.inc("act", act.copy(
                rep_sb[:].rearrange("p (q c f) -> p q c f", q=2, c=4)[:, p, :, 128:160],
                pD2[p][:].rearrange("p (c f) -> p c f", c=4)[:, :, 0:32]))
            t_repact[0] = te2
            sm.wait(dve, te, t_kfd[0])
            tk1 = sm.inc("dve", dve.tensor_tensor(
                kf2[:].rearrange("p (q c f) -> p q c f", q=2, c=4)[:, p, :, 0:128],
                pD1[p][:].rearrange("p (c f) -> p c f", c=4)[:, :, 128:256],
                rep_sb[:].rearrange("p (q c f) -> p q c f", q=2, c=4)[:, p, :, 0:128],
                ALU.mult))
            sm.wait(dve, te2, tk1)
            t_kfd[0] = sm.inc("dve", dve.tensor_tensor(
                kf2[:].rearrange("p (q c f) -> p q c f", q=2, c=4)[:, p, :, 128:160],
                pD2[p][:].rearrange("p (c f) -> p c f", c=4)[:, :, 32:64],
                rep_sb[:].rearrange("p (q c f) -> p q c f", q=2, c=4)[:, p, :, 128:160],
                ALU.mult))
            t_kf_h[h] = t_kfd[0]
            t_reprd[p] = t_kfd[0]

        def emit_states(h, part):
            p = h % 2
            hg = h % 4
            last = None
            for cc in range(4):
                sm.wait(pe, t_kf_h[h], t_vpre, t_u1p, t_u2p)
                if part == 0:
                    last = pe.matmul(PSS1p[:, 128 * hg:128 * hg + EV],
                                     kf2[:, 1024 * p + 256 * cc:1024 * p + 256 * cc + 128],
                                     vpre[:, 1040 * cc + EV * h:1040 * cc + EV * (h + 1)],
                                     start=(cc == 0), stop=(cc == 3),
                                     skip_group_check=True)
                else:
                    last = pe.matmul(PSS2p[32 * (h % 4):32 * (h % 4) + 32, 0:EV],
                                     kf2[:, 1024 * p + 256 * cc + 128:1024 * p + 256 * cc + DE],
                                     vpre[:, 1040 * cc + EV * h:1040 * cc + EV * (h + 1)],
                                     start=(cc == 0), stop=(cc == 3),
                                     tile_position=(0, 32 * (h % 4)),
                                     skip_group_check=True)
            return sm.inc("pe", last)

        t_reprd = [t_rep_rd, t_rep_rd]
        t_repact = [t_rep_act]
        t_kfd = [t_kf_last[0]]
        for h in range(NH + 1):
            if h < NH:
                emit_reps(h)
            if h >= 1:
                emit_states(h - 1, 0)
                t_state = emit_states(h - 1, 1)
            if h >= 1 and h % 4 == 0:
                g = h // 4 - 1
                sm.wait(dve, t_state, t_sz1, t_sz2, t_u1p, t_u2p)
                t_u1p = sm.inc("dve", dve.tensor_tensor(
                    S1[:].rearrange("p (g e) -> p g e", g=NH)[:, 4 * g:4 * g + 4],
                    S1[:].rearrange("p (g e) -> p g e", g=NH)[:, 4 * g:4 * g + 4],
                    PSS1p[:].rearrange("p (g e) -> p g e", g=4)[:, :, 0:EV], ALU.add))
                sm.wait(dve, t_u1p)
                t_u2p = sm.inc("dve", dve.tensor_tensor(
                    S2[:, 65 * g:65 * (g + 1)], S2[:, 65 * g:65 * (g + 1)],
                    PSS2p[:, 0:EV], ALU.add))
        t_rep_rd = t_reprd[1]
        t_rep_act = t_repact[0]
        t_kf_last[0] = t_kfd[0]
        free("pD1a", "pD1b", "pD2a", "pD2b", "PSS1p", "PSS2p")
        for k in range(8):
            t_xh_war[buf][k] = t_state

    free(*([f"xbf{p}_{k}" for p in range(2) for k in range(8)] +
           [f"xh{p}_{k}" for p in range(2) for k in range(8)] +
           ["sqb", "prowM", "prowR", "prowS", "prowMb", "prowRb", "knat", "ktp",
            "kf", "rep_sb", "vpre", "pmb", "wv", "wk"]))
    phase_gate(3)

    S1b = sb("S1b", (128, NH * EV), BF16)
    S2b = sb("S2b", (128, 4 * EV), BF16)
    t_s1 = t_u1p
    t_s2 = t_u2p
    sm.wait(act, t_u1p, t_u2p, t_sz1, t_sz2)
    t_s1b = sm.inc("act", act.copy(S1b[:], S1[:]))
    t_s2b = sm.inc("act", act.copy(S2b[:], S2[:]))
    o = dbg_out("s1", (128, NH * EV), F32)
    if o is not None:
        sm.wait(sq, t_s1)
        sm.inc("st", sq.dma_start(out=o, in_=S1[:]), 16)

    y_sb = [sb(f"y{k}", (128, NT), BF16) for k in range(8)]
    qt = sb("qt", (FEAT + 1, NH * CH), BF16)
    kt = sb("kt", (FEAT + 1, NH * CH), BF16)
    qf1 = sb("qf1", (128, NH * CH), BF16)
    qf2 = sb("qf2", (128, 4 * CH), BF16)
    kfo = sb("kfo", (128, NH * DE), BF16)
    repo = sb("repo", (128, 4 * DE), BF16)
    qrep = sb("qrep", (128, 512), BF16)
    sqf = sb("sqf", (128, NH * CH), BF16)
    at = sb("at", (128, NH * CH), BF16)
    nums = sb("nums", (EV, NH * CH), BF16)
    dens = sb("dens", (128, NH), F32)
    recs = sb("recs", (128, NH), F32)
    recT = sb("recT", (16, CH), BF16)

    t_qt = t_kt = None
    t_kfo = t_qf = t_at = t_num_ev = t_y = t_recT = None
    t_u1 = t_u2 = None
    t_pss = t_dmm = t_bc = t_tr = None
    t_bar4 = bar()

    for c in range(NCH):
        tsl = slice(CH * c, CH * (c + 1))
        sm.wait(sq, t_kb_all, t_qt, t_dmm, *(t_bar4 if c == 0 else ()))
        t_qt = sm.inc("ld", sq.dma_start(
            out=qt[:].rearrange("f (h t) -> f h t", h=NH),
            in_=KBOWN[0, :, :, tsl]), 16)
        sm.wait(sq, t_kt, t_dmm)
        t_kt = sm.inc("ld", sq.dma_start(
            out=kt[:].rearrange("f (h t) -> f h t", h=NH),
            in_=KBOWN[1, :, :, tsl]), 16)

        pRc = psum("scRc", (128, 2048))
        t_kfo_new = t_kfo
        for rnd in range(4):
            for hh in range(4):
                h = 4 * rnd + hh
                lhs = kt[:, CH * h:CH * (h + 1)]
                sm.wait(pe, t_kt, t_consts, t_kfo_new, t_num_ev, t_y)
                i1 = pe.matmul(pRc[:, 512 * hh:512 * hh + 2 * DE], lhs,
                               cst["R12KT"][:], start=True, stop=True)
            t_rep = sm.inc("pe", i1)
            sm.wait(act, t_rep)
            te = sm.inc("act", act.copy(
                repo[:].rearrange("p (a f) -> p a f", a=4),
                pRc[:].rearrange("p (a f) -> p a f", a=4)[:, :, 0:DE]))
            sm.wait(dve, te, t_pss)
            t_kfo_new = sm.inc("dve", dve.tensor_tensor(
                kfo[:].rearrange("p (h f) -> p h f", h=NH)[:, 4 * rnd:4 * rnd + 4, :],
                pRc[:].rearrange("p (a f) -> p a f", a=4)[:, :, DE:2 * DE],
                repo[:].rearrange("p (a f) -> p a f", a=4), ALU.mult))
        t_kfo = t_kfo_new
        free("scRc")

        PSS1 = psum("PSS1", (128, 2048))
        PSS2 = psum("PSS2", (128, 512))
        for h in range(NH):
            sm.wait(pe, t_kfo, t_vones, t_u1, t_u2, t_num_ev, t_y)
            i1 = pe.matmul(PSS1[:, 128 * h:128 * h + EV],
                           kfo[:, DE * h:DE * h + 128],
                           v_sb[:, 1040 * c + EV * h:1040 * c + EV * (h + 1)],
                           start=True, stop=True)
            i2 = pe.matmul(PSS2[32 * (h % 4):32 * (h % 4) + 32,
                                EV * (h // 4):EV * (h // 4) + EV],
                           kfo[:, DE * h + 128:DE * h + DE],
                           v_sb[:, 1040 * c + EV * h:1040 * c + EV * (h + 1)],
                           start=True, stop=True, tile_position=(0, 32 * (h % 4)))
        t_pss = sm.inc("pe", i2)

        pQ1 = psum("scQ1", (128, 512))
        pQ2 = psum("scQ2", (128, 512))
        t_qf_new = t_qf
        for g in range(4):
            rhs = qt[:, 512 * g:512 * (g + 1)]
            sm.wait(pe, t_qt, t_consts, t_qf_new, t_num_ev, t_y)
            i1 = pe.matmul(pQ1[:], cst["R1QD1"][:], rhs, start=True, stop=True)
            i2 = pe.matmul(pQ2[:], cst["R2QD1"][:], rhs, start=True, stop=True)
            t_rep = sm.inc("pe", i2)
            sm.wait(act, t_rep)
            te = sm.inc("act", act.copy(qrep[:], pQ1[:]))
            sm.wait(dve, te)
            t_qf_new = sm.inc("dve", dve.tensor_tensor(
                qf1[:, 512 * g:512 * (g + 1)], pQ2[:], qrep[:], ALU.mult))
        for h in range(NH):
            rhs = qt[:, CH * h:CH * (h + 1)]
            jj = h % 4
            sm.wait(pe, t_qt, t_consts, t_qf_new)
            i1 = pe.matmul(pQ1[32 * jj:32 * jj + 32, 128 * (h // 4):128 * (h // 4) + 128],
                           cst["R1QD2"][:], rhs, start=True, stop=True,
                           tile_position=(0, 32 * jj))
            i2 = pe.matmul(pQ2[32 * jj:32 * jj + 32, 128 * (h // 4):128 * (h // 4) + 128],
                           cst["R2QD2"][:], rhs, start=True, stop=True,
                           tile_position=(0, 32 * jj))
        t_rep = sm.inc("pe", i2)
        sm.wait(act, t_rep)
        te = sm.inc("act", act.copy(qrep[:], pQ1[:]))
        sm.wait(dve, te)
        t_qf = sm.inc("dve", dve.tensor_tensor(qf2[:], pQ2[:], qrep[:], ALU.mult))
        free("scQ1", "scQ2")

        sm.wait(dve, t_pss, t_s1, t_s2, t_s1b, t_s2b)
        t_u1 = sm.inc("dve", dve.tensor_tensor(
            S1[:], S1[:],
            PSS1[:].rearrange("p (h e) -> p h e", h=NH)[:, :, 0:EV], ALU.add))
        sm.wait(dve, t_u1)
        t_u2 = sm.inc("dve", dve.tensor_tensor(S2[:], S2[:], PSS2[:, 0:4 * EV], ALU.add))
        free("PSS1", "PSS2")

        PS_S = psum("PS_S", (128, 2048))
        for h in range(NH):
            sm.wait(pe, t_qt, t_kt, t_at, t_u2)
            ii = pe.matmul(PS_S[:, CH * h:CH * (h + 1)],
                           kt[0:FEAT, CH * h:CH * (h + 1)],
                           qt[0:FEAT, CH * h:CH * (h + 1)], start=True, stop=True)
        t_smm = sm.inc("pe", ii)
        sm.wait(act, t_smm, t_at)
        t_sq = sm.inc("act", act.activation(sqf[:], PS_S[:], AF.Square,
                                            bias=1.0, scale=0.25))
        sm.wait(dve, t_sq, t_consts, t_dmm)
        t_at = sm.inc("dve", dve.scalar_tensor_tensor(
            at[:], sqf[:], 1.0, cst["MASKH"][:], ALU.add, ALU.mult))
        free("PS_S")

        PS_N = psum("PS_N", (EV, 2048))
        for h in range(NH):
            hs = slice(CH * h, CH * (h + 1))
            jj = h % 4
            sm.wait(pe, t_at, t_vones, t_s1b, t_s2b, t_qf, t_num_ev)
            i1 = pe.matmul(PS_N[:, hs],
                           v_sb[:, 1040 * c + EV * h:1040 * c + EV * (h + 1)],
                           at[:, hs], start=True, stop=False)
            i2 = pe.matmul(PS_N[:, hs], S1b[:, EV * h:EV * (h + 1)],
                           qf1[:, hs], start=False, stop=False)
            i3 = pe.matmul(PS_N[:, hs],
                           S2b[32 * jj:32 * jj + 32, EV * (h // 4):EV * (h // 4) + EV],
                           qf2[32 * jj:32 * jj + 32, 128 * (h // 4):128 * (h // 4) + 128],
                           start=False, stop=True, tile_position=(32 * jj, 0))
        t_num = sm.inc("pe", i3)

        PS_D = psum("PS_D", (128, 16))
        for h in range(NH):
            hs = slice(CH * h, CH * (h + 1))
            jj = h % 4
            sm.wait(pe, t_at, t_s1b, t_s2b, t_qf, t_recT)
            i1 = pe.matmul(PS_D[:, h:h + 1], at[:, hs], cst["ONECB"][:],
                           start=True, stop=False)
            i2 = pe.matmul(PS_D[:, h:h + 1], qf1[:, hs],
                           S1b[:, EV * h + 64:EV * h + 65], start=False, stop=False)
            i3 = pe.matmul(PS_D[:, h:h + 1],
                           qf2[32 * jj:32 * jj + 32, 128 * (h // 4):128 * (h // 4) + 128],
                           S2b[32 * jj:32 * jj + 32,
                               EV * (h // 4) + 64:EV * (h // 4) + 65],
                           start=False, stop=True, tile_position=(32 * jj, 0))
        t_dmm = sm.inc("pe", i3)
        sm.wait(dve, t_dmm, t_tr)
        td1 = sm.inc("dve", dve.tensor_scalar_add(dens[:], PS_D[:], ATT_EPS))
        sm.wait(dve, td1)
        td2 = sm.inc("dve", dve.reciprocal(recs[:], dens[:]))
        free("PS_D")
        PS_R = psum("PS_R", (16, 128))
        sm.wait(pe, td2, t_recT)
        t_tr = sm.inc("pe", pe.transpose(PS_R[:], recs[:], cst["IDENTF"][:]))
        sm.wait(act, t_tr, t_recT, t_bc)
        t_recT = sm.inc("act", act.copy(recT[:], PS_R[:]))
        free("PS_R")
        PS_B = psum("PS_B", (EV, 2048))
        for h in range(NH):
            sm.wait(pe, t_recT, t_consts, t_y)
            ii = pe.matmul(PS_B[:, CH * h:CH * (h + 1)],
                           cst["ESEL"][:, EV * h:EV * (h + 1)], recT[:],
                           start=True, stop=True)
        t_bc = sm.inc("pe", ii)
        sm.wait(act, t_num, t_num_ev, t_y)
        t_num_ev = sm.inc("act", act.copy(nums[:], PS_N[:]))
        t_y_new = t_y
        for h in range(NH):
            hs = slice(CH * h, CH * (h + 1))
            sm.wait(dve, t_num_ev, t_bc)
            t_y_new = sm.inc("dve", dve.tensor_tensor(
                y_sb[h // 2][64 * (h % 2):64 * (h % 2) + 64, tsl],
                nums[0:64, hs], PS_B[0:64, hs], ALU.mult))
        t_y = t_y_new
        free("PS_B", "PS_N")

        sm.wait(act, t_u1, t_u2, t_num, t_dmm)
        t_s1b = sm.inc("act", act.copy(S1b[:], S1[:]))
        t_s2b = sm.inc("act", act.copy(S2b[:], S2[:]))

    o = dbg_out("ysb", (128, 8 * NT), BF16)
    if o is not None:
        sm.wait(sq, t_y)
        for k in range(8):
            sm.inc("st", sq.dma_start(out=o[:, NT * k:NT * (k + 1)], in_=y_sb[k][:]), 16)
    phase_gate(4)

    free("recT", "recs", "dens", "nums", "at", "sqf", "qrep", "repo",
         "kfo", "qf2", "qf1", "kt", "qt")

    wo = sb("wo", (128, 8 * 1024), BF16)
    sm.wait(gp, *bar())
    t_wo = wload(wo, ins["WoT"])
    t_r1 = [None] * 8
    t_pwo_rd = None
    for m in range(8):
        for ct in range(NT // 512):
            cs = slice(512 * ct, 512 * (ct + 1))
            pw = psum("pwo", (128, 512))
            for k in range(8):
                sm.wait(pe, t_wo, t_y, t_pwo_rd)
                ii = pe.matmul(pw[:], wo[:, 1024 * k + 128 * m:1024 * k + 128 * (m + 1)],
                               y_sb[k][:, cs], start=(k == 0), stop=(k == 7))
            t_mm = sm.inc("pe", ii)
            sm.wait(dve, t_mm, t_r1[m])
            t_r1[m] = sm.inc("dve", dve.tensor_tensor(r[m][:, cs], r[m][:, cs],
                                                      pw[:], ALU.add))
            t_pwo_rd = t_r1[m]
            free("pwo")
    free(*(["wo"] + [f"y{k}" for k in range(8)] +
           ["S2b", "S1b", "S2", "S1", "v_sb"]))
    o = dbg_out("r1", (128, 8 * NT), F32)
    if o is not None:
        sm.wait(sq, *t_r1)
        for k in range(8):
            sm.inc("st", sq.dma_start(out=o[:, NT * k:NT * (k + 1)], in_=r[k][:]), 16)

    phase_gate(5)
    Bcol = sb("Bcol", (32, 1), F32)
    masksb = sb("masksb", (128, NCH * E_EXP), BF16)
    scrowb = sb("scrowb", (1, NT), BF16)
    rks = sb("rks", (128, E_EXP), F32)
    brow = sb("brow", (1, E_EXP), F32)
    browb = sb("browb", (1, E_EXP), BF16)
    slotf = sb("slotf", (128, NCH), F32)
    sloti = sb("sloti", (128, NCH), I32)
    tmp32 = sb("tmp32", (128, E_EXP), F32)
    tmpc = sb("tmpc", (128, 1), F32)
    zero_sb = sb("zero_sb", (128, D), BF16)
    xst = sb("xst", (128, D), BF16)
    rtt = sb("rtt", (128, D), BF16)
    rowM = sb("rowM", (1, NT), F32)
    rowR = sb("rowR", (1, NT), F32)
    ln_bf = [sb(f"ln{k}", (128, NT), BF16) for k in range(8)]

    t_ln2 = ln_exact("l2", [t_r1[7]] * 8, rowM, rowR)
    o = dbg_out("ln2", (128, 8 * NT), BF16)
    if o is not None:
        sm.wait(sq, t_ln2)
        for k in range(8):
            sm.inc("st", sq.dma_start(out=o[:, NT * k:NT * (k + 1)], in_=ln_bf[k][:]), 16)

    phase_gate(6)
    negC = sb("negC", (32, NT), F32)
    mnrow = sb("mnrow", (16, CH), F32)
    ctok = sb("ctok", (128, NCH * E_EXP), F32)
    mneg = sb("mneg", (128, NCH), F32)
    scr0 = sb("scr0", (1, NT), F32)
    scrV = sb("scrV", (1, NT), F32)

    pBc = psum("pBc", (32, 1))
    for k in range(8):
        sm.wait(pe, t_gate, t_consts, t_r1[7], t_ln2)
        ii = pe.matmul(pBc[:], gate[:, 32 * k:32 * (k + 1)], cst["ONECF"][:, 0:1],
                       start=(k == 0), stop=(k == 7))
    t_bm = sm.inc("pe", ii)
    sm.wait(dve, t_bm)
    t_bcol = sm.inc("dve", dve.tensor_copy(Bcol[:], pBc[:]))
    free("pBc")

    t_negC = None
    for ct in range(NT // 512):
        cs = slice(512 * ct, 512 * (ct + 1))
        pAg = psum("pAg", (32, 512))
        for k in range(8):
            sm.wait(pe, t_r1[k], t_gate, t_ln2, t_negC, t_bcol)
            ii = pe.matmul(pAg[:], gate[:, 32 * k:32 * (k + 1)], r[k][:, cs],
                           start=(k == 0), stop=(k == 7))
        t_ag = sm.inc("pe", ii)
        pMb = psum("pMb", (32, 512))
        sm.wait(pe, t_negC)
        t_mb = sm.inc("pe", pe.matmul(pMb[:], cst["ONESF"][0:1, 0:32], rowM[0:1, cs],
                                      start=True, stop=True))
        sm.wait(act, t_mb, t_negC)
        t_me = sm.inc("act", act.copy(negC[:, cs], pMb[:]))
        sm.wait(dve, t_me, t_ag, t_bcol)
        t_negC = sm.inc("dve", dve.scalar_tensor_tensor(
            negC[:, cs], negC[:, cs], Bcol[:, 0:1], pAg[:], ALU.mult, ALU.subtract))
        free("pAg", "pMb")

    pT = psum("pT", (128, 512))
    t_ct = None
    for tt in range(NCH):
        sm.wait(pe, t_negC, t_ct)
        ii = pe.transpose(pT[:, 32 * tt:32 * (tt + 1)],
                          negC[:, CH * tt:CH * (tt + 1)], cst["IDENTF"][0:32, 0:32])
        t_ct = sm.inc("pe", ii)
    sm.wait(dve, t_ct)
    t_ctok = sm.inc("dve", dve.tensor_copy(ctok[:], pT[:]))
    free("pT")
    sm.wait(dve, t_ctok)
    t_mn = sm.inc("dve", dve.tensor_reduce(
        mneg[:].rearrange("p (t o) -> p t o", o=1),
        ctok[:].rearrange("p (t e) -> p t e", e=E_EXP), axis=AX.X, op=ALU.min))
    t_mk = None
    for tt in range(NCH):
        sm.wait(dve, t_mn)
        t_mk = sm.inc("dve", dve.tensor_single_scalar(
            masksb[:, 32 * tt:32 * (tt + 1)], ctok[:, 32 * tt:32 * (tt + 1)],
            mneg[:, tt:tt + 1], ALU.is_equal))
    t_mkb = t_mk

    PS_MT = psum("PS_MT", (16, 128))
    sm.wait(pe, t_mn)
    t_mt = sm.inc("pe", pe.transpose(PS_MT[:], mneg[:], cst["IDENTF"][:]))
    sm.wait(act, t_mt)
    t_mr = sm.inc("act", act.copy(mnrow[:], PS_MT[:]))
    free("PS_MT")
    sm.wait(sq, t_mr)
    t_rw = sm.inc("st", sq.dma_start(
        out=ROWB.rearrange("(a b) -> a b", a=16), in_=mnrow[:]), 16)
    sm.wait(sq, t_rw)
    t_rr = sm.inc("ld", sq.dma_start(
        out=scr0[0:1, :], in_=ROWB.rearrange("(o t) -> o t", o=1)), 16)
    sm.wait(dve, t_rr, t_ln2)
    t_val = sm.inc("dve", dve.scalar_tensor_tensor(
        scrV[0:1, :], scr0[0:1, :], -1.0, rowR[0:1, :], ALU.mult, ALU.mult))
    sm.wait(act, t_val)
    t_sc = sm.inc("act", act.activation(scr0[0:1, :], scrV[0:1, :], AF.Sigmoid))
    sm.wait(dve, t_sc)
    t_scb = sm.inc("dve", dve.tensor_copy(scrowb[:], scr0[0:1, :]))
    o = dbg_out("scores", (1, NT), F32)
    if o is not None:
        sm.wait(sq, t_sc)
        sm.inc("st", sq.dma_start(out=o, in_=scr0[0:1, :]), 16)
    o = dbg_out("masks", (128, NCH * E_EXP), BF16)
    if o is not None:
        sm.wait(sq, t_mk)
        sm.inc("st", sq.dma_start(out=o, in_=masksb[:]), 16)
    sm.wait(sq, t_ctok, t_rw, t_scb, t_mk)
    free("scrV", "scr0", "mneg", "ctok", "mnrow", "negC")

    phase_gate(7)
    xs = [sb(f"xs{k}", (128, NT), BF16) for k in range(8)]
    t_xs = [None] * 8
    for ct in range(NT // 512):
        cs = slice(512 * ct, 512 * (ct + 1))
        pSc = psum("pSc", (128, 512))
        sm.wait(pe, t_scb, t_xs[7])
        t_sb_ = sm.inc("pe", pe.matmul(pSc[:], cst["ONESB"][0:1, 0:128], scrowb[0:1, cs],
                                       start=True, stop=True))
        for k in range(8):
            sm.wait(dve, t_sb_, t_ln2, ("st", 1))
            t_xs[k] = sm.inc("dve", dve.tensor_tensor(xs[k][:, cs], ln_bf[k][:, cs],
                                                      pSc[:], ALU.mult))
        free("pSc")

    sgws = [sb("sgws0", (128, 8 * 512), BF16)]
    suws = [sb("suws0", (128, 8 * 512), BF16)]
    sdws = [sb("sdws0", (128, 4 * 1024), BF16)]
    silu_sb = sb("silu_sb", (128, 256), BF16)
    tsh = sb("tsh", (128, 256), BF16)
    t_sw_used = [None]
    t_r2 = list(t_r1)

    sm.wait(gp, *bar())
    for isl in range(4):
        b = 0
        sm.wait(gp, t_sw_used[b])
        tg1 = sm.inc("gld", gp.dma_start(
            out=sgws[b][:].rearrange("p (k c) -> p k c", k=8),
            in_=ins["SGW"][:, 512 * isl:512 * (isl + 1)].rearrange("(k p) c -> p k c", p=128)), 16)
        tg2 = sm.inc("gld", gp.dma_start(
            out=suws[b][:].rearrange("p (k c) -> p k c", k=8),
            in_=ins["SUW"][:, 512 * isl:512 * (isl + 1)].rearrange("(k p) c -> p k c", p=128)), 16)
        tg3 = sm.inc("gld", gp.dma_start(
            out=sdws[b][:].rearrange("p (k c) -> p k c", k=4),
            in_=ins["SDW"][512 * isl:512 * (isl + 1), :].rearrange("(k p) c -> p k c", p=128)), 16)
        t_si = t_th = None
        for ct in range(NT // 256):
            cs = slice(256 * ct, 256 * (ct + 1))
            pDn = psum("pDn", (128, 2048))
            for it in range(4):
                pG = psum("pG", (128, 256))
                pU = psum("pU", (128, 256))
                for k in range(8):
                    sm.wait(pe, tg1, tg2, t_ln2, t_si, t_xs[7])
                    i1 = pe.matmul(pG[:], sgws[b][:, 512 * k + 128 * it:512 * k + 128 * (it + 1)],
                                   ln_bf[k][:, cs], start=(k == 0), stop=(k == 7))
                tgm = sm.inc("pe", i1)
                for k in range(8):
                    sm.wait(pe, tg2, t_th)
                    i2 = pe.matmul(pU[:], suws[b][:, 512 * k + 128 * it:512 * k + 128 * (it + 1)],
                                   ln_bf[k][:, cs], start=(k == 0), stop=(k == 7))
                tum = sm.inc("pe", i2)
                sm.wait(act, tgm, t_th)
                t_si = sm.inc("act", act.activation(silu_sb[:], pG[:], AF.Silu))
                sm.wait(dve, t_si, tum, t_th)
                t_th = sm.inc("dve", dve.tensor_tensor(tsh[:], silu_sb[:], pU[:],
                                                       ALU.mult))
                free("pG", "pU")
                for m in range(8):
                    sm.wait(pe, t_th, tg3, t_r2[7])
                    i3 = pe.matmul(pDn[:, 256 * m:256 * (m + 1)],
                                   sdws[b][:, 1024 * it + 128 * m:1024 * it + 128 * (m + 1)],
                                   tsh[:], start=(it == 0 and m % 2 == 0), stop=(it == 3),
                                   skip_group_check=True)
            t_dn = sm.inc("pe", i3)
            for m in range(8):
                sm.wait(dve, t_dn, t_r2[m])
                t_r2[m] = sm.inc("dve", dve.tensor_tensor(
                    r[m][:, cs], r[m][:, cs], pDn[:, 256 * m:256 * (m + 1)], ALU.add))
            free("pDn")
        t_sw_used[b] = t_r2[7]
    t_shared_done = t_dn

    free("tsh", "silu_sb", "sdws0", "suws0", "sgws0")

    phase_gate(8)
    sm.wait(dve, t_consts)
    t_br = sm.inc("dve", dve.memset(brow[:], 0.0))
    sm.wait(dve, t_br)
    t_brb = sm.inc("dve", dve.tensor_copy(browb[:], brow[:]))
    t_slot = None
    for tt in range(NCH):
        ms = slice(32 * tt, 32 * (tt + 1))
        pRk = psum("pRk", (128, 32))
        pBb = psum("pBb", (128, 32))
        pTt = psum("pTt", (1, 32))
        sm.wait(pe, t_mkb, t_consts, t_brb, t_r2[7], t_shared_done)
        i1 = pe.matmul(pRk[:], cst["UEXCL"][:], masksb[:, ms], start=True, stop=True)
        t_rk = sm.inc("pe", i1)
        i2 = pe.matmul(pBb[:], cst["ONESB"][0:1, 0:128],
                       browb[:], start=True, stop=True)
        t_bb = sm.inc("pe", i2)
        i3 = pe.matmul(pTt[0:1, :], cst["ONECB"][:], masksb[:, ms], start=True, stop=True)
        t_tt = sm.inc("pe", i3)
        sm.wait(dve, t_rk, t_bb, t_slot)
        t1 = sm.inc("dve", dve.tensor_copy(rks[:], pRk[:]))
        sm.wait(dve, t1)
        t2 = sm.inc("dve", dve.tensor_tensor(rks[:], rks[:], pBb[:], ALU.add))
        sm.wait(dve, t2, t_tt)
        t3 = sm.inc("dve", dve.tensor_tensor(brow[:], brow[:], pTt[0:1, :], ALU.add))
        sm.wait(dve, t3)
        t_brb = sm.inc("dve", dve.tensor_copy(browb[:], brow[:]))
        free("pRk", "pBb", "pTt")
        sm.wait(dve, t2, t_mk)
        t4 = sm.inc("dve", dve.tensor_tensor(tmp32[:], rks[:], cst["EIDX"][:], ALU.add))
        sm.wait(dve, t4)
        t5 = sm.inc("dve", dve.tensor_tensor(tmp32[:], tmp32[:], masksb[:, ms], ALU.mult))
        sm.wait(dve, t5)
        t6 = sm.inc("dve", dve.tensor_reduce(
            slotf[:, tt:tt + 1], tmp32[:], axis=AX.X, op=ALU.add))
        sm.wait(dve, t6)
        t7 = sm.inc("dve", dve.tensor_tensor(tmp32[:], rks[:], masksb[:, ms], ALU.mult))
        sm.wait(dve, t7)
        t8 = sm.inc("dve", dve.tensor_reduce(tmpc[:], tmp32[:], axis=AX.X, op=ALU.add))
        sm.wait(dve, t8)
        t9 = sm.inc("dve", dve.tensor_single_scalar(tmpc[:], tmpc[:], float(CAP - 0.5),
                                                    ALU.is_gt))
        sm.wait(dve, t6, t9)
        ta = sm.inc("dve", dve.tensor_scalar(
            tmp32[:, 0:1], slotf[:, tt:tt + 1], -1.0, float(TRASH), ALU.mult, ALU.add))
        sm.wait(dve, ta)
        tb = sm.inc("dve", dve.tensor_tensor(tmp32[:, 0:1], tmp32[:, 0:1], tmpc[:],
                                             ALU.mult))
        sm.wait(dve, tb)
        tcp = sm.inc("dve", dve.tensor_tensor(slotf[:, tt:tt + 1], slotf[:, tt:tt + 1],
                                              tmp32[:, 0:1], ALU.add))
        sm.wait(dve, tcp)
        t_slot = sm.inc("dve", dve.tensor_copy(sloti[:, tt:tt + 1], slotf[:, tt:tt + 1]))
    o = dbg_out("sloti", (128, NCH), I32)
    if o is not None:
        sm.wait(sq, t_slot)
        sm.inc("st", sq.dma_start(out=o, in_=sloti[:]), 16)

    sm.wait(dve, t_slot)
    t_z = sm.inc("dve", dve.memset(zero_sb[:], 0.0))
    t_zs = None
    sm.wait(sq, t_z)
    for blk in range(NSLOT // 128):
        t_zs = sm.inc("st", sq.dma_start(out=XSG[128 * blk:128 * (blk + 1), :],
                                         in_=zero_sb[:]), 16)
    t_zs = sm.inc("st", sq.dma_start(out=XSG[NSLOT - 16:NSLOT, :], in_=zero_sb[0:16, :]), 16)
    t_zr = sm.inc("st", sq.dma_start(out=RTD[TRASH:TRASH + 16, :], in_=zero_sb[0:16, :]), 16)

    t_xst = None
    t_scat = None
    for tt in range(NCH):
        tsl = slice(CH * tt, CH * (tt + 1))
        pX = psum("pX", (128, 128), BF16)
        t_tp = None
        for k in range(8):
            sm.wait(pe, t_xs[k], t_consts, t_xst, t_slot)
            t_tp = sm.inc("pe", pe.transpose(pX[:], xs[k][:, tsl], cst["IDENTB"][:]))
            sm.wait(act, t_tp, t_scat)
            t_xst = sm.inc("act", act.copy(xst[:, 128 * k:128 * (k + 1)], pX[:]))
        free("pX")
        sm.wait(gp, t_xst, t_zs, t_slot)
        t_scat = sm.inc("sct", gp.indirect_dma_start(
            out=XSG, out_offset=IndirectOffsetOnAxis(ap=sloti[:, tt:tt + 1], axis=0),
            in_=xst[:], in_offset=None), 16)
    t_scat_all = t_scat

    free(*[f"xs{k}" for k in range(8)])
    free(*[f"ln{k}" for k in range(8)] + ["rowR", "rowM"])

    xg = [sb(f"xg{k}", (128, E_EXP * CAP), BF16) for k in range(8)]
    t_xg = [None] * 8
    for k in range(8):
        sm.wait(sq, t_scat_all, t_r2[7], t_shared_done)
        t_xg[k] = sm.inc("ld", sq.dma_start(
            out=xg[k][:], in_=XSG[0:E_EXP * CAP, 128 * k:128 * (k + 1)],
            transpose=True), 16)

    NSL = 8
    rgws = [sb(f"rgws{b}", (128, 8 * 256), BF16) for b in range(2)]
    ruws = [sb(f"ruws{b}", (128, 8 * 256), BF16) for b in range(2)]
    rdws = [sb(f"rdws{b}", (128, 2 * 1024), BF16) for b in range(2)]
    te_sb = sb("te_sb", (128, 2 * CAP), BF16)
    sil2 = sb("sil2", (128, 2 * CAP), BF16)
    rts = sb("rts", (128, 1024), BF16)

    t_buf_used = [None, None]
    t_rts = None
    t_rt_store = None
    t_si2 = t_te = t_dn_pe = None
    nsl_ctr = 0

    for e in range(E_EXP):
        PRo = psum("PRo", (128, 1024))
        for isl in range(NSL):
            b = nsl_ctr % 2
            nsl_ctr += 1
            wsem = "wt%d" % b
            sm.wait(gp, t_buf_used[b], t_r2[7], t_shared_done, ("sct", 1))
            tg1 = sm.inc(wsem, gp.dma_start(
                out=rgws[b][:].rearrange("p (k c) -> p k c", k=8),
                in_=ins["RGW"][e, :, 256 * isl:256 * (isl + 1)].rearrange("(k p) c -> p k c", p=128)), 16)
            tg2 = sm.inc(wsem, gp.dma_start(
                out=ruws[b][:].rearrange("p (k c) -> p k c", k=8),
                in_=ins["RUW"][e, :, 256 * isl:256 * (isl + 1)].rearrange("(k p) c -> p k c", p=128)), 16)
            tg3 = sm.inc(wsem, gp.dma_start(
                out=rdws[b][:].rearrange("p (k c) -> p k c", k=2),
                in_=ins["RDW"][e, 256 * isl:256 * (isl + 1), :].rearrange("(k p) c -> p k c", p=128)), 16)
            t_te_it = [None, None]
            pGs = [psum(f"pG2{x}", (128, CAP)) for x in range(2)]
            pUs = [psum(f"pU2{x}", (128, CAP)) for x in range(2)]
            for it in range(2):
                pG, pU = pGs[it], pUs[it]
                for k in range(8):
                    sm.wait(pe, tg1, t_xg[k], t_si2)
                    i1 = pe.matmul(pG[:], rgws[b][:, 256 * k + 128 * it:256 * k + 128 * (it + 1)],
                                   xg[k][:, CAP * e:CAP * (e + 1)],
                                   start=(k == 0), stop=(k == 7))
                tgm = sm.inc("pe", i1)
                for k in range(8):
                    sm.wait(pe, tg2, t_te_it[it] or t_te)
                    i2 = pe.matmul(pU[:], ruws[b][:, 256 * k + 128 * it:256 * k + 128 * (it + 1)],
                                   xg[k][:, CAP * e:CAP * (e + 1)],
                                   start=(k == 0), stop=(k == 7))
                tum = sm.inc("pe", i2)
                sm.wait(act, tgm, t_te)
                t_si2 = sm.inc("act", act.activation(sil2[:, CAP * it:CAP * (it + 1)], pG[:], AF.Silu))
                sm.wait(dve, t_si2, tum, t_te, t_dn_pe)
                t_te = sm.inc("dve", dve.tensor_tensor(
                    te_sb[:, CAP * it:CAP * (it + 1)],
                    sil2[:, CAP * it:CAP * (it + 1)], pU[:], ALU.mult))
                t_te_it[it] = t_te
            for it in range(2):
                for nn in range(2):
                    sm.wait(pe, t_te_it[it], tg3, t_rts, t_r2[7])
                    i3 = pe.matmul(PRo[:, 512 * nn:512 * (nn + 1)],
                                   te_sb[0:128, CAP * it:CAP * (it + 1)],
                                   rdws[b][:, 1024 * it + 512 * nn:1024 * it + 512 * (nn + 1)],
                                   start=(isl == 0 and it == 0), stop=(isl == NSL - 1 and it == 1),
                                   skip_group_check=True)
            t_dn = sm.inc("pe", i3)
            free("pG20", "pG21", "pU20", "pU21")
            t_buf_used[b] = t_dn
            t_dn_pe = t_dn
        sm.wait(act, t_dn, t_rt_store)
        t_rts = sm.inc("act", act.copy(rts[:], PRo[:]))
        free("PRo")
        sm.wait(sq, t_rts, t_zr)
        t_rt_store = sm.inc("st", sq.dma_start(
            out=RTD[CAP * e:CAP * (e + 1), :], in_=rts[:]), 16)

    free(*(["rts", "sil2", "te_sb", "rdws0", "rdws1", "ruws0", "ruws1",
            "rgws0", "rgws1"] + [f"xg{k}" for k in range(8)]))

    phase_gate(9)
    t_rtt = None
    t_pX2_rd = None
    t_out = [None] * 8
    for tt in range(NCH):
        tsl = slice(CH * tt, CH * (tt + 1))
        sm.wait(gp, t_rt_store, t_slot, t_rtt)
        t_g = sm.inc("sct", gp.indirect_dma_start(
            out=rtt[:], out_offset=None,
            in_=RTD, in_offset=IndirectOffsetOnAxis(ap=sloti[:, tt:tt + 1], axis=0)), 16)
        for k in range(8):
            pX = psum("pX2", (128, 128), BF16)
            sm.wait(pe, t_g, t_consts, t_pX2_rd, t_rts, t_te)
            t_tp = sm.inc("pe", pe.transpose(pX[:], rtt[:, 128 * k:128 * (k + 1)],
                                             cst["IDENTB"][:]))
            sm.wait(dve, t_tp, t_r2[k], t_out[k])
            t_out[k] = sm.inc("dve", dve.tensor_tensor(r[k][:, tsl], r[k][:, tsl],
                                                       pX[:], ALU.add))
            t_pX2_rd = t_out[k]
            free("pX2")
        t_rtt = t_out[7]

    t_fin = None
    for k in range(8):
        sm.wait(sq, t_out[k])
        t_fin = sm.inc("st", sq.dma_start(out=OUT[128 * k:128 * (k + 1), :], in_=r[k][:]), 16)
    sm.wait(sq, t_fin)

    stack.close()
    return nc, dbg


def _noop():
    pass


def kernel(L0_attn_hidden_states, L0_attn_Wq, L0_attn_Wk, L0_attn_Wv,
           L0_attn_Wo, L0_ffn_gate_w, L0_ffn_shared_gate_w, L0_ffn_shared_up_w,
           L0_ffn_shared_down_w, L0_ffn_routed_gate_w, L0_ffn_routed_up_w,
           L0_ffn_routed_down_w):
    x = np.asarray(L0_attn_hidden_states, np.float32).reshape(T_FULL, D)
    XT = np.ascontiguousarray(x.T)
    shared = {
        "WqT": np.ascontiguousarray(np.asarray(L0_attn_Wq, np.float32).T),
        "WkT": np.ascontiguousarray(np.asarray(L0_attn_Wk, np.float32).T),
        "WvT": np.ascontiguousarray(np.asarray(L0_attn_Wv, np.float32).T),
        "WoT": np.ascontiguousarray(np.asarray(L0_attn_Wo, np.float32).T),
        "GATE": np.ascontiguousarray(np.asarray(L0_ffn_gate_w, np.float32)),
        "SGW": np.ascontiguousarray(np.asarray(L0_ffn_shared_gate_w, np.float32)),
        "SUW": np.ascontiguousarray(np.asarray(L0_ffn_shared_up_w, np.float32)),
        "SDW": np.ascontiguousarray(np.asarray(L0_ffn_shared_down_w, np.float32)),
        "RGW": np.ascontiguousarray(np.asarray(L0_ffn_routed_gate_w, np.float32)),
        "RUW": np.ascontiguousarray(np.asarray(L0_ffn_routed_up_w, np.float32)),
        "RDW": np.ascontiguousarray(np.asarray(L0_ffn_routed_down_w, np.float32)),
    }
    shared.update(CONSTS)
    in_maps = []
    for c in range(NCORES):
        t0 = NT * c
        xpre = np.zeros((D, PRE), np.float32)
        if t0 > 0:
            xpre[:, :t0] = XT[:, :t0]
        pmt = np.zeros((128, PRE // CH), np.float32)
        pmt[:, :t0 // CH] = 1.0
        m = dict(shared)
        m["XS_T"] = np.ascontiguousarray(XT[:, t0:t0 + NT])
        m["XPRE_T"] = xpre
        m["PMT"] = pmt
        in_maps.append(m)
    nc, _ = build(tuple(DEBUG_OUTS))
    res = run_bass_kernel_spmd(nc, in_maps, core_ids=list(range(NCORES)))
    out = np.concatenate(
        [np.asarray(res.results[c]["OUT"], np.float32).T for c in range(NCORES)],
        axis=0)
    return out.reshape(1, T_FULL, D)
